# version 24
# baseline (speedup 1.0000x reference)
import numpy as np
import ml_dtypes

import concourse.bass as bass
import concourse.mybir as mybir
from concourse.bass_utils import run_bass_kernel_spmd

F32 = mybir.dt.float32
BF16 = mybir.dt.bfloat16
AF = mybir.ActivationFunctionType
ALU = mybir.AluOpType
AX = mybir.AxisListType

D = 1024
SEQ = 4096
NSEQ = 2
NTILE = SEQ // 128
NGROUP = SEQ // 512
NSTRIP = SEQ // 256
NEGBIG = -1.0e30
INVSQ = 1.0 / 32.0

ENGS = ("tensor", "vector", "scalar", "sync", "gpsimd")


class Sem:
    def __init__(self, handle):
        self.h = handle
        self.n = 0


class Builder:
    def __init__(self):
        self.ops = {e: [] for e in ENGS}
        self.queue_sems = {"sync": {}, "gpsimd": {}, "scalar": {}}
        self.eng_sem = {}

    def E(self, eng, fn, waits=(), incs=(), chain=True):
        waits = list(waits)
        incs = list(incs)
        es = self.eng_sem.get(eng) if chain else None
        if es is not None:
            waits.append((es, es.n))
            if not any(s is es for s, _ in incs):
                incs.append((es, 1))
        rincs = []
        out = {}
        for s, amt in incs:
            s.n += amt
            rincs.append((s.h, amt))
            out[s] = s.n
        self.ops[eng].append(
            (fn, tuple((s.h, int(v)) for s, v in waits if v > 0), tuple(rincs)))
        return out

    def dma(self, eng, sem, fn, waits=()):
        sem.n += 16
        self.queue_sems[eng][sem] = True
        self.ops[eng].append(
            (fn, tuple((s.h, int(v)) for s, v in waits if v > 0), ((sem.h, 16),)))
        return sem.n


def build_nc(debug=False):
    nc = bass.Bass("TRN2", target_bir_lowering=False)
    scr_kind = "ExternalOutput" if debug else "Internal"

    x_in = nc.dram_tensor("x", [NSEQ, SEQ, D], F32, kind="ExternalInput").ap()
    w_h = {n: nc.dram_tensor(n, [D, D], F32, kind="ExternalInput").ap()
           for n in ("wq", "wk", "wv", "wo", "fk")}
    b_h = {n: nc.dram_tensor(n, [1, D], F32, kind="ExternalInput").ap()
           for n in ("bq", "bk", "bv", "bo", "fbias")}
    fa_h = nc.dram_tensor("fa", [D, 16], F32, kind="ExternalInput").ap()
    fb_h = nc.dram_tensor("fb", [16, D], F32, kind="ExternalInput").ap()
    y_out = nc.dram_tensor("y", [NSEQ, SEQ, D], F32, kind="ExternalOutput").ap()

    q_scr = nc.dram_tensor("q_scr", [NSEQ, NSTRIP, 128, 2048], BF16, kind=scr_kind).ap()
    k_scr = nc.dram_tensor("k_scr", [NSEQ, NTILE, 128, 1024], BF16, kind=scr_kind).ap()
    v_dbg = nc.dram_tensor("v_dbg", [128, NTILE * 1024], BF16, kind="ExternalOutput").ap() if debug else None
    wT_scr = {n: nc.dram_tensor(n + "T_h", [128, 8192], BF16, kind="Internal").ap()
              for n in ("wq", "wk", "wv", "wo")}
    wfq_h = nc.dram_tensor("wfq_h", [128, 8192], BF16, kind=scr_kind).ap()

    ident_b_h = nc.inline_tensor(np.eye(128, dtype=ml_dtypes.bfloat16), "ident_b").ap()
    ident_f_h = nc.inline_tensor(np.eye(128, dtype=np.float32), "ident_f").ap()
    mknp = np.empty((128, 512), dtype=np.float32)
    kk = np.arange(128)[:, None]
    qq = np.arange(256)[None, :]
    mknp[:, 0:256] = np.where(kk <= qq, 0.0, NEGBIG)
    mknp[:, 256:512] = np.where(128 + kk <= qq, 0.0, NEGBIG)
    masks_h = nc.inline_tensor(mknp, "masks_h").ap()

    from contextlib import ExitStack
    with ExitStack() as ctx:
        ec = ctx.enter_context

        V_buf = ec(nc.sbuf_tensor("V_buf", [128, NTILE * 1024], BF16))
        wsl = ec(nc.sbuf_tensor("wsl", [128, 3 * 8192], BF16))
        a_bf = ec(nc.sbuf_tensor("a_bf", [128, 128], BF16))
        b_bf = ec(nc.sbuf_tensor("b_bf", [16, 1024], BF16))
        ident = ec(nc.sbuf_tensor("ident", [128, 128], BF16))
        identf = ec(nc.sbuf_tensor("identf", [128, 128], F32))
        masks = ec(nc.sbuf_tensor("masks", [128, 512], F32))
        bq32 = ec(nc.sbuf_tensor("bq32", [128, 8], F32))
        bkc = ec(nc.sbuf_tensor("bkc", [128, 8], F32))
        bvrow = ec(nc.sbuf_tensor("bvrow", [1, 1024], BF16))
        fbrow = ec(nc.sbuf_tensor("fbrow", [1, 1024], BF16))
        bo_tile = ec(nc.sbuf_tensor("bo_tile", [128, 1024], F32))
        onesrow = ec(nc.sbuf_tensor("onesrow", [1, 256], BF16))
        onesrowf = ec(nc.sbuf_tensor("onesrowf", [1, 128], F32))
        onescol = ec(nc.sbuf_tensor("onescol", [128, 1], BF16))
        zrow = ec(nc.sbuf_tensor("zrow", [1, 512], BF16))
        epscol = ec(nc.sbuf_tensor("epscol", [128, 1], F32))
        rowstage = ec(nc.sbuf_tensor("rowstage", [1, 1024], F32))
        stats = ec(nc.sbuf_tensor("stats", [128, 32], F32))
        gq = ec(nc.sbuf_tensor("gq", [128, 16], F32))
        rl = ec(nc.sbuf_tensor("rl", [128, 2], F32))

        xb = ec(nc.sbuf_tensor("xb", [128, 4096], F32))
        sqb = ec(nc.sbuf_tensor("sqb", [128, 2048], F32))
        lnb = ec(nc.sbuf_tensor("lnb", [128, 2048], BF16))
        lnT = ec(nc.sbuf_tensor("lnT", [128, 4096], BF16))
        Qs = ec(nc.sbuf_tensor("Qs", [128, 4096], BF16))
        Ktr = ec(nc.sbuf_tensor("Ktr", [128, 4096], BF16))
        Sa = ec(nc.sbuf_tensor("Sa", [128, 512], F32))
        PT = ec(nc.sbuf_tensor("PT", [128, 512], BF16))
        OT_sb = ec(nc.sbuf_tensor("OT_sb", [128, 2048], BF16))
        t1 = ec(nc.sbuf_tensor("t1", [128, 2048], F32))
        t2 = ec(nc.sbuf_tensor("t2", [128, 1024], F32))
        x2 = ec(nc.sbuf_tensor("x2", [128, 1024], F32))
        xaT = ec(nc.sbuf_tensor("xaT", [16, 256], BF16))

        OTp = [ec(nc.psum_tensor(f"OTp{i}", [128, 512], F32)) for i in range(4)]
        SBp = ec(nc.psum_tensor("SBp", [128, 1024], F32))
        LBp = ec(nc.psum_tensor("LBp", [128, 512], F32))
        trs_guard = nc.psum_tensor("TRs", [128, 1024], BF16)
        TRs = trs_guard.__enter__()

        def mksem(name):
            return Sem(ec(nc.semaphore(name)))

        PE, DV, AC, GP, BAR = (mksem(n) for n in ("PE", "DV", "AC", "GP", "BAR"))
        XS = [mksem(f"XS{i}") for i in range(4)]
        YS = [mksem(f"YS{i}") for i in range(4)]
        QS = [mksem(f"QS{i}") for i in range(2)]
        KS = [mksem(f"KS{i}") for i in range(4)]
        WS = [mksem(f"WS{i}") for i in range(2)]
        FK = [mksem(f"FK{i}") for i in range(2)]
        FS = [mksem(f"FS{i}") for i in range(2)]
        TS = [mksem(f"TS{i}") for i in range(2)]
        TA, TB = mksem("TA"), mksem("TB")
        QG, KG, WL, RS, CONST, WT = (mksem(n) for n in ("QG", "KG", "WL", "RS", "CONST", "WT"))

        B = Builder()
        B.eng_sem = {"vector": DV, "scalar": AC, "gpsimd": GP}
        E, dma = B.E, B.dma

        block = ec(nc.Block())

        def barrier():
            k = BAR.n // 5 + 1
            for eng in ENGS:
                w = []
                if eng in ("sync", "gpsimd", "scalar"):
                    w = [(s, s.n) for s in B.queue_sems[eng]]

                def f(e, k=k, bh=BAR.h):
                    e.drain()
                    e.sem_inc(bh, 1)
                    e.wait_ge(bh, 5 * k)
                E(eng, f, waits=w, chain=False)
            BAR.n = 5 * k

        ln_slot = [0]

        def emit_ln(src_ap, out_ap, sq_slot, waits_src=(), waits_sq=(), waits_out=(),
                    veng="vector", vsem=None, apply_eng="scalar"):
            sl = (ln_slot[0] % 4) * 8
            ln_slot[0] += 1
            cS1 = stats[:, sl + 0:sl + 1]
            cMn = stats[:, sl + 1:sl + 2]
            cS2 = stats[:, sl + 2:sl + 3]
            cM2 = stats[:, sl + 3:sl + 4]
            cB2 = stats[:, sl + 4:sl + 5]
            cLn = stats[:, sl + 5:sl + 6]
            cR = stats[:, sl + 6:sl + 7]
            cBn = stats[:, sl + 7:sl + 8]
            sq = sqb[:, sq_slot * 1024:(sq_slot + 1) * 1024]
            VS = vsem if vsem is not None else DV

            E(veng, lambda e, s=src_ap, o=cS1: e.reduce_sum(out=o, in_=s, axis=AX.X),
              waits=waits_src)
            rsq = E(veng,
                    lambda e, s=src_ap, o=sq, a=cS2: e.scalar_tensor_tensor(
                        out=o, in0=s, scalar=1.0, in1=s, op0=ALU.mult, op1=ALU.mult, accum_out=a),
                    waits=tuple(waits_src) + tuple(waits_sq))
            r2 = E(veng, lambda e, i=cS1, o=cMn: e.tensor_scalar_mul(out=o, in0=i, scalar1=-1.0 / D))
            E("scalar", lambda e, i=cMn, o=cM2: e.activation(out=o, in_=i, func=AF.Square, scale=1.0),
              waits=[(VS, r2[VS])])
            E("scalar", lambda e, i=cM2, o=cB2, b=epscol[:, 0:1]: e.activation(
                out=o, in_=i, func=AF.Identity, bias=b, scale=-1.0))
            E("scalar", lambda e, i=cS2, o=cLn, b=cB2: e.activation(
                out=o, in_=i, func=AF.Ln, bias=b, scale=1.0 / D),
                waits=[(VS, rsq[VS])])
            ax = E("scalar", lambda e, i=cLn, o=cR: e.activation(out=o, in_=i, func=AF.Exp, scale=-0.5),
                   incs=[(AC, 1)])
            r4 = E(veng, lambda e, a=cMn, bb=cR, o=cBn: e.tensor_mul(out=o, in0=a, in1=bb),
                   waits=[(AC, ax[AC])])
            if apply_eng == "scalar":
                a3 = E("scalar",
                       lambda e, s=src_ap, o=out_ap, sc=cR, b=cBn: e.activation(out=o, in_=s, func=AF.Identity, bias=b, scale=sc),
                       waits=tuple(waits_out) + ((VS, r4[VS]),), incs=[(AC, 1)])
                return (AC, a3[AC]), rsq[VS], rsq[VS]
            a3 = E("vector",
                   lambda e, s=src_ap, o=out_ap, sc=cR, b=cBn: e.tensor_scalar(
                       out=o, in0=s, scalar1=sc, scalar2=b, op0=ALU.mult, op1=ALU.add),
                   waits=waits_out, incs=[(DV, 1)])
            return (DV, a3[DV]), rsq[VS], a3[DV]

        for dst, src in ((ident[:], ident_b_h), (identf[:], ident_f_h), (masks[:], masks_h)):
            dma("sync", CONST, lambda e, d=dst, s=src: e.dma_start(out=d, in_=s))
        ld_consts = CONST.n

        for fn in (lambda e: e.memset(onesrow[:], 1.0),
                   lambda e: e.memset(onesrowf[:], 1.0),
                   lambda e: e.memset(onescol[:], 1.0),
                   lambda e: e.memset(zrow[:], 0.0)):
            E("gpsimd", fn)
        gp_memset = E("gpsimd", lambda e: e.memset(epscol[:], 1e-5), incs=[(GP, 1)])[GP]

        pe_rowstage = 0
        dv_rowstage = 0

        def prep_colbias(src_h, dest, scale):
            nonlocal pe_rowstage, dv_rowstage
            dv_lbp_prev = DV.n
            rsv = dma("sync", RS, lambda e, s=src_h: e.dma_start(out=rowstage[:], in_=s),
                      waits=[(PE, pe_rowstage), (DV, dv_rowstage)])
            for c in range(8):
                w = [(RS, rsv), (GP, gp_memset), (DV, dv_lbp_prev)] if c == 0 else []
                E("tensor",
                  lambda e, c=c: e.matmul(out=LBp[:, c:c + 1],
                                          lhsT=rowstage[0:1, c * 128:(c + 1) * 128],
                                          rhs=onesrowf[0:1, 0:1], start=True, stop=True),
                  waits=w, incs=[(PE, 1)] if c == 7 else ())
            pe_rowstage = PE.n
            r = E("vector", lambda e, d=dest, sc=scale: e.tensor_scalar_mul(out=d, in0=LBp[:, 0:8], scalar1=sc),
                  waits=[(PE, pe_rowstage)], incs=[(DV, 1)])
            return r[DV]

        prep_colbias(b_h["bq"], bq32[:], INVSQ)
        prep_colbias(b_h["bk"], bkc[:], 1.0)

        def prep_rowbias(src_h, dest):
            nonlocal pe_rowstage, dv_rowstage
            rsv = dma("sync", RS, lambda e, s=src_h: e.dma_start(out=rowstage[:], in_=s),
                      waits=[(PE, pe_rowstage), (DV, dv_rowstage)])
            r = E("vector", lambda e, d=dest: e.tensor_copy(out=d, in_=rowstage[:]),
                  waits=[(RS, rsv)], incs=[(DV, 1)])
            dv_rowstage = r[DV]

        prep_rowbias(b_h["bv"], bvrow[:])
        prep_rowbias(b_h["fbias"], fbrow[:])

        dv_lbp_prev = DV.n
        rsv = dma("sync", RS, lambda e: e.dma_start(out=rowstage[:], in_=b_h["bo"]),
                  waits=[(PE, pe_rowstage), (DV, dv_rowstage)])
        for hh in range(2):
            E("tensor", lambda e, hh=hh: e.matmul(
                out=LBp[:, 0:512], lhsT=onesrowf[0:1, 0:128],
                rhs=rowstage[0:1, hh * 512:(hh + 1) * 512], start=True, stop=True),
                waits=[(RS, rsv), (DV, dv_lbp_prev)] if hh == 0 else [(DV, dv_lbp_prev)],
                incs=[(PE, 1)])
            dv_lbp_prev = E("vector", lambda e, hh=hh: e.tensor_copy(
                out=bo_tile[:, hh * 512:(hh + 1) * 512], in_=LBp[:, 0:512]),
                waits=[(PE, PE.n)], incs=[(DV, 1)])[DV]
        pe_rowstage = PE.n
        dv_rowstage = dv_lbp_prev

        for dc in range(8):
            dma("sync", XS[0], lambda e, dc=dc: e.dma_start(
                out=xb[:, dc * 16:(dc + 1) * 16], in_=fa_h[dc * 128:(dc + 1) * 128, :]))
        dv_a = E("vector", lambda e: e.tensor_copy(out=a_bf[:], in_=xb[:, 0:128]),
                 waits=[(XS[0], XS[0].n)], incs=[(DV, 1)])[DV]
        dma("sync", XS[1], lambda e: e.dma_start(out=xb[0:16, 1024:2048], in_=fb_h))
        dv_b = E("vector", lambda e: e.tensor_copy(out=b_bf[:], in_=xb[0:16, 1024:2048]),
                 waits=[(XS[1], XS[1].n)], incs=[(DV, 1)])[DV]

        dv_absmax = {}
        for c in range(8):
            sl = c % 4
            w = [(DV, max(dv_a, dv_b))] if c < 4 else [(DV, dv_absmax[c - 4])]
            xv = dma("sync", XS[sl], lambda e, c=c, sl=sl: e.dma_start(
                out=xb[:, sl * 1024:(sl + 1) * 1024],
                in_=w_h["fk"][c * 128:(c + 1) * 128, :]), waits=w)
            r = E("vector", lambda e, c=c, sl=sl: e.tensor_reduce(
                out=gq[:, c:c + 1], in_=xb[:, sl * 1024:(sl + 1) * 1024],
                axis=AX.X, op=ALU.max, apply_absolute_value=True),
                waits=[(XS[sl], xv)], incs=[(DV, 1)])
            dv_absmax[c] = r[DV]
        dv_g = E("vector", lambda e: e.tensor_reduce(
            out=gq[:, 8:9], in_=gq[:, 0:8], axis=AX.X, op=ALU.max),
            incs=[(DV, 1)])[DV]
        pe_gt = E("tensor", lambda e: e.transpose(out=LBp[0:1, 0:128], in_=gq[:, 8:9], identity=identf[:]),
                  waits=[(DV, dv_g), (CONST, ld_consts)], incs=[(PE, 1)])[PE]
        E("vector", lambda e: e.tensor_reduce(out=gq[0:1, 9:10], in_=LBp[0:1, 0:128], axis=AX.X, op=ALU.max),
          waits=[(PE, pe_gt)])
        E("vector", lambda e: e.reciprocal(out=gq[0:1, 10:11], in_=gq[0:1, 9:10]))
        E("vector", lambda e: e.tensor_scalar_mul(out=gq[0:1, 11:12], in0=gq[0:1, 10:11], scalar1=7.0))
        dv_qs = E("vector", lambda e: e.tensor_scalar_mul(out=gq[0:1, 12:13], in0=gq[0:1, 9:10], scalar1=1.0 / 7.0),
                  incs=[(DV, 1)])[DV]
        pe_bc = None
        for (srcc, dstc) in ((11, 13), (12, 14)):
            pe_bc = E("tensor", lambda e, s=srcc, d=dstc: e.matmul(
                out=LBp[:, 200 + d:201 + d], lhsT=onesrowf[0:1, 0:128],
                rhs=gq[0:1, s:s + 1], start=True, stop=True),
                waits=[(DV, dv_qs)], incs=[(PE, 1)])[PE]
        dv_bcast = E("vector", lambda e: e.tensor_copy(out=gq[:, 13:15], in_=LBp[:, 213:215]),
                     waits=[(PE, pe_bc)], incs=[(DV, 1)])[DV]

        dv_cast_slot = {i: 0 for i in range(4)}
        pe_stage_slot = {i: 0 for i in range(4)}
        dv_trcopy = 0
        wt_wo_store = 0

        def prep_wT(wname, region, to_scratch):
            nonlocal dv_trcopy
            dv_last = 0
            for h in range(2):
                xv = {}
                for fl in range(4):
                    xv[fl] = dma("sync", XS[fl], lambda e, w=wname, h=h, fl=fl: e.dma_start(
                        out=xb[:, fl * 1024:(fl + 1) * 1024],
                        in_=w_h[w][h * 512 + fl * 128:h * 512 + (fl + 1) * 128, :]),
                        waits=[(DV, dv_bcast), (DV, dv_cast_slot[fl])])
                for fl in range(4):
                    r = E("vector", lambda e, fl=fl: e.tensor_copy(
                        out=Qs[:, fl * 1024:(fl + 1) * 1024], in_=xb[:, fl * 1024:(fl + 1) * 1024]),
                        waits=[(XS[fl], xv[fl]), (PE, pe_stage_slot[fl])], incs=[(DV, 1)])
                    dv_cast_slot[fl] = r[DV]
                for fl in range(4):
                    for dc in range(8):
                        w = [(DV, dv_cast_slot[fl])]
                        if dc == 0:
                            w += [(DV, dv_trcopy), (CONST, ld_consts)]
                        E("tensor", lambda e, fl=fl, dc=dc: e.transpose(
                            out=TRs[:, dc * 128:(dc + 1) * 128],
                            in_=Qs[:, fl * 1024 + dc * 128:fl * 1024 + (dc + 1) * 128],
                            identity=ident[:]),
                            waits=w, incs=[(PE, 1)] if dc == 7 else ())
                    pe_stage_slot[fl] = PE.n
                    pe_last = PE.n
                    dst = wsl.ap().rearrange("p (r dcc k) -> p r dcc k", r=3, dcc=8)[
                        :, region, :, h * 512 + fl * 128:h * 512 + (fl + 1) * 128]
                    src3 = TRs.ap().rearrange("p (dcc t) -> p dcc t", dcc=8)
                    w2 = [(PE, pe_last)]
                    if region == 0 and wname != "wo":
                        w2.append((WT, wt_wo_store))
                    r = E("vector", lambda e, d=dst, s3=src3: e.tensor_copy(out=d, in_=s3),
                          waits=w2, incs=[(DV, 1)])
                    dv_trcopy = r[DV]
                    dv_last = r[DV]
            return dma("gpsimd", WT, lambda e, s=to_scratch, rg=region: e.dma_start(
                out=s, in_=wsl[:, rg * 8192:(rg + 1) * 8192]), waits=[(DV, dv_last)])

        wt_wo_store = prep_wT("wo", 0, wT_scr["wo"])
        prep_wT("wq", 0, wT_scr["wq"])
        prep_wT("wk", 1, wT_scr["wk"])
        prep_wT("wv", 2, wT_scr["wv"])

        trs_guard.__exit__(None, None, None)

        barrier()

        def phase_a(seq, reload_weights, TRA, do_quant=False):
            wl_val = 0
            if reload_weights:
                for rg, n in ((0, "wq"), (1, "wk"), (2, "wv")):
                    wl_val = dma("sync", WL, lambda e, rg=rg, n=n: e.dma_start(
                        out=wsl[:, rg * 8192:(rg + 1) * 8192], in_=wT_scr[n]))

            ac_x_free = {}
            pe_lnb = {0: 0, 1: 0}
            dv_sqb = {0: 0, 1: 0}
            st_a = {"ac_lnTc": 0}
            ac_sbbank = {0: 0, 1: 0}
            dv_obank = {0: (AC, 0), 1: (AC, 0)}
            qg_val = 0
            kg_val = 0
            chains = {}

            def emit_chain(g, sub):
                t = 4 * g + sub
                sl4 = t % 4
                sl2 = t % 2
                xs_v = dma("sync", XS[sl4], lambda e, seq=seq, t=t, sl4=sl4: e.dma_start(
                    out=xb[:, sl4 * 1024:(sl4 + 1) * 1024],
                    in_=x_in[seq, t * 128:(t + 1) * 128, :]),
                    waits=[ac_x_free.get(sl4, (AC, 0))])
                aca, dvs, _ = emit_ln(
                    xb[:, sl4 * 1024:(sl4 + 1) * 1024],
                    lnb[:, sl2 * 1024:(sl2 + 1) * 1024],
                    sq_slot=sl2,
                    waits_src=[(XS[sl4], xs_v)],
                    waits_sq=[(DV, dv_sqb[sl2])],
                    waits_out=[(PE, pe_lnb[sl2])])
                ac_x_free[sl4] = aca
                chains.setdefault(g, {})[sub] = aca
                dv_sqb[sl2] = dvs

            def emit_T(g, sub):
                t = 4 * g + sub
                sl2 = t % 2
                aca = chains[g][sub]
                for dc in range(8):
                    w = [aca]
                    if dc == 0:
                        w.append((AC, st_a["ac_lnTc"]))
                    E("tensor", lambda e, dc=dc, sl2=sl2, TRA=TRA: e.transpose(
                        out=TRA[:, dc * 128:(dc + 1) * 128],
                        in_=lnb[:, sl2 * 1024 + dc * 128:sl2 * 1024 + (dc + 1) * 128],
                        identity=ident[:]),
                        waits=w, incs=[(PE, 1)] if dc == 7 else ())
                pe_lnb[sl2] = PE.n
                pe_tr = PE.n
                dst = lnT.ap().rearrange("p (dcc t) -> p dcc t", dcc=8)[:, :, sub * 128:(sub + 1) * 128]
                src3 = TRA.ap().rearrange("p (dcc t) -> p dcc t", dcc=8)
                r = E("scalar", lambda e, d=dst, s3=src3: e.activation(out=d, in_=s3, func=AF.Identity, scale=1.0),
                      waits=[(PE, pe_tr)], incs=[(AC, 1)])
                st_a["ac_lnTc"] = r[AC]

            q_t1_free = {0: 0, 1: 0}
            q_ot_free = {0: 0, 1: 0}
            fkv = {0: 0, 1: 0}

            def quant_chunk(c):
                sl = c % 2
                fkv[sl] = dma("sync", FK[sl], lambda e, c=c, sl=sl: e.dma_start(
                    out=t1[:, sl * 1024:(sl + 1) * 1024],
                    in_=w_h["fk"][c * 128:(c + 1) * 128, :]),
                    waits=[(DV, q_t1_free[sl])])
                xsl = t1[:, sl * 1024:(sl + 1) * 1024]
                ssl = x2[:]
                tsl = t2[:, 0:1024]
                osl = OT_sb[:, sl * 1024:(sl + 1) * 1024]
                r = E("vector", lambda e, x_=xsl, s=ssl: e.tensor_scalar_mul(
                    out=s, in0=x_, scalar1=gq[:, 13:14]),
                    waits=[(FK[sl], fkv[sl])])
                q_t1_free[sl] = r[DV]
                E("vector", lambda e, s=ssl, t=tsl: e.tensor_scalar(
                    out=t, in0=s, scalar1=-6.5, scalar2=-7.0, op0=ALU.is_ge, op1=ALU.add))
                for kk_ in range(-6, 7):
                    E("vector", lambda e, s=ssl, t=tsl, c_=kk_ + 0.5: e.scalar_tensor_tensor(
                        out=t, in0=s, scalar=c_, in1=t, op0=ALU.is_ge, op1=ALU.add))
                dvq = E("vector", lambda e, t=tsl, o=osl: e.tensor_scalar_mul(
                    out=o, in0=t, scalar1=gq[:, 14:15]),
                    waits=[(FS[sl], q_ot_free[sl])], incs=[(DV, 1)])[DV]
                q_ot_free[sl] = dma("gpsimd", FS[sl], lambda e, c=c, o=osl: e.dma_start(
                    out=wfq_h[:, c * 1024:(c + 1) * 1024], in_=o), waits=[(DV, dvq)])

            emit_chain(0, 0)
            emit_chain(0, 1)
            for sub in range(4):
                emit_T(0, sub)
                if sub < 2:
                    emit_chain(0, sub + 2)

            for g in range(NGROUP):
                if g + 1 < NGROUP:
                    emit_chain(g + 1, 0)
                    emit_chain(g + 1, 1)
                if do_quant and g > 0:
                    quant_chunk(g - 1)

                def proj_fm(wreg, biascol, scale, evac_dst_fn, first_waits, src_shape_fn=None):
                    nonlocal ac_sbbank
                    if src_shape_fn is None:
                        src_shape_fn = lambda bk_: SBp[:, bk_ * 512:bk_ * 512 + 512]
                    last_ac = 0
                    for kc in range(8):
                        bk_ = kc % 2
                        w = [(AC, st_a["ac_lnTc"]), (AC, ac_sbbank[bk_])]
                        if kc == 0:
                            w += list(first_waits)
                            if wl_val:
                                w.append((WL, wl_val))
                        for dc in range(8):
                            E("tensor", lambda e, wreg=wreg, kc=kc, dc=dc, bk_=bk_: e.matmul(
                                out=SBp[:, bk_ * 512:bk_ * 512 + 512],
                                lhsT=wsl[:, wreg * 8192 + dc * 1024 + kc * 128: wreg * 8192 + dc * 1024 + (kc + 1) * 128],
                                rhs=lnT[:, dc * 512:(dc + 1) * 512],
                                start=(dc == 0), stop=(dc == 7)),
                                waits=w if dc == 0 else (), incs=[(PE, 1)] if dc == 7 else ())
                        pe_mm = PE.n
                        r = E("scalar", lambda e, kc=kc, bk_=bk_, bc=biascol, sc=scale, df=evac_dst_fn, sf=src_shape_fn: e.activation(
                            out=df(kc), in_=sf(bk_),
                            func=AF.Identity, bias=bc[:, kc:kc + 1], scale=sc),
                            waits=[(PE, pe_mm)], incs=[(AC, 1)])
                        ac_sbbank[bk_] = r[AC]
                        last_ac = r[AC]
                    return last_ac

                ac_q = proj_fm(0, bq32, INVSQ,
                               lambda kc: Qs[:, kc * 512:(kc + 1) * 512],
                               [(QG, qg_val)])
                for ss in range(2):
                    src = Qs.ap().rearrange("p (dcc t) -> p dcc t", dcc=8)[:, :, ss * 256:(ss + 1) * 256]
                    qg_val = dma("gpsimd", QG, lambda e, seq=seq, g=g, ss=ss, s=src: e.dma_start(
                        out=q_scr[seq, 2 * g + ss], in_=s), waits=[(AC, ac_q)])

                ktr_view = Ktr.ap().rearrange("p (tl ck) -> p tl ck", tl=4)
                sb3 = SBp.ap().rearrange("p (bb tl t) -> p bb tl t", bb=2, tl=4)
                ac_k = proj_fm(1, bkc, 1.0,
                               lambda kc: ktr_view[:, :, kc * 128:(kc + 1) * 128],
                               [(KG, kg_val)],
                               src_shape_fn=lambda bk_: sb3[:, bk_, :, :])
                for tl in range(4):
                    kg_val = dma("gpsimd", KG, lambda e, seq=seq, g=g, tl=tl: e.dma_start(
                        out=k_scr[seq, 4 * g + tl], in_=Ktr[:, tl * 1024:(tl + 1) * 1024]),
                        waits=[(AC, ac_k)])

                for sub in range(4):
                    t = 4 * g + sub
                    pidx = sub % 2
                    for h in range(2):
                        ps = OTp[pidx * 2 + h]
                        w = [dv_obank[pidx], chains[g][sub]]
                        for dc in range(8):
                            E("tensor", lambda e, sub=sub, h=h, dc=dc, ps=ps: e.matmul(
                                out=ps[:],
                                lhsT=lnT[:, dc * 512 + sub * 128:dc * 512 + (sub + 1) * 128],
                                rhs=wsl[:, 2 * 8192 + dc * 1024 + h * 512: 2 * 8192 + dc * 1024 + (h + 1) * 512],
                                start=(dc == 0), stop=False),
                                waits=w if dc == 0 else ())
                        E("tensor", lambda e, h=h, ps=ps: e.matmul(
                            out=ps[:],
                            lhsT=onesrow[0:1, 0:128],
                            rhs=bvrow[0:1, h * 512:(h + 1) * 512],
                            start=False, stop=True), incs=[(PE, 1)])
                    pe_v = PE.n
                    r = None
                    for h in range(2):
                        r = E("scalar", lambda e, t=t, h=h, ps=OTp[pidx * 2 + h]: e.activation(
                            out=V_buf[:, t * 1024 + h * 512:t * 1024 + (h + 1) * 512],
                            in_=ps[:], func=AF.Identity, scale=1.0),
                            waits=[(PE, pe_v)], incs=[(AC, 1)] if h == 1 else ())
                    dv_obank[pidx] = (AC, r[AC])
                    if g + 1 < NGROUP:
                        emit_T(g + 1, sub)
                        if sub < 2:
                            emit_chain(g + 1, sub + 2)
                if do_quant and g == NGROUP - 1:
                    quant_chunk(g)

        def phase_b(seq, EB):
            x2s = {0: x2, 1: t2}
            wl_val = 0
            for rg, src in ((0, wT_scr["wo"]), (1, wfq_h)):
                wl_val = dma("sync", WL, lambda e, rg=rg, s=src: e.dma_start(
                    out=wsl[:, rg * 8192:(rg + 1) * 8192], in_=s))

            st = {
                "ys": {i: 0 for i in range(4)},
                "pe_slot": {i: 0 for i in range(4)},
                "pe_flashend": {},
                "pe_pvlast": {},
                "ac_otevac": 0,
                "ac_otevac_b": {i: 0 for i in range(4)},
                "dv_rl": {},
                "ac_rl_user": 0,
                "ac_eb": 0,
                "pe_eb_reader": 0,
                "dv_sqb": {0: (DV, 0), 1: (DV, 0)},
                "pe_ln2T": {0: 0, 1: 0},
                "pe_ffn": {0: 0, 1: 0},
                "dv_xat": {0: 0, 1: 0},
                "dv_t1": {0: 0, 1: 0},
                "qs": {0: 0, 1: 0},
                "xs": {},
                "ks": {},
                "ot_sb_reader": 0,
                "ac_abs_bank": {0: 0, 1: 0},
                "pe_pv_slot": {0: 0, 1: 0},
            }
            pend = []

            def make_epilogue(s):
                pre = []
                main = []
                spill = []

                def t_rl():
                    r = E("vector", lambda e: e.reciprocal(out=rl[:], in_=LBp[:, 0:2]),
                          waits=[(PE, st["pe_pvlast"][s]), (AC, st["ac_rl_user"])],
                          incs=[(DV, 1)])
                    st["dv_rl"][s] = r[DV]
                pre.append(t_rl)

                def t_otevac(b):
                    def f():
                        r = E("scalar", lambda e, b=b: e.activation(
                            out=OT_sb[:, b * 512:(b + 1) * 512], in_=OTp[b][:],
                            func=AF.Identity, scale=1.0),
                            waits=[(PE, st["pe_pvlast"][s]), (PE, st["ot_sb_reader"])],
                            incs=[(AC, 1)])
                        st["ac_otevac"] = r[AC]
                        st["ac_otevac_b"][b] = r[AC]
                    return f
                for b_ in range(4):
                    pre.append(t_otevac(b_))

                boxes = {0: {}, 1: {}}

                def outproj_h(qc, h):
                    def f():
                        box = boxes[qc]
                        w = [(AC, st["ac_otevac"]), (AC, st["ac_eb"]), (WL, wl_val)]
                        for vc in range(8):
                            E("tensor", lambda e, h=h, vc=vc, qc=qc: e.matmul(
                                out=EB[:],
                                lhsT=OT_sb[:, vc * 256 + qc * 128:vc * 256 + (qc + 1) * 128],
                                rhs=wsl[:, vc * 1024 + h * 512:vc * 1024 + (h + 1) * 512],
                                start=(vc == 0), stop=(vc == 7)),
                                waits=w if vc == 0 else (), incs=[(PE, 1)] if vc == 7 else ())
                        st["ot_sb_reader"] = PE.n
                        pe_mm = PE.n
                        r = E("scalar", lambda e, h=h, qc=qc: e.activation(
                            out=t1[:, qc * 1024 + h * 512:qc * 1024 + (h + 1) * 512], in_=EB[:],
                            func=AF.Identity, scale=rl[:, qc:qc + 1]),
                            waits=[(PE, pe_mm), (DV, st["dv_rl"][s]), (DV, st["dv_t1"][qc])],
                            incs=[(AC, 1)])
                        st["ac_eb"] = r[AC]
                        st["ac_rl_user"] = r[AC]
                        box[f"ev{h}"] = r[AC]
                    return f

                def lnattn_a(qc):
                    def f():
                        box = boxes[qc]
                        t1s = t1[:, qc * 1024:(qc + 1) * 1024]
                        dvb = E("vector", lambda e, t1s=t1s: e.tensor_add(out=t1s, in0=t1s, in1=bo_tile[:]),
                                waits=[(AC, box["ev1"])], incs=[(DV, 1)])[DV]
                        aca, dvs, dv1 = emit_ln(t1s, sqb[:, qc * 1024:(qc + 1) * 1024], sq_slot=qc,
                                                waits_src=[(DV, dvb)],
                                                waits_sq=[st["dv_sqb"][qc],
                                                          (PE, st["pe_ln2T"][qc])],
                                                apply_eng="vector")
                        box["dv_ln1"] = dv1
                        box["ac_ln1"] = aca
                    return f

                def lnattn_b(qc):
                    def f():
                        box = boxes[qc]
                        t = 2 * s + qc
                        sl4 = t % 4
                        dv_x2 = E("vector", lambda e, sl4=sl4, qc=qc: e.tensor_add(
                            out=x2s[qc][:], in0=xb[:, sl4 * 1024:(sl4 + 1) * 1024],
                            in1=sqb[:, qc * 1024:(qc + 1) * 1024]),
                            waits=[box["ac_ln1"], (XS[sl4], st["xs"][t])], incs=[(DV, 1)])[DV]
                        ac2, dvs2, _ = emit_ln(x2s[qc][:], sqb[:, qc * 1024:(qc + 1) * 1024], sq_slot=qc,
                                               waits_src=[(DV, dv_x2)],
                                               waits_sq=[(DV, dv_x2)],
                                               waits_out=[(PE, st["pe_ln2T"][qc])],
                                               apply_eng="vector")
                        box["ac_ln2"] = ac2
                        st["dv_sqb"][qc] = (DV, dvs2)
                    return f

                def ln2t_pe(qc):
                    def f():
                        box = boxes[qc]
                        for rnd in range(2):
                            wfirst = [box["ac_ln2"], (AC, st["ac_eb"])]
                            for c in range(4):
                                dc = 4 * rnd + c
                                E("tensor", lambda e, qc=qc, dc=dc, c=c: e.transpose(
                                    out=EB[:, c * 128:(c + 1) * 128],
                                    in_=sqb[:, qc * 1024 + dc * 128:qc * 1024 + (dc + 1) * 128],
                                    identity=identf[:]),
                                    waits=wfirst if c == 0 else (), incs=[(PE, 1)] if c == 3 else ())
                            pe_t = PE.n
                            r2 = E("scalar", lambda e, qc=qc, rnd=rnd: e.activation(
                                out=lnT[:, qc * 1024 + rnd * 512:qc * 1024 + (rnd + 1) * 512],
                                in_=EB[:, 0:512], func=AF.Identity, scale=1.0),
                                waits=[(PE, pe_t)], incs=[(AC, 1)])
                            st["ac_eb"] = r2[AC]
                        st["pe_ln2T"][qc] = PE.n
                        box["ac_lnT"] = st["ac_eb"]
                    return f

                def xa_stage(qc):
                    def f():
                        box = boxes[qc]
                        w = [(AC, box["ac_lnT"]), (AC, st["ac_eb"]),
                             (DV, st["dv_rl"][s])]
                        for dc in range(8):
                            E("tensor", lambda e, dc=dc, qc=qc: e.matmul(
                                out=EB[0:16, 0:128],
                                lhsT=a_bf[:, dc * 16:(dc + 1) * 16],
                                rhs=lnT[:, qc * 1024 + dc * 128:qc * 1024 + (dc + 1) * 128],
                                start=(dc == 0), stop=(dc == 7)),
                                waits=w if dc == 0 else (), incs=[(PE, 1)] if dc == 7 else ())
                        pe_xa = PE.n
                        r = E("vector", lambda e, qc=qc: e.tensor_copy(
                            out=xaT[:, qc * 128:(qc + 1) * 128], in_=EB[0:16, 0:128]),
                            waits=[(PE, pe_xa)], incs=[(DV, 1)])
                        st["dv_xat"][qc] = r[DV]
                    return f

                def ffn_h(qc, h):
                    def f():
                        box = boxes[qc]
                        w = [(DV, st["dv_xat"][qc]), (AC, st["ac_eb"])]
                        for dc in range(8):
                            E("tensor", lambda e, h=h, dc=dc, qc=qc: e.matmul(
                                out=EB[:],
                                lhsT=lnT[:, qc * 1024 + dc * 128:qc * 1024 + (dc + 1) * 128],
                                rhs=wsl[:, 8192 + dc * 1024 + h * 512:8192 + dc * 1024 + (h + 1) * 512],
                                start=(dc == 0), stop=False),
                                waits=w if dc == 0 else ())
                        E("tensor", lambda e, h=h: e.matmul(
                            out=EB[:], lhsT=onesrow[0:1, 0:128], rhs=fbrow[0:1, h * 512:(h + 1) * 512],
                            start=False, stop=False))
                        E("tensor", lambda e, h=h, qc=qc: e.matmul(
                            out=EB[:], lhsT=xaT[0:16, qc * 128:(qc + 1) * 128],
                            rhs=b_bf[0:16, h * 512:(h + 1) * 512],
                            start=False, stop=True), incs=[(PE, 1)])
                        st["pe_ffn"][qc] = PE.n
                        pe_mm = PE.n
                        r = E("scalar", lambda e, h=h, qc=qc: e.activation(
                            out=t1[:, qc * 1024 + h * 512:qc * 1024 + (h + 1) * 512], in_=EB[:],
                            func=AF.Identity, scale=1.0),
                            waits=[(PE, pe_mm), (DV, box["dv_ln1"])], incs=[(AC, 1)])
                        st["ac_eb"] = r[AC]
                        box[f"fev{h}"] = r[AC]
                    return f

                def lnffn_chain(qc):
                    def f():
                        box = boxes[qc]
                        t = 2 * s + qc
                        sl4 = t % 4
                        acf, dvs3, dv1f = emit_ln(t1[:, qc * 1024:(qc + 1) * 1024],
                                                  sqb[:, qc * 1024:(qc + 1) * 1024], sq_slot=qc,
                                                  waits_src=[(AC, box["fev1"])],
                                                  waits_sq=[st["dv_sqb"][qc],
                                                            (PE, st["pe_ln2T"][qc])],
                                                  apply_eng="vector")
                        st["dv_t1"][qc] = dv1f
                        dv_fin = E("vector", lambda e, sl4=sl4, qc=qc: e.tensor_add(
                            out=xb[:, sl4 * 1024:(sl4 + 1) * 1024],
                            in0=x2s[qc][:], in1=sqb[:, qc * 1024:(qc + 1) * 1024]),
                            waits=[acf], incs=[(DV, 1)])[DV]
                        st["dv_sqb"][qc] = (DV, dv_fin)
                        box["dv_fin"] = dv_fin
                    return f

                def store_stage(qc):
                    def f():
                        box = boxes[qc]
                        t = 2 * s + qc
                        sl4 = t % 4
                        st["ys"][sl4] = dma("gpsimd", YS[sl4], lambda e, seq=seq, t=t, sl4=sl4: e.dma_start(
                            out=y_out[seq, t * 128:(t + 1) * 128, :],
                            in_=xb[:, sl4 * 1024:(sl4 + 1) * 1024]),
                            waits=[(DV, box["dv_fin"])])
                        if t + 4 < 2 * NSTRIP:
                            st["xs"][t + 4] = dma("sync", XS[sl4], lambda e, seq=seq, t=t + 4, sl4=sl4: e.dma_start(
                                out=xb[:, sl4 * 1024:(sl4 + 1) * 1024],
                                in_=x_in[seq, t * 128:(t + 1) * 128, :]),
                                waits=[(YS[sl4], st["ys"][sl4])])
                    return f

                main.append((0.36, outproj_h(0, 0)))
                main.append((0.40, outproj_h(0, 1)))
                main.append((0.44, outproj_h(1, 0)))
                main.append((0.48, outproj_h(1, 1)))
                main.append((0.50, lnattn_a(0)))
                main.append((0.53, lnattn_a(1)))
                main.append((0.56, lnattn_b(0)))
                main.append((0.60, lnattn_b(1)))
                main.append((0.74, ln2t_pe(0)))
                main.append((0.78, xa_stage(0)))
                main.append((0.81, ffn_h(0, 0)))
                main.append((0.85, ffn_h(0, 1)))
                main.append((0.89, lnffn_chain(0)))
                spill.append((0.04, ln2t_pe(1)))
                spill.append((0.07, xa_stage(1)))
                if st.get("store1_thunk") is not None:
                    spill.append((0.07, st["store1_thunk"]))
                spill.append((0.10, ffn_h(1, 0)))
                spill.append((0.14, ffn_h(1, 1)))
                spill.append((0.18, lnffn_chain(1)))
                spill.append((0.28, store_stage(0)))
                st["store1_thunk"] = store_stage(1)
                return pre, main, spill

            epi_pre = []
            pend = []
            spill_next = []

            for s in range(NSTRIP):
                st["qs"][s % 2] = dma("sync", QS[s % 2], lambda e, seq=seq, s=s: e.dma_start(
                    out=Qs[:, (s % 2) * 2048:(s % 2 + 1) * 2048], in_=q_scr[seq, s]),
                    waits=[(PE, st["pe_flashend"].get(s - 2, 0))])
                if s < 2:
                    for qc in range(2):
                        t = 2 * s + qc
                        st["xs"][t] = dma("sync", XS[t % 4], lambda e, seq=seq, t=t: e.dma_start(
                            out=xb[:, (t % 4) * 1024:(t % 4 + 1) * 1024],
                            in_=x_in[seq, t * 128:(t + 1) * 128, :]))
                nt = 2 * s + 2

                for f in epi_pre:
                    f()
                epi_pre = []

                ac_abs = {}
                ac_exp = {}
                pe_pv = {}
                qof = (s % 2) * 2048
                n_pend = len(pend)
                popped = 0

                def emit_S(j):
                    sp = j % 2
                    w = [(KS[j % 4], st["ks"][j]),
                         (AC, ac_abs.get(j - 2, st["ac_abs_bank"][sp]))]
                    if j == 0:
                        w += [(QS[s % 2], st["qs"][s % 2])]
                    for dc in range(8):
                        E("tensor", lambda e, j=j, dc=dc, sp=sp, qof=qof: e.matmul(
                            out=SBp[:, sp * 512:sp * 512 + 256],
                            lhsT=Ktr[:, (j % 4) * 1024 + dc * 128:(j % 4) * 1024 + (dc + 1) * 128],
                            rhs=Qs[:, qof + dc * 256:qof + (dc + 1) * 256],
                            start=(dc == 0), stop=(dc == 7)),
                            waits=w if dc == 0 else (), incs=[(PE, 1)] if dc == 7 else ())
                    st["pe_slot"][j % 4] = PE.n
                    pe_sT = PE.n
                    r = E("scalar", lambda e, sp=sp: e.activation(
                        out=Sa[:, sp * 256:(sp + 1) * 256], in_=SBp[:, sp * 512:sp * 512 + 256],
                        func=AF.Abs, scale=1.0), waits=[(PE, pe_sT)], incs=[(AC, 1)])
                    ac_abs[j] = r[AC]
                    st["ac_abs_bank"][sp] = r[AC]
                    wexp = [(AC, ac_abs[j])]
                    if j >= 2 * s:
                        r = E("vector", lambda e, sp=sp, ms=(j - 2 * s): e.tensor_tensor(
                            out=Sa[:, sp * 256:(sp + 1) * 256], in0=Sa[:, sp * 256:(sp + 1) * 256],
                            in1=masks[:, ms * 256:(ms + 1) * 256], op=ALU.add),
                            waits=[(AC, ac_abs[j]), (CONST, ld_consts)], incs=[(DV, 1)])
                        wexp.append((DV, r[DV]))
                    wexp.append((PE, pe_pv.get(j - 2, st["pe_pv_slot"][sp])))
                    r = E("scalar", lambda e, sp=sp: e.activation(
                        out=PT[:, sp * 256:(sp + 1) * 256], in_=Sa[:, sp * 256:(sp + 1) * 256],
                        func=AF.Exp, scale=1.0),
                        waits=wexp, incs=[(AC, 1)])
                    ac_exp[j] = r[AC]

                def emit_PV(j):
                    sp = j % 2
                    last = (j == nt - 1)
                    frst = (j == 0)
                    w = [(AC, ac_exp[j])]
                    first = True
                    for vc in range(8):
                        b_ = vc // 2
                        col = (vc % 2) * 256
                        wv = list(w) if first else []
                        if frst and vc % 2 == 0:
                            wv.append((AC, st["ac_otevac_b"][b_]))
                        E("tensor", lambda e, j=j, sp=sp, vc=vc, b_=b_, col=col,
                          stt=(frst and vc % 2 == 0),
                          lst=(last and vc % 2 == 1): e.matmul(
                            out=OTp[b_][:, col:col + 256],
                            lhsT=V_buf[:, j * 1024 + vc * 128:j * 1024 + (vc + 1) * 128],
                            rhs=PT[:, sp * 256:(sp + 1) * 256],
                            start=stt, stop=lst),
                            waits=wv)
                        first = False
                    for qc in range(2):
                        wl_ = []
                        if frst and qc == 0:
                            wl_ = [(DV, st["dv_rl"].get(s - 1, 0))]
                        E("tensor", lambda e, j=j, sp=sp, qc=qc, stt=(frst and qc == 0),
                          lst=(last and qc == 1): e.matmul(
                            out=LBp[:, qc:qc + 1],
                            lhsT=PT[:, sp * 256 + qc * 128:sp * 256 + (qc + 1) * 128],
                            rhs=onescol[:, 0:1], start=stt, stop=lst),
                            waits=wl_, incs=[(PE, 1)] if qc == 1 else ())
                    pe_pv[j] = PE.n
                    st["pe_pv_slot"][sp] = PE.n

                for j in range(nt):
                    st["ks"][j] = dma("gpsimd", KS[j % 4], lambda e, seq=seq, j=j: e.dma_start(
                        out=Ktr[:, (j % 4) * 1024:(j % 4 + 1) * 1024], in_=k_scr[seq, j]),
                        waits=[(PE, st["pe_slot"][j % 4])])
                    emit_S(j)
                    if j > 0:
                        emit_PV(j - 1)
                    while popped < n_pend and int(pend[popped][0] * nt) <= j:
                        pend[popped][1]()
                        popped += 1
                emit_PV(nt - 1)
                while popped < n_pend:
                    pend[popped][1]()
                    popped += 1
                st["pe_flashend"][s] = PE.n
                st["pe_pvlast"][s] = PE.n
                epi_pre, e_main, e_spill = make_epilogue(s)
                pend = list(spill_next) + list(e_main)
                spill_next = e_spill

            for f in epi_pre:
                f()
            for _, f in pend:
                f()
            for _, f in spill_next:
                f()
            if st.get("store1_thunk") is not None:
                st["store1_thunk"]()
                st["store1_thunk"] = None

        def with_bank7(name, shape, dtype, fn):
            g = nc.psum_tensor(name, shape, dtype)
            t = g.__enter__()
            fn(t)
            g.__exit__(None, None, None)

        with_bank7("TRA0", [128, 1024], BF16, lambda t: phase_a(0, False, t, do_quant=True))
        barrier()
        if debug:
            dma("gpsimd", WT, lambda e: e.dma_start(out=v_dbg, in_=V_buf[:]))
        with_bank7("EB0", [128, 512], F32, lambda t: phase_b(0, t))
        barrier()
        with_bank7("TRA1", [128, 1024], BF16, lambda t: phase_a(1, True, t))
        barrier()
        with_bank7("EB1", [128, 512], F32, lambda t: phase_b(1, t))
        barrier()
        fin = [(s, s.n) for s in B.queue_sems["gpsimd"]]
        E("gpsimd", lambda e: e.nop(), waits=fin, chain=False)

        def replay(eng_name):
            def go(eng):
                for fn, waits, incs in B.ops[eng_name]:
                    for sh, v in waits:
                        eng.wait_ge(sh, v)
                    inst = fn(eng)
                    if incs:
                        for sh, amt in incs:
                            inst.then_inc(sh, amt)
            return go

        block.tensor(replay("tensor"))
        block.vector(replay("vector"))
        block.scalar(replay("scalar"))
        block.sync(replay("sync"))
        block.gpsimd(replay("gpsimd"))

    return nc


_CACHE = {}


def kernel(**inputs):
    x = np.ascontiguousarray(np.asarray(inputs["L0_attn_x"], dtype=np.float32))
    wmap = {
        "wq": inputs["L0_attn_Wq"], "wk": inputs["L0_attn_Wk"],
        "wv": inputs["L0_attn_Wv"], "wo": inputs["L0_attn_Wo"],
        "fk": inputs["L0_ffn_kernel"],
    }
    bmap = {
        "bq": inputs["L0_attn_bq"], "bk": inputs["L0_attn_bk"],
        "bv": inputs["L0_attn_bv"], "bo": inputs["L0_attn_bo"],
        "fbias": inputs["L0_ffn_bias"],
    }
    base = {}
    for k, v in wmap.items():
        base[k] = np.ascontiguousarray(np.asarray(v, dtype=np.float32))
    for k, v in bmap.items():
        base[k] = np.ascontiguousarray(np.asarray(v, dtype=np.float32)).reshape(1, D)
    base["fa"] = np.ascontiguousarray(np.asarray(inputs["L0_ffn_a"], dtype=np.float32))
    base["fb"] = np.ascontiguousarray(np.asarray(inputs["L0_ffn_b"], dtype=np.float32))

    if "nc" not in _CACHE:
        _CACHE["nc"] = build_nc()
    nc = _CACHE["nc"]

    in_maps = []
    for c in range(8):
        m = dict(base)
        m["x"] = np.ascontiguousarray(x[2 * c:2 * c + 2])
        in_maps.append(m)

    res = run_bass_kernel_spmd(nc, in_maps, core_ids=list(range(8)))
    y = np.concatenate([r["y"] for r in res.results], axis=0)
    return y.reshape(16, SEQ, D)


if __name__ == "__main__":
    build_nc()
    print("build ok")


# version 25
# speedup vs baseline: 1.0180x; 1.0180x over previous
import numpy as np
import ml_dtypes

import concourse.bass as bass
import concourse.mybir as mybir
from concourse.bass_utils import run_bass_kernel_spmd

F32 = mybir.dt.float32
BF16 = mybir.dt.bfloat16
AF = mybir.ActivationFunctionType
ALU = mybir.AluOpType
AX = mybir.AxisListType

D = 1024
SEQ = 4096
NSEQ = 2
NTILE = SEQ // 128
NGROUP = SEQ // 512
NSTRIP = SEQ // 256
NEGBIG = -1.0e30
INVSQ = 1.0 / 32.0

ENGS = ("tensor", "vector", "scalar", "sync", "gpsimd")


class Sem:
    def __init__(self, handle):
        self.h = handle
        self.n = 0


class Builder:
    def __init__(self):
        self.ops = {e: [] for e in ENGS}
        self.queue_sems = {"sync": {}, "gpsimd": {}, "scalar": {}}
        self.eng_sem = {}

    def E(self, eng, fn, waits=(), incs=(), chain=True):
        waits = list(waits)
        incs = list(incs)
        es = self.eng_sem.get(eng) if chain else None
        if es is not None:
            waits.append((es, es.n))
            if not any(s is es for s, _ in incs):
                incs.append((es, 1))
        rincs = []
        out = {}
        for s, amt in incs:
            s.n += amt
            rincs.append((s.h, amt))
            out[s] = s.n
        self.ops[eng].append(
            (fn, tuple((s.h, int(v)) for s, v in waits if v > 0), tuple(rincs)))
        return out

    def dma(self, eng, sem, fn, waits=()):
        sem.n += 16
        self.queue_sems[eng][sem] = True
        self.ops[eng].append(
            (fn, tuple((s.h, int(v)) for s, v in waits if v > 0), ((sem.h, 16),)))
        return sem.n


def build_nc(debug=False):
    nc = bass.Bass("TRN2", target_bir_lowering=False)
    scr_kind = "ExternalOutput" if debug else "Internal"

    x_in = nc.dram_tensor("x", [NSEQ, SEQ, D], F32, kind="ExternalInput").ap()
    w_h = {n: nc.dram_tensor(n, [D, D], F32, kind="ExternalInput").ap()
           for n in ("wq", "wk", "wv", "wo", "fk")}
    b_h = {n: nc.dram_tensor(n, [1, D], F32, kind="ExternalInput").ap()
           for n in ("bq", "bk", "bv", "bo", "fbias")}
    fa_h = nc.dram_tensor("fa", [D, 16], F32, kind="ExternalInput").ap()
    fb_h = nc.dram_tensor("fb", [16, D], F32, kind="ExternalInput").ap()
    y_out = nc.dram_tensor("y", [NSEQ, SEQ, D], F32, kind="ExternalOutput").ap()

    q_scr = nc.dram_tensor("q_scr", [NSEQ, NSTRIP, 128, 2048], BF16, kind=scr_kind).ap()
    k_scr = nc.dram_tensor("k_scr", [NSEQ, NTILE, 128, 1024], BF16, kind=scr_kind).ap()
    v_dbg = nc.dram_tensor("v_dbg", [128, NTILE * 1024], BF16, kind="ExternalOutput").ap() if debug else None
    wT_scr = {n: nc.dram_tensor(n + "T_h", [128, 8192], BF16, kind="Internal").ap()
              for n in ("wq", "wk", "wv", "wo")}
    wfq_h = nc.dram_tensor("wfq_h", [128, 8192], BF16, kind=scr_kind).ap()

    ident_b_h = nc.inline_tensor(np.eye(128, dtype=ml_dtypes.bfloat16), "ident_b").ap()
    ident_f_h = nc.inline_tensor(np.eye(128, dtype=np.float32), "ident_f").ap()
    mknp = np.empty((128, 512), dtype=np.float32)
    kk = np.arange(128)[:, None]
    qq = np.arange(256)[None, :]
    mknp[:, 0:256] = np.where(kk <= qq, 0.0, NEGBIG)
    mknp[:, 256:512] = np.where(128 + kk <= qq, 0.0, NEGBIG)
    masks_h = nc.inline_tensor(mknp, "masks_h").ap()

    from contextlib import ExitStack
    with ExitStack() as ctx:
        ec = ctx.enter_context

        V_buf = ec(nc.sbuf_tensor("V_buf", [128, NTILE * 1024], BF16))
        wsl = ec(nc.sbuf_tensor("wsl", [128, 3 * 8192], BF16))
        a_bf = ec(nc.sbuf_tensor("a_bf", [128, 128], BF16))
        b_bf = ec(nc.sbuf_tensor("b_bf", [16, 1024], BF16))
        ident = ec(nc.sbuf_tensor("ident", [128, 128], BF16))
        identf = ec(nc.sbuf_tensor("identf", [128, 128], F32))
        masks = ec(nc.sbuf_tensor("masks", [128, 512], F32))
        bq32 = ec(nc.sbuf_tensor("bq32", [128, 8], F32))
        bkc = ec(nc.sbuf_tensor("bkc", [128, 8], F32))
        bvrow = ec(nc.sbuf_tensor("bvrow", [1, 1024], BF16))
        fbrow = ec(nc.sbuf_tensor("fbrow", [1, 1024], BF16))
        bo_tile = ec(nc.sbuf_tensor("bo_tile", [128, 1024], F32))
        onesrow = ec(nc.sbuf_tensor("onesrow", [1, 256], BF16))
        onesrowf = ec(nc.sbuf_tensor("onesrowf", [1, 128], F32))
        onescol = ec(nc.sbuf_tensor("onescol", [128, 1], BF16))
        zrow = ec(nc.sbuf_tensor("zrow", [1, 512], BF16))
        epscol = ec(nc.sbuf_tensor("epscol", [128, 1], F32))
        rowstage = ec(nc.sbuf_tensor("rowstage", [1, 1024], F32))
        stats = ec(nc.sbuf_tensor("stats", [128, 32], F32))
        gq = ec(nc.sbuf_tensor("gq", [128, 16], F32))
        rl = ec(nc.sbuf_tensor("rl", [128, 2], F32))

        xb = ec(nc.sbuf_tensor("xb", [128, 4096], F32))
        sqb = ec(nc.sbuf_tensor("sqb", [128, 2048], F32))
        lnb = ec(nc.sbuf_tensor("lnb", [128, 2048], BF16))
        lnT = ec(nc.sbuf_tensor("lnT", [128, 4096], BF16))
        Qs = ec(nc.sbuf_tensor("Qs", [128, 4096], BF16))
        Ktr = ec(nc.sbuf_tensor("Ktr", [128, 4096], BF16))
        Sa = ec(nc.sbuf_tensor("Sa", [128, 512], F32))
        PT = ec(nc.sbuf_tensor("PT", [128, 512], BF16))
        OT_sb = ec(nc.sbuf_tensor("OT_sb", [128, 2048], BF16))
        t1 = ec(nc.sbuf_tensor("t1", [128, 2048], F32))
        t2 = ec(nc.sbuf_tensor("t2", [128, 1024], F32))
        x2 = ec(nc.sbuf_tensor("x2", [128, 1024], F32))
        xaT = ec(nc.sbuf_tensor("xaT", [16, 256], BF16))

        OTp = [ec(nc.psum_tensor(f"OTp{i}", [128, 512], F32)) for i in range(4)]
        SBp = ec(nc.psum_tensor("SBp", [128, 1024], F32))
        LBp = ec(nc.psum_tensor("LBp", [128, 512], F32))
        trs_guard = nc.psum_tensor("TRs", [128, 1024], BF16)
        TRs = trs_guard.__enter__()

        def mksem(name):
            return Sem(ec(nc.semaphore(name)))

        PE, DV, AC, GP, BAR = (mksem(n) for n in ("PE", "DV", "AC", "GP", "BAR"))
        XS = [mksem(f"XS{i}") for i in range(4)]
        YS = [mksem(f"YS{i}") for i in range(4)]
        QS = [mksem(f"QS{i}") for i in range(2)]
        KS = [mksem(f"KS{i}") for i in range(4)]
        WS = [mksem(f"WS{i}") for i in range(2)]
        FK = [mksem(f"FK{i}") for i in range(2)]
        FS = [mksem(f"FS{i}") for i in range(2)]
        WR = mksem("WR")
        TS = [mksem(f"TS{i}") for i in range(2)]
        TA, TB = mksem("TA"), mksem("TB")
        QG, KG, WL, RS, CONST, WT = (mksem(n) for n in ("QG", "KG", "WL", "RS", "CONST", "WT"))

        B = Builder()
        B.eng_sem = {"vector": DV, "scalar": AC, "gpsimd": GP}
        E, dma = B.E, B.dma

        block = ec(nc.Block())

        def barrier():
            k = BAR.n // 5 + 1
            for eng in ENGS:
                w = []
                if eng in ("sync", "gpsimd", "scalar"):
                    w = [(s, s.n) for s in B.queue_sems[eng]]

                def f(e, k=k, bh=BAR.h):
                    e.drain()
                    e.sem_inc(bh, 1)
                    e.wait_ge(bh, 5 * k)
                E(eng, f, waits=w, chain=False)
            BAR.n = 5 * k

        ln_slot = [0]

        def emit_ln(src_ap, out_ap, sq_slot, waits_src=(), waits_sq=(), waits_out=(),
                    veng="vector", vsem=None, apply_eng="scalar"):
            sl = (ln_slot[0] % 4) * 8
            ln_slot[0] += 1
            cS1 = stats[:, sl + 0:sl + 1]
            cMn = stats[:, sl + 1:sl + 2]
            cS2 = stats[:, sl + 2:sl + 3]
            cM2 = stats[:, sl + 3:sl + 4]
            cB2 = stats[:, sl + 4:sl + 5]
            cLn = stats[:, sl + 5:sl + 6]
            cR = stats[:, sl + 6:sl + 7]
            cBn = stats[:, sl + 7:sl + 8]
            sq = sqb[:, sq_slot * 1024:(sq_slot + 1) * 1024]
            VS = vsem if vsem is not None else DV

            E(veng, lambda e, s=src_ap, o=cS1: e.reduce_sum(out=o, in_=s, axis=AX.X),
              waits=waits_src)
            rsq = E(veng,
                    lambda e, s=src_ap, o=sq, a=cS2: e.scalar_tensor_tensor(
                        out=o, in0=s, scalar=1.0, in1=s, op0=ALU.mult, op1=ALU.mult, accum_out=a),
                    waits=tuple(waits_src) + tuple(waits_sq))
            r2 = E(veng, lambda e, i=cS1, o=cMn: e.tensor_scalar_mul(out=o, in0=i, scalar1=-1.0 / D))
            E("scalar", lambda e, i=cMn, o=cM2: e.activation(out=o, in_=i, func=AF.Square, scale=1.0),
              waits=[(VS, r2[VS])])
            E("scalar", lambda e, i=cM2, o=cB2, b=epscol[:, 0:1]: e.activation(
                out=o, in_=i, func=AF.Identity, bias=b, scale=-1.0))
            E("scalar", lambda e, i=cS2, o=cLn, b=cB2: e.activation(
                out=o, in_=i, func=AF.Ln, bias=b, scale=1.0 / D),
                waits=[(VS, rsq[VS])])
            ax = E("scalar", lambda e, i=cLn, o=cR: e.activation(out=o, in_=i, func=AF.Exp, scale=-0.5),
                   incs=[(AC, 1)])
            r4 = E(veng, lambda e, a=cMn, bb=cR, o=cBn: e.tensor_mul(out=o, in0=a, in1=bb),
                   waits=[(AC, ax[AC])])
            if apply_eng == "scalar":
                a3 = E("scalar",
                       lambda e, s=src_ap, o=out_ap, sc=cR, b=cBn: e.activation(out=o, in_=s, func=AF.Identity, bias=b, scale=sc),
                       waits=tuple(waits_out) + ((VS, r4[VS]),), incs=[(AC, 1)])
                return (AC, a3[AC]), rsq[VS], rsq[VS]
            a3 = E("vector",
                   lambda e, s=src_ap, o=out_ap, sc=cR, b=cBn: e.tensor_scalar(
                       out=o, in0=s, scalar1=sc, scalar2=b, op0=ALU.mult, op1=ALU.add),
                   waits=waits_out, incs=[(DV, 1)])
            return (DV, a3[DV]), rsq[VS], a3[DV]

        for dst, src in ((ident[:], ident_b_h), (identf[:], ident_f_h), (masks[:], masks_h)):
            dma("sync", CONST, lambda e, d=dst, s=src: e.dma_start(out=d, in_=s))
        ld_consts = CONST.n

        for fn in (lambda e: e.memset(onesrow[:], 1.0),
                   lambda e: e.memset(onesrowf[:], 1.0),
                   lambda e: e.memset(onescol[:], 1.0),
                   lambda e: e.memset(zrow[:], 0.0)):
            E("gpsimd", fn)
        gp_memset = E("gpsimd", lambda e: e.memset(epscol[:], 1e-5), incs=[(GP, 1)])[GP]

        pe_rowstage = 0
        dv_rowstage = 0

        def prep_colbias(src_h, dest, scale):
            nonlocal pe_rowstage, dv_rowstage
            dv_lbp_prev = DV.n
            rsv = dma("sync", RS, lambda e, s=src_h: e.dma_start(out=rowstage[:], in_=s),
                      waits=[(PE, pe_rowstage), (DV, dv_rowstage)])
            for c in range(8):
                w = [(RS, rsv), (GP, gp_memset), (DV, dv_lbp_prev)] if c == 0 else []
                E("tensor",
                  lambda e, c=c: e.matmul(out=LBp[:, c:c + 1],
                                          lhsT=rowstage[0:1, c * 128:(c + 1) * 128],
                                          rhs=onesrowf[0:1, 0:1], start=True, stop=True),
                  waits=w, incs=[(PE, 1)] if c == 7 else ())
            pe_rowstage = PE.n
            r = E("vector", lambda e, d=dest, sc=scale: e.tensor_scalar_mul(out=d, in0=LBp[:, 0:8], scalar1=sc),
                  waits=[(PE, pe_rowstage)], incs=[(DV, 1)])
            return r[DV]

        prep_colbias(b_h["bq"], bq32[:], INVSQ)
        prep_colbias(b_h["bk"], bkc[:], 1.0)

        def prep_rowbias(src_h, dest):
            nonlocal pe_rowstage, dv_rowstage
            rsv = dma("sync", RS, lambda e, s=src_h: e.dma_start(out=rowstage[:], in_=s),
                      waits=[(PE, pe_rowstage), (DV, dv_rowstage)])
            r = E("vector", lambda e, d=dest: e.tensor_copy(out=d, in_=rowstage[:]),
                  waits=[(RS, rsv)], incs=[(DV, 1)])
            dv_rowstage = r[DV]

        prep_rowbias(b_h["bv"], bvrow[:])
        prep_rowbias(b_h["fbias"], fbrow[:])

        dv_lbp_prev = DV.n
        rsv = dma("sync", RS, lambda e: e.dma_start(out=rowstage[:], in_=b_h["bo"]),
                  waits=[(PE, pe_rowstage), (DV, dv_rowstage)])
        for hh in range(2):
            E("tensor", lambda e, hh=hh: e.matmul(
                out=LBp[:, 0:512], lhsT=onesrowf[0:1, 0:128],
                rhs=rowstage[0:1, hh * 512:(hh + 1) * 512], start=True, stop=True),
                waits=[(RS, rsv), (DV, dv_lbp_prev)] if hh == 0 else [(DV, dv_lbp_prev)],
                incs=[(PE, 1)])
            dv_lbp_prev = E("vector", lambda e, hh=hh: e.tensor_copy(
                out=bo_tile[:, hh * 512:(hh + 1) * 512], in_=LBp[:, 0:512]),
                waits=[(PE, PE.n)], incs=[(DV, 1)])[DV]
        pe_rowstage = PE.n
        dv_rowstage = dv_lbp_prev

        for dc in range(8):
            dma("sync", XS[0], lambda e, dc=dc: e.dma_start(
                out=xb[:, dc * 16:(dc + 1) * 16], in_=fa_h[dc * 128:(dc + 1) * 128, :]))
        dv_a = E("vector", lambda e: e.tensor_copy(out=a_bf[:], in_=xb[:, 0:128]),
                 waits=[(XS[0], XS[0].n)], incs=[(DV, 1)])[DV]
        dma("sync", XS[1], lambda e: e.dma_start(out=xb[0:16, 1024:2048], in_=fb_h))
        dv_b = E("vector", lambda e: e.tensor_copy(out=b_bf[:], in_=xb[0:16, 1024:2048]),
                 waits=[(XS[1], XS[1].n)], incs=[(DV, 1)])[DV]

        dv_absmax = {}
        for c in range(8):
            sl = c % 4
            w = [(DV, max(dv_a, dv_b))] if c < 4 else [(DV, dv_absmax[c - 4])]
            xv = dma("sync", XS[sl], lambda e, c=c, sl=sl: e.dma_start(
                out=xb[:, sl * 1024:(sl + 1) * 1024],
                in_=w_h["fk"][c * 128:(c + 1) * 128, :]), waits=w)
            r = E("vector", lambda e, c=c, sl=sl: e.tensor_reduce(
                out=gq[:, c:c + 1], in_=xb[:, sl * 1024:(sl + 1) * 1024],
                axis=AX.X, op=ALU.max, apply_absolute_value=True),
                waits=[(XS[sl], xv)], incs=[(DV, 1)])
            dv_absmax[c] = r[DV]
        dv_g = E("vector", lambda e: e.tensor_reduce(
            out=gq[:, 8:9], in_=gq[:, 0:8], axis=AX.X, op=ALU.max),
            incs=[(DV, 1)])[DV]
        pe_gt = E("tensor", lambda e: e.transpose(out=LBp[0:1, 0:128], in_=gq[:, 8:9], identity=identf[:]),
                  waits=[(DV, dv_g), (CONST, ld_consts)], incs=[(PE, 1)])[PE]
        E("vector", lambda e: e.tensor_reduce(out=gq[0:1, 9:10], in_=LBp[0:1, 0:128], axis=AX.X, op=ALU.max),
          waits=[(PE, pe_gt)])
        E("vector", lambda e: e.reciprocal(out=gq[0:1, 10:11], in_=gq[0:1, 9:10]))
        E("vector", lambda e: e.tensor_scalar_mul(out=gq[0:1, 11:12], in0=gq[0:1, 10:11], scalar1=7.0))
        dv_qs = E("vector", lambda e: e.tensor_scalar_mul(out=gq[0:1, 12:13], in0=gq[0:1, 9:10], scalar1=1.0 / 7.0),
                  incs=[(DV, 1)])[DV]
        pe_bc = None
        for (srcc, dstc) in ((11, 13), (12, 14)):
            pe_bc = E("tensor", lambda e, s=srcc, d=dstc: e.matmul(
                out=LBp[:, 200 + d:201 + d], lhsT=onesrowf[0:1, 0:128],
                rhs=gq[0:1, s:s + 1], start=True, stop=True),
                waits=[(DV, dv_qs)], incs=[(PE, 1)])[PE]
        dv_bcast = E("vector", lambda e: e.tensor_copy(out=gq[:, 13:15], in_=LBp[:, 213:215]),
                     waits=[(PE, pe_bc)], incs=[(DV, 1)])[DV]

        dv_cast_slot = {i: 0 for i in range(4)}
        pe_stage_slot = {i: 0 for i in range(4)}
        dv_trcopy = 0
        wt_wo_store = 0

        def prep_wT(wname, region, to_scratch):
            nonlocal dv_trcopy
            dv_last = 0
            for h in range(2):
                xv = {}
                for fl in range(4):
                    xv[fl] = dma("sync", XS[fl], lambda e, w=wname, h=h, fl=fl: e.dma_start(
                        out=xb[:, fl * 1024:(fl + 1) * 1024],
                        in_=w_h[w][h * 512 + fl * 128:h * 512 + (fl + 1) * 128, :]),
                        waits=[(DV, dv_bcast), (DV, dv_cast_slot[fl])])
                for fl in range(4):
                    r = E("vector", lambda e, fl=fl: e.tensor_copy(
                        out=Qs[:, fl * 1024:(fl + 1) * 1024], in_=xb[:, fl * 1024:(fl + 1) * 1024]),
                        waits=[(XS[fl], xv[fl]), (PE, pe_stage_slot[fl])], incs=[(DV, 1)])
                    dv_cast_slot[fl] = r[DV]
                for fl in range(4):
                    for dc in range(8):
                        w = [(DV, dv_cast_slot[fl])]
                        if dc == 0:
                            w += [(DV, dv_trcopy), (CONST, ld_consts)]
                        E("tensor", lambda e, fl=fl, dc=dc: e.transpose(
                            out=TRs[:, dc * 128:(dc + 1) * 128],
                            in_=Qs[:, fl * 1024 + dc * 128:fl * 1024 + (dc + 1) * 128],
                            identity=ident[:]),
                            waits=w, incs=[(PE, 1)] if dc == 7 else ())
                    pe_stage_slot[fl] = PE.n
                    pe_last = PE.n
                    dst = wsl.ap().rearrange("p (r dcc k) -> p r dcc k", r=3, dcc=8)[
                        :, region, :, h * 512 + fl * 128:h * 512 + (fl + 1) * 128]
                    src3 = TRs.ap().rearrange("p (dcc t) -> p dcc t", dcc=8)
                    w2 = [(PE, pe_last)]
                    if region == 0 and wname != "wo":
                        w2.append((WT, wt_wo_store))
                    r = E("vector", lambda e, d=dst, s3=src3: e.tensor_copy(out=d, in_=s3),
                          waits=w2, incs=[(DV, 1)])
                    dv_trcopy = r[DV]
                    dv_last = r[DV]
            return dma("gpsimd", WT, lambda e, s=to_scratch, rg=region: e.dma_start(
                out=s, in_=wsl[:, rg * 8192:(rg + 1) * 8192]), waits=[(DV, dv_last)])

        wt_wo_store = prep_wT("wo", 0, wT_scr["wo"])
        prep_wT("wq", 0, wT_scr["wq"])
        prep_wT("wk", 1, wT_scr["wk"])
        prep_wT("wv", 2, wT_scr["wv"])

        trs_guard.__exit__(None, None, None)

        barrier()

        BW = {"wl": 0}

        def phase_a(seq, reload_weights, TRA, do_quant=False):
            wl_val = 0
            if reload_weights:
                for rg, n in ((0, "wq"), (1, "wk"), (2, "wv")):
                    wl_val = dma("gpsimd", WR, lambda e, rg=rg, n=n: e.dma_start(
                        out=wsl[:, rg * 8192:(rg + 1) * 8192], in_=wT_scr[n]))

            ac_x_free = {}
            pe_lnb = {0: 0, 1: 0}
            dv_sqb = {0: 0, 1: 0}
            st_a = {"ac_lnTc": 0}
            ac_sbbank = {0: 0, 1: 0}
            dv_obank = {0: (AC, 0), 1: (AC, 0)}
            qg_val = 0
            kg_val = 0
            chains = {}

            def emit_chain(g, sub):
                t = 4 * g + sub
                sl4 = t % 4
                sl2 = t % 2
                xs_v = dma("sync", XS[sl4], lambda e, seq=seq, t=t, sl4=sl4: e.dma_start(
                    out=xb[:, sl4 * 1024:(sl4 + 1) * 1024],
                    in_=x_in[seq, t * 128:(t + 1) * 128, :]),
                    waits=[ac_x_free.get(sl4, (AC, 0))])
                aca, dvs, _ = emit_ln(
                    xb[:, sl4 * 1024:(sl4 + 1) * 1024],
                    lnb[:, sl2 * 1024:(sl2 + 1) * 1024],
                    sq_slot=sl2,
                    waits_src=[(XS[sl4], xs_v)],
                    waits_sq=[(DV, dv_sqb[sl2])],
                    waits_out=[(PE, pe_lnb[sl2])])
                ac_x_free[sl4] = aca
                chains.setdefault(g, {})[sub] = aca
                dv_sqb[sl2] = dvs

            def emit_T(g, sub):
                t = 4 * g + sub
                sl2 = t % 2
                aca = chains[g][sub]
                for dc in range(8):
                    w = [aca]
                    if dc == 0:
                        w.append((AC, st_a["ac_lnTc"]))
                    E("tensor", lambda e, dc=dc, sl2=sl2, TRA=TRA: e.transpose(
                        out=TRA[:, dc * 128:(dc + 1) * 128],
                        in_=lnb[:, sl2 * 1024 + dc * 128:sl2 * 1024 + (dc + 1) * 128],
                        identity=ident[:]),
                        waits=w, incs=[(PE, 1)] if dc == 7 else ())
                pe_lnb[sl2] = PE.n
                pe_tr = PE.n
                dst = lnT.ap().rearrange("p (dcc t) -> p dcc t", dcc=8)[:, :, sub * 128:(sub + 1) * 128]
                src3 = TRA.ap().rearrange("p (dcc t) -> p dcc t", dcc=8)
                r = E("scalar", lambda e, d=dst, s3=src3: e.activation(out=d, in_=s3, func=AF.Identity, scale=1.0),
                      waits=[(PE, pe_tr)], incs=[(AC, 1)])
                st_a["ac_lnTc"] = r[AC]

            q_t1_free = {0: 0, 1: 0}
            q_ot_free = {0: 0, 1: 0}
            fkv = {0: 0, 1: 0}

            def quant_chunk(c):
                sl = c % 2
                fkv[sl] = dma("sync", FK[sl], lambda e, c=c, sl=sl: e.dma_start(
                    out=t1[:, sl * 1024:(sl + 1) * 1024],
                    in_=w_h["fk"][c * 128:(c + 1) * 128, :]),
                    waits=[(DV, q_t1_free[sl])])
                xsl = t1[:, sl * 1024:(sl + 1) * 1024]
                ssl = x2[:]
                tsl = t2[:, 0:1024]
                osl = OT_sb[:, sl * 1024:(sl + 1) * 1024]
                r = E("vector", lambda e, x_=xsl, s=ssl: e.tensor_scalar_mul(
                    out=s, in0=x_, scalar1=gq[:, 13:14]),
                    waits=[(FK[sl], fkv[sl])])
                q_t1_free[sl] = r[DV]
                E("vector", lambda e, s=ssl, t=tsl: e.tensor_scalar(
                    out=t, in0=s, scalar1=-6.5, scalar2=-7.0, op0=ALU.is_ge, op1=ALU.add))
                for kk_ in range(-6, 7):
                    E("vector", lambda e, s=ssl, t=tsl, c_=kk_ + 0.5: e.scalar_tensor_tensor(
                        out=t, in0=s, scalar=c_, in1=t, op0=ALU.is_ge, op1=ALU.add))
                dvq = E("vector", lambda e, t=tsl, o=osl: e.tensor_scalar_mul(
                    out=o, in0=t, scalar1=gq[:, 14:15]),
                    waits=[(FS[sl], q_ot_free[sl])], incs=[(DV, 1)])[DV]
                q_ot_free[sl] = dma("gpsimd", FS[sl], lambda e, c=c, o=osl: e.dma_start(
                    out=wfq_h[:, c * 1024:(c + 1) * 1024], in_=o), waits=[(DV, dvq)])

            emit_chain(0, 0)
            emit_chain(0, 1)
            for sub in range(4):
                emit_T(0, sub)
                if sub < 2:
                    emit_chain(0, sub + 2)

            for g in range(NGROUP):
                if g + 1 < NGROUP:
                    emit_chain(g + 1, 0)
                    emit_chain(g + 1, 1)
                if do_quant and g > 0:
                    quant_chunk(g - 1)

                def proj_fm(wreg, biascol, scale, evac_dst_fn, first_waits, src_shape_fn=None):
                    nonlocal ac_sbbank
                    if src_shape_fn is None:
                        src_shape_fn = lambda bk_: SBp[:, bk_ * 512:bk_ * 512 + 512]
                    last_ac = 0
                    for kc in range(8):
                        bk_ = kc % 2
                        w = [(AC, st_a["ac_lnTc"]), (AC, ac_sbbank[bk_])]
                        if kc == 0:
                            w += list(first_waits)
                            if wl_val:
                                w.append((WR, wl_val))
                        for dc in range(8):
                            E("tensor", lambda e, wreg=wreg, kc=kc, dc=dc, bk_=bk_: e.matmul(
                                out=SBp[:, bk_ * 512:bk_ * 512 + 512],
                                lhsT=wsl[:, wreg * 8192 + dc * 1024 + kc * 128: wreg * 8192 + dc * 1024 + (kc + 1) * 128],
                                rhs=lnT[:, dc * 512:(dc + 1) * 512],
                                start=(dc == 0), stop=(dc == 7)),
                                waits=w if dc == 0 else (), incs=[(PE, 1)] if dc == 7 else ())
                        pe_mm = PE.n
                        r = E("scalar", lambda e, kc=kc, bk_=bk_, bc=biascol, sc=scale, df=evac_dst_fn, sf=src_shape_fn: e.activation(
                            out=df(kc), in_=sf(bk_),
                            func=AF.Identity, bias=bc[:, kc:kc + 1], scale=sc),
                            waits=[(PE, pe_mm)], incs=[(AC, 1)])
                        ac_sbbank[bk_] = r[AC]
                        last_ac = r[AC]
                    return last_ac

                ac_q = proj_fm(0, bq32, INVSQ,
                               lambda kc: Qs[:, kc * 512:(kc + 1) * 512],
                               [(QG, qg_val)])
                for ss in range(2):
                    src = Qs.ap().rearrange("p (dcc t) -> p dcc t", dcc=8)[:, :, ss * 256:(ss + 1) * 256]
                    qg_val = dma("gpsimd", QG, lambda e, seq=seq, g=g, ss=ss, s=src: e.dma_start(
                        out=q_scr[seq, 2 * g + ss], in_=s), waits=[(AC, ac_q)])

                ktr_view = Ktr.ap().rearrange("p (tl ck) -> p tl ck", tl=4)
                sb3 = SBp.ap().rearrange("p (bb tl t) -> p bb tl t", bb=2, tl=4)
                ac_k = proj_fm(1, bkc, 1.0,
                               lambda kc: ktr_view[:, :, kc * 128:(kc + 1) * 128],
                               [(KG, kg_val)],
                               src_shape_fn=lambda bk_: sb3[:, bk_, :, :])
                for tl in range(4):
                    kg_val = dma("gpsimd", KG, lambda e, seq=seq, g=g, tl=tl: e.dma_start(
                        out=k_scr[seq, 4 * g + tl], in_=Ktr[:, tl * 1024:(tl + 1) * 1024]),
                        waits=[(AC, ac_k)])

                for sub in range(4):
                    t = 4 * g + sub
                    pidx = sub % 2
                    for h in range(2):
                        ps = OTp[pidx * 2 + h]
                        w = [dv_obank[pidx], chains[g][sub]]
                        for dc in range(8):
                            E("tensor", lambda e, sub=sub, h=h, dc=dc, ps=ps: e.matmul(
                                out=ps[:],
                                lhsT=lnT[:, dc * 512 + sub * 128:dc * 512 + (sub + 1) * 128],
                                rhs=wsl[:, 2 * 8192 + dc * 1024 + h * 512: 2 * 8192 + dc * 1024 + (h + 1) * 512],
                                start=(dc == 0), stop=False),
                                waits=w if dc == 0 else ())
                        E("tensor", lambda e, h=h, ps=ps: e.matmul(
                            out=ps[:],
                            lhsT=onesrow[0:1, 0:128],
                            rhs=bvrow[0:1, h * 512:(h + 1) * 512],
                            start=False, stop=True), incs=[(PE, 1)])
                    pe_v = PE.n
                    r = None
                    for h in range(2):
                        r = E("scalar", lambda e, t=t, h=h, ps=OTp[pidx * 2 + h]: e.activation(
                            out=V_buf[:, t * 1024 + h * 512:t * 1024 + (h + 1) * 512],
                            in_=ps[:], func=AF.Identity, scale=1.0),
                            waits=[(PE, pe_v)], incs=[(AC, 1)] if h == 1 else ())
                    dv_obank[pidx] = (AC, r[AC])
                    if g + 1 < NGROUP:
                        emit_T(g + 1, sub)
                        if sub < 2:
                            emit_chain(g + 1, sub + 2)
                if do_quant and g == NGROUP - 1:
                    quant_chunk(g)
                if g == NGROUP - 1:
                    BW["wl"] = dma("sync", WL, lambda e: e.dma_start(
                        out=wsl[:, 0:8192], in_=wT_scr["wo"]), waits=[(PE, PE.n)])
                    BW["wl"] = dma("sync", WL, lambda e: e.dma_start(
                        out=wsl[:, 8192:16384], in_=wfq_h),
                        waits=[(PE, PE.n), (FS[0], q_ot_free[0]), (FS[1], q_ot_free[1])])

        def phase_b(seq, EB):
            x2s = {0: x2, 1: t2}
            wl_val = BW["wl"]

            st = {
                "ys": {i: 0 for i in range(4)},
                "pe_slot": {i: 0 for i in range(4)},
                "pe_flashend": {},
                "pe_pvlast": {},
                "ac_otevac": 0,
                "ac_otevac_b": {i: 0 for i in range(4)},
                "dv_rl": {},
                "ac_rl_user": 0,
                "ac_eb": 0,
                "pe_eb_reader": 0,
                "dv_sqb": {0: (DV, 0), 1: (DV, 0)},
                "pe_ln2T": {0: 0, 1: 0},
                "pe_ffn": {0: 0, 1: 0},
                "dv_xat": {0: 0, 1: 0},
                "dv_t1": {0: 0, 1: 0},
                "qs": {0: 0, 1: 0},
                "xs": {},
                "ks": {},
                "ot_sb_reader": 0,
                "ac_abs_bank": {0: 0, 1: 0},
                "pe_pv_slot": {0: 0, 1: 0},
            }
            pend = []

            def make_epilogue(s):
                pre = []
                main = []
                spill = []

                def t_rl():
                    r = E("vector", lambda e: e.reciprocal(out=rl[:], in_=LBp[:, 0:2]),
                          waits=[(PE, st["pe_pvlast"][s]), (AC, st["ac_rl_user"])],
                          incs=[(DV, 1)])
                    st["dv_rl"][s] = r[DV]
                pre.append(t_rl)

                def t_otevac(b):
                    def f():
                        r = E("scalar", lambda e, b=b: e.activation(
                            out=OT_sb[:, b * 512:(b + 1) * 512], in_=OTp[b][:],
                            func=AF.Identity, scale=1.0),
                            waits=[(PE, st["pe_pvlast"][s]), (PE, st["ot_sb_reader"])],
                            incs=[(AC, 1)])
                        st["ac_otevac"] = r[AC]
                        st["ac_otevac_b"][b] = r[AC]
                    return f
                for b_ in range(4):
                    pre.append(t_otevac(b_))

                boxes = {0: {}, 1: {}}

                def outproj_h(qc, h):
                    def f():
                        box = boxes[qc]
                        w = [(AC, st["ac_otevac"]), (AC, st["ac_eb"]), (WL, wl_val)]
                        for vc in range(8):
                            E("tensor", lambda e, h=h, vc=vc, qc=qc: e.matmul(
                                out=EB[:],
                                lhsT=OT_sb[:, vc * 256 + qc * 128:vc * 256 + (qc + 1) * 128],
                                rhs=wsl[:, vc * 1024 + h * 512:vc * 1024 + (h + 1) * 512],
                                start=(vc == 0), stop=(vc == 7)),
                                waits=w if vc == 0 else (), incs=[(PE, 1)] if vc == 7 else ())
                        st["ot_sb_reader"] = PE.n
                        pe_mm = PE.n
                        r = E("scalar", lambda e, h=h, qc=qc: e.activation(
                            out=t1[:, qc * 1024 + h * 512:qc * 1024 + (h + 1) * 512], in_=EB[:],
                            func=AF.Identity, scale=rl[:, qc:qc + 1]),
                            waits=[(PE, pe_mm), (DV, st["dv_rl"][s]), (DV, st["dv_t1"][qc])],
                            incs=[(AC, 1)])
                        st["ac_eb"] = r[AC]
                        st["ac_rl_user"] = r[AC]
                        box[f"ev{h}"] = r[AC]
                    return f

                def lnattn_a(qc):
                    def f():
                        box = boxes[qc]
                        t1s = t1[:, qc * 1024:(qc + 1) * 1024]
                        dvb = E("vector", lambda e, t1s=t1s: e.tensor_add(out=t1s, in0=t1s, in1=bo_tile[:]),
                                waits=[(AC, box["ev1"])], incs=[(DV, 1)])[DV]
                        aca, dvs, dv1 = emit_ln(t1s, sqb[:, qc * 1024:(qc + 1) * 1024], sq_slot=qc,
                                                waits_src=[(DV, dvb)],
                                                waits_sq=[st["dv_sqb"][qc],
                                                          (PE, st["pe_ln2T"][qc])],
                                                apply_eng="vector")
                        box["dv_ln1"] = dv1
                        box["ac_ln1"] = aca
                    return f

                def lnattn_b(qc):
                    def f():
                        box = boxes[qc]
                        t = 2 * s + qc
                        sl4 = t % 4
                        dv_x2 = E("vector", lambda e, sl4=sl4, qc=qc: e.tensor_add(
                            out=x2s[qc][:], in0=xb[:, sl4 * 1024:(sl4 + 1) * 1024],
                            in1=sqb[:, qc * 1024:(qc + 1) * 1024]),
                            waits=[box["ac_ln1"], (XS[sl4], st["xs"][t])], incs=[(DV, 1)])[DV]
                        ac2, dvs2, _ = emit_ln(x2s[qc][:], sqb[:, qc * 1024:(qc + 1) * 1024], sq_slot=qc,
                                               waits_src=[(DV, dv_x2)],
                                               waits_sq=[(DV, dv_x2)],
                                               waits_out=[(PE, st["pe_ln2T"][qc])],
                                               apply_eng="vector")
                        box["ac_ln2"] = ac2
                        st["dv_sqb"][qc] = (DV, dvs2)
                    return f

                def ln2t_pe(qc):
                    def f():
                        box = boxes[qc]
                        for rnd in range(2):
                            wfirst = [box["ac_ln2"], (AC, st["ac_eb"])]
                            for c in range(4):
                                dc = 4 * rnd + c
                                E("tensor", lambda e, qc=qc, dc=dc, c=c: e.transpose(
                                    out=EB[:, c * 128:(c + 1) * 128],
                                    in_=sqb[:, qc * 1024 + dc * 128:qc * 1024 + (dc + 1) * 128],
                                    identity=identf[:]),
                                    waits=wfirst if c == 0 else (), incs=[(PE, 1)] if c == 3 else ())
                            pe_t = PE.n
                            r2 = E("scalar", lambda e, qc=qc, rnd=rnd: e.activation(
                                out=lnT[:, qc * 1024 + rnd * 512:qc * 1024 + (rnd + 1) * 512],
                                in_=EB[:, 0:512], func=AF.Identity, scale=1.0),
                                waits=[(PE, pe_t)], incs=[(AC, 1)])
                            st["ac_eb"] = r2[AC]
                        st["pe_ln2T"][qc] = PE.n
                        box["ac_lnT"] = st["ac_eb"]
                    return f

                def xa_stage(qc):
                    def f():
                        box = boxes[qc]
                        w = [(AC, box["ac_lnT"]), (AC, st["ac_eb"]),
                             (DV, st["dv_rl"][s])]
                        for dc in range(8):
                            E("tensor", lambda e, dc=dc, qc=qc: e.matmul(
                                out=EB[0:16, 0:128],
                                lhsT=a_bf[:, dc * 16:(dc + 1) * 16],
                                rhs=lnT[:, qc * 1024 + dc * 128:qc * 1024 + (dc + 1) * 128],
                                start=(dc == 0), stop=(dc == 7)),
                                waits=w if dc == 0 else (), incs=[(PE, 1)] if dc == 7 else ())
                        pe_xa = PE.n
                        r = E("vector", lambda e, qc=qc: e.tensor_copy(
                            out=xaT[:, qc * 128:(qc + 1) * 128], in_=EB[0:16, 0:128]),
                            waits=[(PE, pe_xa)], incs=[(DV, 1)])
                        st["dv_xat"][qc] = r[DV]
                    return f

                def ffn_h(qc, h):
                    def f():
                        box = boxes[qc]
                        w = [(DV, st["dv_xat"][qc]), (AC, st["ac_eb"])]
                        for dc in range(8):
                            E("tensor", lambda e, h=h, dc=dc, qc=qc: e.matmul(
                                out=EB[:],
                                lhsT=lnT[:, qc * 1024 + dc * 128:qc * 1024 + (dc + 1) * 128],
                                rhs=wsl[:, 8192 + dc * 1024 + h * 512:8192 + dc * 1024 + (h + 1) * 512],
                                start=(dc == 0), stop=False),
                                waits=w if dc == 0 else ())
                        E("tensor", lambda e, h=h: e.matmul(
                            out=EB[:], lhsT=onesrow[0:1, 0:128], rhs=fbrow[0:1, h * 512:(h + 1) * 512],
                            start=False, stop=False))
                        E("tensor", lambda e, h=h, qc=qc: e.matmul(
                            out=EB[:], lhsT=xaT[0:16, qc * 128:(qc + 1) * 128],
                            rhs=b_bf[0:16, h * 512:(h + 1) * 512],
                            start=False, stop=True), incs=[(PE, 1)])
                        st["pe_ffn"][qc] = PE.n
                        pe_mm = PE.n
                        r = E("scalar", lambda e, h=h, qc=qc: e.activation(
                            out=t1[:, qc * 1024 + h * 512:qc * 1024 + (h + 1) * 512], in_=EB[:],
                            func=AF.Identity, scale=1.0),
                            waits=[(PE, pe_mm), (DV, box["dv_ln1"])], incs=[(AC, 1)])
                        st["ac_eb"] = r[AC]
                        box[f"fev{h}"] = r[AC]
                    return f

                def lnffn_chain(qc):
                    def f():
                        box = boxes[qc]
                        t = 2 * s + qc
                        sl4 = t % 4
                        acf, dvs3, dv1f = emit_ln(t1[:, qc * 1024:(qc + 1) * 1024],
                                                  sqb[:, qc * 1024:(qc + 1) * 1024], sq_slot=qc,
                                                  waits_src=[(AC, box["fev1"])],
                                                  waits_sq=[st["dv_sqb"][qc],
                                                            (PE, st["pe_ln2T"][qc])],
                                                  apply_eng="vector")
                        st["dv_t1"][qc] = dv1f
                        dv_fin = E("vector", lambda e, sl4=sl4, qc=qc: e.tensor_add(
                            out=xb[:, sl4 * 1024:(sl4 + 1) * 1024],
                            in0=x2s[qc][:], in1=sqb[:, qc * 1024:(qc + 1) * 1024]),
                            waits=[acf], incs=[(DV, 1)])[DV]
                        st["dv_sqb"][qc] = (DV, dv_fin)
                        box["dv_fin"] = dv_fin
                    return f

                def store_stage(qc):
                    def f():
                        box = boxes[qc]
                        t = 2 * s + qc
                        sl4 = t % 4
                        st["ys"][sl4] = dma("gpsimd", YS[sl4], lambda e, seq=seq, t=t, sl4=sl4: e.dma_start(
                            out=y_out[seq, t * 128:(t + 1) * 128, :],
                            in_=xb[:, sl4 * 1024:(sl4 + 1) * 1024]),
                            waits=[(DV, box["dv_fin"])])
                        if t + 4 < 2 * NSTRIP:
                            st["xs"][t + 4] = dma("sync", XS[sl4], lambda e, seq=seq, t=t + 4, sl4=sl4: e.dma_start(
                                out=xb[:, sl4 * 1024:(sl4 + 1) * 1024],
                                in_=x_in[seq, t * 128:(t + 1) * 128, :]),
                                waits=[(YS[sl4], st["ys"][sl4])])
                    return f

                main.append((0.40, outproj_h(0, 0)))
                main.append((0.44, outproj_h(0, 1)))
                main.append((0.48, outproj_h(1, 0)))
                main.append((0.52, outproj_h(1, 1)))
                main.append((0.54, lnattn_a(0)))
                main.append((0.58, lnattn_b(0)))
                main.append((0.78, ln2t_pe(0)))
                main.append((0.84, xa_stage(0)))
                main.append((0.87, ffn_h(0, 0)))
                main.append((0.90, ffn_h(0, 1)))
                main.append((0.93, lnffn_chain(0)))
                spill.append((0.02, lnattn_a(1)))
                spill.append((0.06, lnattn_b(1)))
                if st.get("store1_thunk") is not None:
                    spill.append((0.10, st["store1_thunk"]))
                spill.append((0.30, ln2t_pe(1)))
                spill.append((0.34, xa_stage(1)))
                spill.append((0.34, store_stage(0)))
                spill.append((0.38, ffn_h(1, 0)))
                spill.append((0.42, ffn_h(1, 1)))
                spill.append((0.46, lnffn_chain(1)))
                st["store1_thunk"] = store_stage(1)
                return pre, main, spill

            epi_pre = []
            pend = []
            spill_next = []

            for s in range(NSTRIP):
                st["qs"][s % 2] = dma("sync", QS[s % 2], lambda e, seq=seq, s=s: e.dma_start(
                    out=Qs[:, (s % 2) * 2048:(s % 2 + 1) * 2048], in_=q_scr[seq, s]),
                    waits=[(PE, st["pe_flashend"].get(s - 2, 0))])
                if s < 2:
                    for qc in range(2):
                        t = 2 * s + qc
                        st["xs"][t] = dma("sync", XS[t % 4], lambda e, seq=seq, t=t: e.dma_start(
                            out=xb[:, (t % 4) * 1024:(t % 4 + 1) * 1024],
                            in_=x_in[seq, t * 128:(t + 1) * 128, :]))
                nt = 2 * s + 2

                for f in epi_pre:
                    f()
                epi_pre = []

                ac_abs = {}
                ac_exp = {}
                pe_pv = {}
                qof = (s % 2) * 2048
                n_pend = len(pend)
                popped = 0

                def emit_S(j):
                    sp = j % 2
                    w = [(KS[j % 4], st["ks"][j]),
                         (AC, ac_abs.get(j - 2, st["ac_abs_bank"][sp]))]
                    if j == 0:
                        w += [(QS[s % 2], st["qs"][s % 2])]
                    for dc in range(8):
                        E("tensor", lambda e, j=j, dc=dc, sp=sp, qof=qof: e.matmul(
                            out=SBp[:, sp * 512:sp * 512 + 256],
                            lhsT=Ktr[:, (j % 4) * 1024 + dc * 128:(j % 4) * 1024 + (dc + 1) * 128],
                            rhs=Qs[:, qof + dc * 256:qof + (dc + 1) * 256],
                            start=(dc == 0), stop=(dc == 7)),
                            waits=w if dc == 0 else (), incs=[(PE, 1)] if dc == 7 else ())
                    st["pe_slot"][j % 4] = PE.n
                    pe_sT = PE.n
                    r = E("scalar", lambda e, sp=sp: e.activation(
                        out=Sa[:, sp * 256:(sp + 1) * 256], in_=SBp[:, sp * 512:sp * 512 + 256],
                        func=AF.Abs, scale=1.0), waits=[(PE, pe_sT)], incs=[(AC, 1)])
                    ac_abs[j] = r[AC]
                    st["ac_abs_bank"][sp] = r[AC]
                    wexp = [(AC, ac_abs[j])]
                    if j >= 2 * s:
                        r = E("vector", lambda e, sp=sp, ms=(j - 2 * s): e.tensor_tensor(
                            out=Sa[:, sp * 256:(sp + 1) * 256], in0=Sa[:, sp * 256:(sp + 1) * 256],
                            in1=masks[:, ms * 256:(ms + 1) * 256], op=ALU.add),
                            waits=[(AC, ac_abs[j]), (CONST, ld_consts)], incs=[(DV, 1)])
                        wexp.append((DV, r[DV]))
                    wexp.append((PE, pe_pv.get(j - 2, st["pe_pv_slot"][sp])))
                    r = E("scalar", lambda e, sp=sp: e.activation(
                        out=PT[:, sp * 256:(sp + 1) * 256], in_=Sa[:, sp * 256:(sp + 1) * 256],
                        func=AF.Exp, scale=1.0),
                        waits=wexp, incs=[(AC, 1)])
                    ac_exp[j] = r[AC]

                def emit_PV(j):
                    sp = j % 2
                    last = (j == nt - 1)
                    frst = (j == 0)
                    w = [(AC, ac_exp[j])]
                    first = True
                    for vc in range(8):
                        b_ = vc // 2
                        col = (vc % 2) * 256
                        wv = list(w) if first else []
                        if frst and vc % 2 == 0:
                            wv.append((AC, st["ac_otevac_b"][b_]))
                        E("tensor", lambda e, j=j, sp=sp, vc=vc, b_=b_, col=col,
                          stt=(frst and vc % 2 == 0),
                          lst=(last and vc % 2 == 1): e.matmul(
                            out=OTp[b_][:, col:col + 256],
                            lhsT=V_buf[:, j * 1024 + vc * 128:j * 1024 + (vc + 1) * 128],
                            rhs=PT[:, sp * 256:(sp + 1) * 256],
                            start=stt, stop=lst),
                            waits=wv)
                        first = False
                    for qc in range(2):
                        wl_ = []
                        if frst and qc == 0:
                            wl_ = [(DV, st["dv_rl"].get(s - 1, 0))]
                        E("tensor", lambda e, j=j, sp=sp, qc=qc, stt=(frst and qc == 0),
                          lst=(last and qc == 1): e.matmul(
                            out=LBp[:, qc:qc + 1],
                            lhsT=PT[:, sp * 256 + qc * 128:sp * 256 + (qc + 1) * 128],
                            rhs=onescol[:, 0:1], start=stt, stop=lst),
                            waits=wl_, incs=[(PE, 1)] if qc == 1 else ())
                    pe_pv[j] = PE.n
                    st["pe_pv_slot"][sp] = PE.n

                for j in range(nt):
                    st["ks"][j] = dma("gpsimd", KS[j % 4], lambda e, seq=seq, j=j: e.dma_start(
                        out=Ktr[:, (j % 4) * 1024:(j % 4 + 1) * 1024], in_=k_scr[seq, j]),
                        waits=[(PE, st["pe_slot"][j % 4])])
                    emit_S(j)
                    if j > 0:
                        emit_PV(j - 1)
                    while popped < n_pend and int(pend[popped][0] * nt) <= j:
                        pend[popped][1]()
                        popped += 1
                emit_PV(nt - 1)
                while popped < n_pend:
                    pend[popped][1]()
                    popped += 1
                st["pe_flashend"][s] = PE.n
                st["pe_pvlast"][s] = PE.n
                epi_pre, e_main, e_spill = make_epilogue(s)
                pend = list(spill_next) + list(e_main)
                spill_next = e_spill

            for f in epi_pre:
                f()
            for _, f in pend:
                f()
            for _, f in spill_next:
                f()
            if st.get("store1_thunk") is not None:
                st["store1_thunk"]()
                st["store1_thunk"] = None

        def with_bank7(name, shape, dtype, fn):
            g = nc.psum_tensor(name, shape, dtype)
            t = g.__enter__()
            fn(t)
            g.__exit__(None, None, None)

        with_bank7("TRA0", [128, 1024], BF16, lambda t: phase_a(0, False, t, do_quant=True))
        barrier()
        if debug:
            dma("gpsimd", WT, lambda e: e.dma_start(out=v_dbg, in_=V_buf[:]))
        with_bank7("EB0", [128, 512], F32, lambda t: phase_b(0, t))
        barrier()
        with_bank7("TRA1", [128, 1024], BF16, lambda t: phase_a(1, True, t))
        barrier()
        with_bank7("EB1", [128, 512], F32, lambda t: phase_b(1, t))
        barrier()
        fin = [(s, s.n) for s in B.queue_sems["gpsimd"]]
        E("gpsimd", lambda e: e.nop(), waits=fin, chain=False)

        def replay(eng_name):
            def go(eng):
                for fn, waits, incs in B.ops[eng_name]:
                    for sh, v in waits:
                        eng.wait_ge(sh, v)
                    inst = fn(eng)
                    if incs:
                        for sh, amt in incs:
                            inst.then_inc(sh, amt)
            return go

        block.tensor(replay("tensor"))
        block.vector(replay("vector"))
        block.scalar(replay("scalar"))
        block.sync(replay("sync"))
        block.gpsimd(replay("gpsimd"))

    return nc


_CACHE = {}


def kernel(**inputs):
    x = np.ascontiguousarray(np.asarray(inputs["L0_attn_x"], dtype=np.float32))
    wmap = {
        "wq": inputs["L0_attn_Wq"], "wk": inputs["L0_attn_Wk"],
        "wv": inputs["L0_attn_Wv"], "wo": inputs["L0_attn_Wo"],
        "fk": inputs["L0_ffn_kernel"],
    }
    bmap = {
        "bq": inputs["L0_attn_bq"], "bk": inputs["L0_attn_bk"],
        "bv": inputs["L0_attn_bv"], "bo": inputs["L0_attn_bo"],
        "fbias": inputs["L0_ffn_bias"],
    }
    base = {}
    for k, v in wmap.items():
        base[k] = np.ascontiguousarray(np.asarray(v, dtype=np.float32))
    for k, v in bmap.items():
        base[k] = np.ascontiguousarray(np.asarray(v, dtype=np.float32)).reshape(1, D)
    base["fa"] = np.ascontiguousarray(np.asarray(inputs["L0_ffn_a"], dtype=np.float32))
    base["fb"] = np.ascontiguousarray(np.asarray(inputs["L0_ffn_b"], dtype=np.float32))

    if "nc" not in _CACHE:
        _CACHE["nc"] = build_nc()
    nc = _CACHE["nc"]

    in_maps = []
    for c in range(8):
        m = dict(base)
        m["x"] = np.ascontiguousarray(x[2 * c:2 * c + 2])
        in_maps.append(m)

    res = run_bass_kernel_spmd(nc, in_maps, core_ids=list(range(8)))
    y = np.concatenate([r["y"] for r in res.results], axis=0)
    return y.reshape(16, SEQ, D)


if __name__ == "__main__":
    build_nc()
    print("build ok")


# version 26
# speedup vs baseline: 1.0815x; 1.0624x over previous
import numpy as np
import ml_dtypes

import concourse.bass as bass
import concourse.mybir as mybir
from concourse.bass_utils import run_bass_kernel_spmd

F32 = mybir.dt.float32
BF16 = mybir.dt.bfloat16
AF = mybir.ActivationFunctionType
ALU = mybir.AluOpType
AX = mybir.AxisListType

D = 1024
SEQ = 4096
NSEQ = 2
NTILE = SEQ // 128
NGROUP = SEQ // 512
NSTRIP = SEQ // 256
NEGBIG = -1.0e30
INVSQ = 1.0 / 32.0

ENGS = ("tensor", "vector", "scalar", "sync", "gpsimd")


class Sem:
    def __init__(self, handle):
        self.h = handle
        self.n = 0


class Builder:
    def __init__(self):
        self.ops = {e: [] for e in ENGS}
        self.queue_sems = {"sync": {}, "gpsimd": {}, "scalar": {}}
        self.eng_sem = {}

    def E(self, eng, fn, waits=(), incs=(), chain=True):
        waits = list(waits)
        incs = list(incs)
        es = self.eng_sem.get(eng) if chain else None
        if es is not None:
            waits.append((es, es.n))
            if not any(s is es for s, _ in incs):
                incs.append((es, 1))
        rincs = []
        out = {}
        for s, amt in incs:
            s.n += amt
            rincs.append((s.h, amt))
            out[s] = s.n
        self.ops[eng].append(
            (fn, tuple((s.h, int(v)) for s, v in waits if v > 0), tuple(rincs)))
        return out

    def dma(self, eng, sem, fn, waits=()):
        sem.n += 16
        self.queue_sems[eng][sem] = True
        self.ops[eng].append(
            (fn, tuple((s.h, int(v)) for s, v in waits if v > 0), ((sem.h, 16),)))
        return sem.n


def build_nc(debug=False):
    nc = bass.Bass("TRN2", target_bir_lowering=False)
    scr_kind = "ExternalOutput" if debug else "Internal"

    x_in = nc.dram_tensor("x", [NSEQ, SEQ, D], F32, kind="ExternalInput").ap()
    w_h = {n: nc.dram_tensor(n, [D, D], F32, kind="ExternalInput").ap()
           for n in ("wq", "wk", "wv", "wo", "fk")}
    b_h = {n: nc.dram_tensor(n, [1, D], F32, kind="ExternalInput").ap()
           for n in ("bq", "bk", "bv", "bo", "fbias")}
    fa_h = nc.dram_tensor("fa", [D, 16], F32, kind="ExternalInput").ap()
    fb_h = nc.dram_tensor("fb", [16, D], F32, kind="ExternalInput").ap()
    y_out = nc.dram_tensor("y", [NSEQ, SEQ, D], F32, kind="ExternalOutput").ap()

    q_scr = nc.dram_tensor("q_scr", [NSEQ, NSTRIP, 128, 2048], BF16, kind=scr_kind).ap()
    k_scr = nc.dram_tensor("k_scr", [NSEQ, NTILE, 128, 1024], BF16, kind=scr_kind).ap()
    v_dbg = nc.dram_tensor("v_dbg", [128, NTILE * 1024], BF16, kind="ExternalOutput").ap() if debug else None
    wT_scr = {n: nc.dram_tensor(n + "T_h", [128, 8192], BF16, kind="Internal").ap()
              for n in ("wq", "wk", "wv", "wo")}
    wfq_h = nc.dram_tensor("wfq_h", [128, 8192], BF16, kind=scr_kind).ap()

    ident_b_h = nc.inline_tensor(np.eye(128, dtype=ml_dtypes.bfloat16), "ident_b").ap()
    ident_f_h = nc.inline_tensor(np.eye(128, dtype=np.float32), "ident_f").ap()
    mknp = np.empty((128, 512), dtype=np.float32)
    kk = np.arange(128)[:, None]
    qq = np.arange(256)[None, :]
    mknp[:, 0:256] = np.where(kk <= qq, 0.0, NEGBIG)
    mknp[:, 256:512] = np.where(128 + kk <= qq, 0.0, NEGBIG)
    masks_h = nc.inline_tensor(mknp, "masks_h").ap()

    from contextlib import ExitStack
    with ExitStack() as ctx:
        ec = ctx.enter_context

        V_buf = ec(nc.sbuf_tensor("V_buf", [128, NTILE * 1024], BF16))
        wsl = ec(nc.sbuf_tensor("wsl", [128, 3 * 8192], BF16))
        a_bf = ec(nc.sbuf_tensor("a_bf", [128, 128], BF16))
        b_bf = ec(nc.sbuf_tensor("b_bf", [16, 1024], BF16))
        ident = ec(nc.sbuf_tensor("ident", [128, 128], BF16))
        identf = ec(nc.sbuf_tensor("identf", [128, 128], F32))
        masks = ec(nc.sbuf_tensor("masks", [128, 512], F32))
        bq32 = ec(nc.sbuf_tensor("bq32", [128, 8], F32))
        bkc = ec(nc.sbuf_tensor("bkc", [128, 8], F32))
        bvrow = ec(nc.sbuf_tensor("bvrow", [1, 1024], BF16))
        fbrow = ec(nc.sbuf_tensor("fbrow", [1, 1024], BF16))
        bo_tile = ec(nc.sbuf_tensor("bo_tile", [128, 1024], F32))
        onesrow = ec(nc.sbuf_tensor("onesrow", [1, 256], BF16))
        onesrowf = ec(nc.sbuf_tensor("onesrowf", [1, 128], F32))
        onescol = ec(nc.sbuf_tensor("onescol", [128, 1], BF16))
        zrow = ec(nc.sbuf_tensor("zrow", [1, 512], BF16))
        epscol = ec(nc.sbuf_tensor("epscol", [128, 1], F32))
        rowstage = ec(nc.sbuf_tensor("rowstage", [1, 1024], F32))
        stats = ec(nc.sbuf_tensor("stats", [128, 32], F32))
        gq = ec(nc.sbuf_tensor("gq", [128, 16], F32))
        rl = ec(nc.sbuf_tensor("rl", [128, 2], F32))

        xb = ec(nc.sbuf_tensor("xb", [128, 4096], F32))
        sqb = ec(nc.sbuf_tensor("sqb", [128, 2048], F32))
        lnb = ec(nc.sbuf_tensor("lnb", [128, 2048], BF16))
        lnT = ec(nc.sbuf_tensor("lnT", [128, 4096], BF16))
        Qs = ec(nc.sbuf_tensor("Qs", [128, 4096], BF16))
        Ktr = ec(nc.sbuf_tensor("Ktr", [128, 4096], BF16))
        Sa = ec(nc.sbuf_tensor("Sa", [128, 512], F32))
        PT = ec(nc.sbuf_tensor("PT", [128, 512], BF16))
        OT_sb = ec(nc.sbuf_tensor("OT_sb", [128, 2048], BF16))
        t1 = ec(nc.sbuf_tensor("t1", [128, 2048], F32))
        t2 = ec(nc.sbuf_tensor("t2", [128, 1024], F32))
        x2 = ec(nc.sbuf_tensor("x2", [128, 1024], F32))
        xaT = ec(nc.sbuf_tensor("xaT", [16, 256], BF16))

        OTp = [ec(nc.psum_tensor(f"OTp{i}", [128, 512], F32)) for i in range(4)]
        SBp = ec(nc.psum_tensor("SBp", [128, 1024], F32))
        LBp = ec(nc.psum_tensor("LBp", [128, 512], F32))
        trs_guard = nc.psum_tensor("TRs", [128, 1024], BF16)
        TRs = trs_guard.__enter__()

        def mksem(name):
            return Sem(ec(nc.semaphore(name)))

        PE, DV, AC, GP, BAR = (mksem(n) for n in ("PE", "DV", "AC", "GP", "BAR"))
        XS = [mksem(f"XS{i}") for i in range(4)]
        YS = [mksem(f"YS{i}") for i in range(4)]
        QS = [mksem(f"QS{i}") for i in range(2)]
        KS = [mksem(f"KS{i}") for i in range(4)]
        WS = [mksem(f"WS{i}") for i in range(2)]
        FK = [mksem(f"FK{i}") for i in range(2)]
        FS = [mksem(f"FS{i}") for i in range(2)]
        WR = mksem("WR")
        TS = [mksem(f"TS{i}") for i in range(2)]
        TA, TB = mksem("TA"), mksem("TB")
        QG, KG, WL, RS, CONST, WT = (mksem(n) for n in ("QG", "KG", "WL", "RS", "CONST", "WT"))

        B = Builder()
        B.eng_sem = {"vector": DV, "scalar": AC, "gpsimd": GP}
        E, dma = B.E, B.dma

        block = ec(nc.Block())

        def barrier():
            k = BAR.n // 5 + 1
            for eng in ENGS:
                w = []
                if eng in ("sync", "gpsimd", "scalar"):
                    w = [(s, s.n) for s in B.queue_sems[eng]]

                def f(e, k=k, bh=BAR.h):
                    e.drain()
                    e.sem_inc(bh, 1)
                    e.wait_ge(bh, 5 * k)
                E(eng, f, waits=w, chain=False)
            BAR.n = 5 * k

        ln_slot = [0]

        def emit_ln(src_ap, out_ap, sq_slot, waits_src=(), waits_sq=(), waits_out=(),
                    veng="vector", vsem=None, apply_eng="scalar"):
            sl = (ln_slot[0] % 4) * 8
            ln_slot[0] += 1
            cS1 = stats[:, sl + 0:sl + 1]
            cMn = stats[:, sl + 1:sl + 2]
            cS2 = stats[:, sl + 2:sl + 3]
            cM2 = stats[:, sl + 3:sl + 4]
            cB2 = stats[:, sl + 4:sl + 5]
            cLn = stats[:, sl + 5:sl + 6]
            cR = stats[:, sl + 6:sl + 7]
            cBn = stats[:, sl + 7:sl + 8]
            sq = sqb[:, sq_slot * 1024:(sq_slot + 1) * 1024]
            VS = vsem if vsem is not None else DV

            E(veng, lambda e, s=src_ap, o=cS1: e.reduce_sum(out=o, in_=s, axis=AX.X),
              waits=waits_src)
            rsq = E(veng,
                    lambda e, s=src_ap, o=sq, a=cS2: e.scalar_tensor_tensor(
                        out=o, in0=s, scalar=1.0, in1=s, op0=ALU.mult, op1=ALU.mult, accum_out=a),
                    waits=tuple(waits_src) + tuple(waits_sq))
            r2 = E(veng, lambda e, i=cS1, o=cMn: e.tensor_scalar_mul(out=o, in0=i, scalar1=-1.0 / D))
            E("scalar", lambda e, i=cMn, o=cM2: e.activation(out=o, in_=i, func=AF.Square, scale=1.0),
              waits=[(VS, r2[VS])])
            E("scalar", lambda e, i=cM2, o=cB2, b=epscol[:, 0:1]: e.activation(
                out=o, in_=i, func=AF.Identity, bias=b, scale=-1.0))
            E("scalar", lambda e, i=cS2, o=cLn, b=cB2: e.activation(
                out=o, in_=i, func=AF.Ln, bias=b, scale=1.0 / D),
                waits=[(VS, rsq[VS])])
            ax = E("scalar", lambda e, i=cLn, o=cR: e.activation(out=o, in_=i, func=AF.Exp, scale=-0.5),
                   incs=[(AC, 1)])
            r4 = E(veng, lambda e, a=cMn, bb=cR, o=cBn: e.tensor_mul(out=o, in0=a, in1=bb),
                   waits=[(AC, ax[AC])])
            if apply_eng == "scalar":
                a3 = E("scalar",
                       lambda e, s=src_ap, o=out_ap, sc=cR, b=cBn: e.activation(out=o, in_=s, func=AF.Identity, bias=b, scale=sc),
                       waits=tuple(waits_out) + ((VS, r4[VS]),), incs=[(AC, 1)])
                return (AC, a3[AC]), rsq[VS], rsq[VS]
            a3 = E("vector",
                   lambda e, s=src_ap, o=out_ap, sc=cR, b=cBn: e.tensor_scalar(
                       out=o, in0=s, scalar1=sc, scalar2=b, op0=ALU.mult, op1=ALU.add),
                   waits=waits_out, incs=[(DV, 1)])
            return (DV, a3[DV]), rsq[VS], a3[DV]

        for dst, src in ((ident[:], ident_b_h), (identf[:], ident_f_h), (masks[:], masks_h)):
            dma("sync", CONST, lambda e, d=dst, s=src: e.dma_start(out=d, in_=s))
        ld_consts = CONST.n

        for fn in (lambda e: e.memset(onesrow[:], 1.0),
                   lambda e: e.memset(onesrowf[:], 1.0),
                   lambda e: e.memset(onescol[:], 1.0),
                   lambda e: e.memset(zrow[:], 0.0)):
            E("gpsimd", fn)
        gp_memset = E("gpsimd", lambda e: e.memset(epscol[:], 1e-5), incs=[(GP, 1)])[GP]

        pe_rowstage = 0
        dv_rowstage = 0

        def prep_colbias(src_h, dest, scale):
            nonlocal pe_rowstage, dv_rowstage
            dv_lbp_prev = DV.n
            rsv = dma("sync", RS, lambda e, s=src_h: e.dma_start(out=rowstage[:], in_=s),
                      waits=[(PE, pe_rowstage), (DV, dv_rowstage)])
            for c in range(8):
                w = [(RS, rsv), (GP, gp_memset), (DV, dv_lbp_prev)] if c == 0 else []
                E("tensor",
                  lambda e, c=c: e.matmul(out=LBp[:, c:c + 1],
                                          lhsT=rowstage[0:1, c * 128:(c + 1) * 128],
                                          rhs=onesrowf[0:1, 0:1], start=True, stop=True),
                  waits=w, incs=[(PE, 1)] if c == 7 else ())
            pe_rowstage = PE.n
            r = E("vector", lambda e, d=dest, sc=scale: e.tensor_scalar_mul(out=d, in0=LBp[:, 0:8], scalar1=sc),
                  waits=[(PE, pe_rowstage)], incs=[(DV, 1)])
            return r[DV]

        prep_colbias(b_h["bq"], bq32[:], INVSQ)
        prep_colbias(b_h["bk"], bkc[:], 1.0)

        def prep_rowbias(src_h, dest):
            nonlocal pe_rowstage, dv_rowstage
            rsv = dma("sync", RS, lambda e, s=src_h: e.dma_start(out=rowstage[:], in_=s),
                      waits=[(PE, pe_rowstage), (DV, dv_rowstage)])
            r = E("vector", lambda e, d=dest: e.tensor_copy(out=d, in_=rowstage[:]),
                  waits=[(RS, rsv)], incs=[(DV, 1)])
            dv_rowstage = r[DV]

        prep_rowbias(b_h["bv"], bvrow[:])
        prep_rowbias(b_h["fbias"], fbrow[:])

        dv_lbp_prev = DV.n
        rsv = dma("sync", RS, lambda e: e.dma_start(out=rowstage[:], in_=b_h["bo"]),
                  waits=[(PE, pe_rowstage), (DV, dv_rowstage)])
        for hh in range(2):
            E("tensor", lambda e, hh=hh: e.matmul(
                out=LBp[:, 0:512], lhsT=onesrowf[0:1, 0:128],
                rhs=rowstage[0:1, hh * 512:(hh + 1) * 512], start=True, stop=True),
                waits=[(RS, rsv), (DV, dv_lbp_prev)] if hh == 0 else [(DV, dv_lbp_prev)],
                incs=[(PE, 1)])
            dv_lbp_prev = E("vector", lambda e, hh=hh: e.tensor_copy(
                out=bo_tile[:, hh * 512:(hh + 1) * 512], in_=LBp[:, 0:512]),
                waits=[(PE, PE.n)], incs=[(DV, 1)])[DV]
        pe_rowstage = PE.n
        dv_rowstage = dv_lbp_prev

        for dc in range(8):
            dma("sync", XS[0], lambda e, dc=dc: e.dma_start(
                out=xb[:, dc * 16:(dc + 1) * 16], in_=fa_h[dc * 128:(dc + 1) * 128, :]))
        dv_a = E("vector", lambda e: e.tensor_copy(out=a_bf[:], in_=xb[:, 0:128]),
                 waits=[(XS[0], XS[0].n)], incs=[(DV, 1)])[DV]
        dma("sync", XS[1], lambda e: e.dma_start(out=xb[0:16, 1024:2048], in_=fb_h))
        dv_b = E("vector", lambda e: e.tensor_copy(out=b_bf[:], in_=xb[0:16, 1024:2048]),
                 waits=[(XS[1], XS[1].n)], incs=[(DV, 1)])[DV]

        dv_absmax = {}
        for c in range(8):
            sl = c % 4
            w = [(DV, max(dv_a, dv_b))] if c < 4 else [(DV, dv_absmax[c - 4])]
            xv = dma("sync", XS[sl], lambda e, c=c, sl=sl: e.dma_start(
                out=xb[:, sl * 1024:(sl + 1) * 1024],
                in_=w_h["fk"][c * 128:(c + 1) * 128, :]), waits=w)
            r = E("vector", lambda e, c=c, sl=sl: e.tensor_reduce(
                out=gq[:, c:c + 1], in_=xb[:, sl * 1024:(sl + 1) * 1024],
                axis=AX.X, op=ALU.max, apply_absolute_value=True),
                waits=[(XS[sl], xv)], incs=[(DV, 1)])
            dv_absmax[c] = r[DV]
        dv_g = E("vector", lambda e: e.tensor_reduce(
            out=gq[:, 8:9], in_=gq[:, 0:8], axis=AX.X, op=ALU.max),
            incs=[(DV, 1)])[DV]
        pe_gt = E("tensor", lambda e: e.transpose(out=LBp[0:1, 0:128], in_=gq[:, 8:9], identity=identf[:]),
                  waits=[(DV, dv_g), (CONST, ld_consts)], incs=[(PE, 1)])[PE]
        E("vector", lambda e: e.tensor_reduce(out=gq[0:1, 9:10], in_=LBp[0:1, 0:128], axis=AX.X, op=ALU.max),
          waits=[(PE, pe_gt)])
        E("vector", lambda e: e.reciprocal(out=gq[0:1, 10:11], in_=gq[0:1, 9:10]))
        E("vector", lambda e: e.tensor_scalar_mul(out=gq[0:1, 11:12], in0=gq[0:1, 10:11], scalar1=7.0))
        dv_qs = E("vector", lambda e: e.tensor_scalar_mul(out=gq[0:1, 12:13], in0=gq[0:1, 9:10], scalar1=1.0 / 7.0),
                  incs=[(DV, 1)])[DV]
        pe_bc = None
        for (srcc, dstc) in ((11, 13), (12, 14)):
            pe_bc = E("tensor", lambda e, s=srcc, d=dstc: e.matmul(
                out=LBp[:, 200 + d:201 + d], lhsT=onesrowf[0:1, 0:128],
                rhs=gq[0:1, s:s + 1], start=True, stop=True),
                waits=[(DV, dv_qs)], incs=[(PE, 1)])[PE]
        dv_bcast = E("vector", lambda e: e.tensor_copy(out=gq[:, 13:15], in_=LBp[:, 213:215]),
                     waits=[(PE, pe_bc)], incs=[(DV, 1)])[DV]

        dv_cast_slot = {i: 0 for i in range(4)}
        pe_stage_slot = {i: 0 for i in range(4)}
        dv_trcopy = 0
        wt_wo_store = 0

        def prep_wT(wname, region, to_scratch):
            nonlocal dv_trcopy
            dv_last = 0
            for h in range(2):
                xv = {}
                for fl in range(4):
                    xv[fl] = dma("sync", XS[fl], lambda e, w=wname, h=h, fl=fl: e.dma_start(
                        out=xb[:, fl * 1024:(fl + 1) * 1024],
                        in_=w_h[w][h * 512 + fl * 128:h * 512 + (fl + 1) * 128, :]),
                        waits=[(DV, dv_bcast), (DV, dv_cast_slot[fl])])
                for fl in range(4):
                    r = E("vector", lambda e, fl=fl: e.tensor_copy(
                        out=Qs[:, fl * 1024:(fl + 1) * 1024], in_=xb[:, fl * 1024:(fl + 1) * 1024]),
                        waits=[(XS[fl], xv[fl]), (PE, pe_stage_slot[fl])], incs=[(DV, 1)])
                    dv_cast_slot[fl] = r[DV]
                for fl in range(4):
                    for dc in range(8):
                        w = [(DV, dv_cast_slot[fl])]
                        if dc == 0:
                            w += [(DV, dv_trcopy), (CONST, ld_consts)]
                        E("tensor", lambda e, fl=fl, dc=dc: e.transpose(
                            out=TRs[:, dc * 128:(dc + 1) * 128],
                            in_=Qs[:, fl * 1024 + dc * 128:fl * 1024 + (dc + 1) * 128],
                            identity=ident[:]),
                            waits=w, incs=[(PE, 1)] if dc == 7 else ())
                    pe_stage_slot[fl] = PE.n
                    pe_last = PE.n
                    dst = wsl.ap().rearrange("p (r dcc k) -> p r dcc k", r=3, dcc=8)[
                        :, region, :, h * 512 + fl * 128:h * 512 + (fl + 1) * 128]
                    src3 = TRs.ap().rearrange("p (dcc t) -> p dcc t", dcc=8)
                    w2 = [(PE, pe_last)]
                    if region == 0 and wname != "wo":
                        w2.append((WT, wt_wo_store))
                    r = E("vector", lambda e, d=dst, s3=src3: e.tensor_copy(out=d, in_=s3),
                          waits=w2, incs=[(DV, 1)])
                    dv_trcopy = r[DV]
                    dv_last = r[DV]
            return dma("gpsimd", WT, lambda e, s=to_scratch, rg=region: e.dma_start(
                out=s, in_=wsl[:, rg * 8192:(rg + 1) * 8192]), waits=[(DV, dv_last)])

        wt_wo_store = prep_wT("wo", 0, wT_scr["wo"])
        prep_wT("wq", 0, wT_scr["wq"])
        prep_wT("wk", 1, wT_scr["wk"])
        prep_wT("wv", 2, wT_scr["wv"])

        trs_guard.__exit__(None, None, None)

        barrier()

        BW = {"wl": 0}

        def phase_a(seq, reload_weights, TRA, do_quant=False):
            wl_val = 0
            if reload_weights:
                for rg, n in ((0, "wq"), (1, "wk"), (2, "wv")):
                    wl_val = dma("gpsimd", WR, lambda e, rg=rg, n=n: e.dma_start(
                        out=wsl[:, rg * 8192:(rg + 1) * 8192], in_=wT_scr[n]))

            ac_x_free = {}
            pe_lnb = {0: 0, 1: 0}
            dv_sqb = {0: 0, 1: 0}
            st_a = {"ac_lnTc": 0}
            ac_sbbank = {0: 0, 1: 0}
            dv_obank = {0: (AC, 0), 1: (AC, 0)}
            qg_val = 0
            kg_val = 0
            chains = {}

            def emit_chain(g, sub):
                t = 4 * g + sub
                sl4 = t % 4
                sl2 = t % 2
                xs_v = dma("sync", XS[sl4], lambda e, seq=seq, t=t, sl4=sl4: e.dma_start(
                    out=xb[:, sl4 * 1024:(sl4 + 1) * 1024],
                    in_=x_in[seq, t * 128:(t + 1) * 128, :]),
                    waits=[ac_x_free.get(sl4, (AC, 0))])
                aca, dvs, _ = emit_ln(
                    xb[:, sl4 * 1024:(sl4 + 1) * 1024],
                    lnb[:, sl2 * 1024:(sl2 + 1) * 1024],
                    sq_slot=sl2,
                    waits_src=[(XS[sl4], xs_v)],
                    waits_sq=[(DV, dv_sqb[sl2])],
                    waits_out=[(PE, pe_lnb[sl2])])
                ac_x_free[sl4] = aca
                chains.setdefault(g, {})[sub] = aca
                dv_sqb[sl2] = dvs

            def emit_T(g, sub):
                t = 4 * g + sub
                sl2 = t % 2
                aca = chains[g][sub]
                for dc in range(8):
                    w = [aca]
                    if dc == 0:
                        w.append((AC, st_a["ac_lnTc"]))
                    E("tensor", lambda e, dc=dc, sl2=sl2, TRA=TRA: e.transpose(
                        out=TRA[:, dc * 128:(dc + 1) * 128],
                        in_=lnb[:, sl2 * 1024 + dc * 128:sl2 * 1024 + (dc + 1) * 128],
                        identity=ident[:]),
                        waits=w, incs=[(PE, 1)] if dc == 7 else ())
                pe_lnb[sl2] = PE.n
                pe_tr = PE.n
                dst = lnT.ap().rearrange("p (dcc t) -> p dcc t", dcc=8)[:, :, sub * 128:(sub + 1) * 128]
                src3 = TRA.ap().rearrange("p (dcc t) -> p dcc t", dcc=8)
                r = E("scalar", lambda e, d=dst, s3=src3: e.activation(out=d, in_=s3, func=AF.Identity, scale=1.0),
                      waits=[(PE, pe_tr)], incs=[(AC, 1)])
                st_a["ac_lnTc"] = r[AC]

            q_t1_free = {0: 0, 1: 0}
            q_ot_free = {0: 0, 1: 0}
            fkv = {0: 0, 1: 0}

            def quant_chunk(c):
                sl = c % 2
                fkv[sl] = dma("sync", FK[sl], lambda e, c=c, sl=sl: e.dma_start(
                    out=t1[:, sl * 1024:(sl + 1) * 1024],
                    in_=w_h["fk"][c * 128:(c + 1) * 128, :]),
                    waits=[(DV, q_t1_free[sl])])
                xsl = t1[:, sl * 1024:(sl + 1) * 1024]
                ssl = x2[:]
                tsl = t2[:, 0:1024]
                osl = OT_sb[:, sl * 1024:(sl + 1) * 1024]
                r = E("vector", lambda e, x_=xsl, s=ssl: e.tensor_scalar_mul(
                    out=s, in0=x_, scalar1=gq[:, 13:14]),
                    waits=[(FK[sl], fkv[sl])])
                q_t1_free[sl] = r[DV]
                E("vector", lambda e, s=ssl, t=tsl: e.tensor_scalar(
                    out=t, in0=s, scalar1=-6.5, scalar2=-7.0, op0=ALU.is_ge, op1=ALU.add))
                for kk_ in range(-6, 7):
                    E("vector", lambda e, s=ssl, t=tsl, c_=kk_ + 0.5: e.scalar_tensor_tensor(
                        out=t, in0=s, scalar=c_, in1=t, op0=ALU.is_ge, op1=ALU.add))
                dvq = E("vector", lambda e, t=tsl, o=osl: e.tensor_scalar_mul(
                    out=o, in0=t, scalar1=gq[:, 14:15]),
                    waits=[(FS[sl], q_ot_free[sl])], incs=[(DV, 1)])[DV]
                q_ot_free[sl] = dma("gpsimd", FS[sl], lambda e, c=c, o=osl: e.dma_start(
                    out=wfq_h[:, c * 1024:(c + 1) * 1024], in_=o), waits=[(DV, dvq)])

            emit_chain(0, 0)
            emit_chain(0, 1)
            for sub in range(4):
                emit_T(0, sub)
                if sub < 2:
                    emit_chain(0, sub + 2)

            for g in range(NGROUP):
                if g + 1 < NGROUP:
                    emit_chain(g + 1, 0)
                    emit_chain(g + 1, 1)
                if do_quant and g > 0:
                    quant_chunk(g - 1)

                def proj_fm(wreg, biascol, scale, evac_dst_fn, first_waits, src_shape_fn=None):
                    nonlocal ac_sbbank
                    if src_shape_fn is None:
                        src_shape_fn = lambda bk_: SBp[:, bk_ * 512:bk_ * 512 + 512]
                    last_ac = 0
                    for kc in range(8):
                        bk_ = kc % 2
                        w = [(AC, st_a["ac_lnTc"]), (AC, ac_sbbank[bk_])]
                        if kc == 0:
                            w += list(first_waits)
                            if wl_val:
                                w.append((WR, wl_val))
                        for dc in range(8):
                            E("tensor", lambda e, wreg=wreg, kc=kc, dc=dc, bk_=bk_: e.matmul(
                                out=SBp[:, bk_ * 512:bk_ * 512 + 512],
                                lhsT=wsl[:, wreg * 8192 + dc * 1024 + kc * 128: wreg * 8192 + dc * 1024 + (kc + 1) * 128],
                                rhs=lnT[:, dc * 512:(dc + 1) * 512],
                                start=(dc == 0), stop=(dc == 7)),
                                waits=w if dc == 0 else (), incs=[(PE, 1)] if dc == 7 else ())
                        pe_mm = PE.n
                        r = E("scalar", lambda e, kc=kc, bk_=bk_, bc=biascol, sc=scale, df=evac_dst_fn, sf=src_shape_fn: e.activation(
                            out=df(kc), in_=sf(bk_),
                            func=AF.Identity, bias=bc[:, kc:kc + 1], scale=sc),
                            waits=[(PE, pe_mm)], incs=[(AC, 1)])
                        ac_sbbank[bk_] = r[AC]
                        last_ac = r[AC]
                    return last_ac

                ac_q = proj_fm(0, bq32, INVSQ,
                               lambda kc: Qs[:, kc * 512:(kc + 1) * 512],
                               [(QG, qg_val)])
                for ss in range(2):
                    src = Qs.ap().rearrange("p (dcc t) -> p dcc t", dcc=8)[:, :, ss * 256:(ss + 1) * 256]
                    qg_val = dma("gpsimd", QG, lambda e, seq=seq, g=g, ss=ss, s=src: e.dma_start(
                        out=q_scr[seq, 2 * g + ss], in_=s), waits=[(AC, ac_q)])

                ktr_view = Ktr.ap().rearrange("p (tl ck) -> p tl ck", tl=4)
                sb3 = SBp.ap().rearrange("p (bb tl t) -> p bb tl t", bb=2, tl=4)
                ac_k = proj_fm(1, bkc, 1.0,
                               lambda kc: ktr_view[:, :, kc * 128:(kc + 1) * 128],
                               [(KG, kg_val)],
                               src_shape_fn=lambda bk_: sb3[:, bk_, :, :])
                for tl in range(4):
                    kg_val = dma("gpsimd", KG, lambda e, seq=seq, g=g, tl=tl: e.dma_start(
                        out=k_scr[seq, 4 * g + tl], in_=Ktr[:, tl * 1024:(tl + 1) * 1024]),
                        waits=[(AC, ac_k)])

                for sub in range(4):
                    t = 4 * g + sub
                    pidx = sub % 2
                    for h in range(2):
                        ps = OTp[pidx * 2 + h]
                        w = [dv_obank[pidx], chains[g][sub]]
                        for dc in range(8):
                            E("tensor", lambda e, sub=sub, h=h, dc=dc, ps=ps: e.matmul(
                                out=ps[:],
                                lhsT=lnT[:, dc * 512 + sub * 128:dc * 512 + (sub + 1) * 128],
                                rhs=wsl[:, 2 * 8192 + dc * 1024 + h * 512: 2 * 8192 + dc * 1024 + (h + 1) * 512],
                                start=(dc == 0), stop=False),
                                waits=w if dc == 0 else ())
                        E("tensor", lambda e, h=h, ps=ps: e.matmul(
                            out=ps[:],
                            lhsT=onesrow[0:1, 0:128],
                            rhs=bvrow[0:1, h * 512:(h + 1) * 512],
                            start=False, stop=True), incs=[(PE, 1)])
                    pe_v = PE.n
                    r = None
                    for h in range(2):
                        r = E("scalar", lambda e, t=t, h=h, ps=OTp[pidx * 2 + h]: e.activation(
                            out=V_buf[:, t * 1024 + h * 512:t * 1024 + (h + 1) * 512],
                            in_=ps[:], func=AF.Identity, scale=1.0),
                            waits=[(PE, pe_v)], incs=[(AC, 1)] if h == 1 else ())
                    dv_obank[pidx] = (AC, r[AC])
                    if g + 1 < NGROUP:
                        emit_T(g + 1, sub)
                        if sub < 2:
                            emit_chain(g + 1, sub + 2)
                if do_quant and g == NGROUP - 1:
                    quant_chunk(g)
                if g == NGROUP - 1:
                    BW["wl"] = dma("sync", WL, lambda e: e.dma_start(
                        out=wsl[:, 0:8192], in_=wT_scr["wo"]), waits=[(PE, PE.n)])
                    BW["wl"] = dma("sync", WL, lambda e: e.dma_start(
                        out=wsl[:, 8192:16384], in_=wfq_h),
                        waits=[(PE, PE.n), (FS[0], q_ot_free[0]), (FS[1], q_ot_free[1])])

        def phase_b(seq, EB):
            x2s = {0: x2, 1: t2}
            EBs = (EB, LBp)
            wl_val = BW["wl"]

            st = {
                "ys": {i: 0 for i in range(4)},
                "pe_slot": {i: 0 for i in range(4)},
                "pe_flashend": {},
                "pe_pvlast": {},
                "ac_otevac": 0,
                "ac_otevac_b": {i: 0 for i in range(4)},
                "dv_rl": {},
                "ac_rl_user": 0,
                "ac_ebh": {0: 0, 1: 0},
                "dv_ebh": {0: 0, 1: 0},
                "eb_flip": 0,
                "dv_lread": 0,
                "pe_eb_reader": 0,
                "dv_sqb": {0: (DV, 0), 1: (DV, 0)},
                "pe_ln2T": {0: 0, 1: 0},
                "pe_ffn": {0: 0, 1: 0},
                "dv_xat": {0: 0, 1: 0},
                "dv_t1": {0: 0, 1: 0},
                "qs": {0: 0, 1: 0},
                "xs": {},
                "ks": {},
                "ot_sb_reader": 0,
                "ac_abs_bank": {0: 0, 1: 0},
                "pe_pv_slot": {0: 0, 1: 0},
            }
            pend = []

            def make_epilogue(s):
                pre = []
                main = []
                spill = []

                def t_rl():
                    v3 = SBp.ap().rearrange("p (b c) -> p b c", b=2)
                    E("vector", lambda e, v3=v3: e.reduce_sum(
                        out=gq[:, 0:1], in_=v3[:, :, 256:256 + s + 1], axis=AX.XY),
                        waits=[(PE, st["pe_pvlast"][s]), (AC, st["ac_rl_user"])])
                    E("vector", lambda e, v3=v3: e.reduce_sum(
                        out=gq[:, 1:2], in_=v3[:, :, 272:272 + s + 1], axis=AX.XY))
                    r = E("vector", lambda e: e.reciprocal(out=rl[:], in_=gq[:, 0:2]),
                          incs=[(DV, 1)])
                    st["dv_rl"][s] = r[DV]
                    st["dv_lread"] = r[DV]
                pre.append(t_rl)

                def t_otevac(b):
                    def f():
                        r = E("scalar", lambda e, b=b: e.activation(
                            out=OT_sb[:, b * 512:(b + 1) * 512], in_=OTp[b][:],
                            func=AF.Identity, scale=1.0),
                            waits=[(PE, st["pe_pvlast"][s]), (PE, st["ot_sb_reader"])],
                            incs=[(AC, 1)])
                        st["ac_otevac"] = r[AC]
                        st["ac_otevac_b"][b] = r[AC]
                    return f
                for b_ in range(4):
                    pre.append(t_otevac(b_))

                boxes = {0: {}, 1: {}}

                def outproj_h(qc, h):
                    def f():
                        box = boxes[qc]
                        bk = st["eb_flip"]
                        st["eb_flip"] ^= 1
                        dst = EBs[bk]
                        w = [(AC, st["ac_otevac"]), (AC, st["ac_ebh"][bk]),
                             (DV, st["dv_ebh"][bk]), (WL, wl_val)]
                        for vc in range(8):
                            E("tensor", lambda e, h=h, vc=vc, qc=qc, dst=dst: e.matmul(
                                out=dst[:],
                                lhsT=OT_sb[:, vc * 256 + qc * 128:vc * 256 + (qc + 1) * 128],
                                rhs=wsl[:, vc * 1024 + h * 512:vc * 1024 + (h + 1) * 512],
                                start=(vc == 0), stop=(vc == 7)),
                                waits=w if vc == 0 else (), incs=[(PE, 1)] if vc == 7 else ())
                        st["ot_sb_reader"] = PE.n
                        pe_mm = PE.n
                        r = E("scalar", lambda e, h=h, qc=qc, dst=dst: e.activation(
                            out=t1[:, qc * 1024 + h * 512:qc * 1024 + (h + 1) * 512], in_=dst[:],
                            func=AF.Identity, scale=rl[:, qc:qc + 1]),
                            waits=[(PE, pe_mm), (DV, st["dv_rl"][s]), (DV, st["dv_t1"][qc])],
                            incs=[(AC, 1)])
                        st["ac_ebh"][bk] = r[AC]
                        st["ac_rl_user"] = r[AC]
                        box[f"ev{h}"] = r[AC]
                    return f

                def lnattn_a(qc):
                    def f():
                        box = boxes[qc]
                        t1s = t1[:, qc * 1024:(qc + 1) * 1024]
                        dvb = E("vector", lambda e, t1s=t1s: e.tensor_add(out=t1s, in0=t1s, in1=bo_tile[:]),
                                waits=[(AC, box["ev1"])], incs=[(DV, 1)])[DV]
                        aca, dvs, dv1 = emit_ln(t1s, sqb[:, qc * 1024:(qc + 1) * 1024], sq_slot=qc,
                                                waits_src=[(DV, dvb)],
                                                waits_sq=[st["dv_sqb"][qc],
                                                          (PE, st["pe_ln2T"][qc])],
                                                apply_eng="vector")
                        box["dv_ln1"] = dv1
                        box["ac_ln1"] = aca
                    return f

                def lnattn_b(qc):
                    def f():
                        box = boxes[qc]
                        t = 2 * s + qc
                        sl4 = t % 4
                        dv_x2 = E("vector", lambda e, sl4=sl4, qc=qc: e.tensor_add(
                            out=x2s[qc][:], in0=xb[:, sl4 * 1024:(sl4 + 1) * 1024],
                            in1=sqb[:, qc * 1024:(qc + 1) * 1024]),
                            waits=[box["ac_ln1"], (XS[sl4], st["xs"][t])], incs=[(DV, 1)])[DV]
                        ac2, dvs2, _ = emit_ln(x2s[qc][:], sqb[:, qc * 1024:(qc + 1) * 1024], sq_slot=qc,
                                               waits_src=[(DV, dv_x2)],
                                               waits_sq=[(DV, dv_x2)],
                                               waits_out=[(PE, st["pe_ln2T"][qc])],
                                               apply_eng="vector")
                        box["ac_ln2"] = ac2
                        st["dv_sqb"][qc] = (DV, dvs2)
                    return f

                def ln2t_pe(qc):
                    def f():
                        box = boxes[qc]
                        last = 0
                        for rnd in range(2):
                            bk = st["eb_flip"]
                            st["eb_flip"] ^= 1
                            dst = EBs[bk]
                            wfirst = [box["ac_ln2"], (AC, st["ac_ebh"][bk]),
                                      (DV, st["dv_ebh"][bk])]
                            for c in range(4):
                                dc = 4 * rnd + c
                                E("tensor", lambda e, qc=qc, dc=dc, c=c, dst=dst: e.transpose(
                                    out=dst[:, c * 128:(c + 1) * 128],
                                    in_=sqb[:, qc * 1024 + dc * 128:qc * 1024 + (dc + 1) * 128],
                                    identity=identf[:]),
                                    waits=wfirst if c == 0 else (), incs=[(PE, 1)] if c == 3 else ())
                            pe_t = PE.n
                            r2 = E("scalar", lambda e, qc=qc, rnd=rnd, dst=dst: e.activation(
                                out=lnT[:, qc * 1024 + rnd * 512:qc * 1024 + (rnd + 1) * 512],
                                in_=dst[:, 0:512], func=AF.Identity, scale=1.0),
                                waits=[(PE, pe_t)], incs=[(AC, 1)])
                            st["ac_ebh"][bk] = r2[AC]
                            last = r2[AC]
                        st["pe_ln2T"][qc] = PE.n
                        box["ac_lnT"] = last
                    return f

                def xa_stage(qc):
                    def f():
                        box = boxes[qc]
                        bk = st["eb_flip"]
                        st["eb_flip"] ^= 1
                        dst = EBs[bk]
                        w = [(AC, box["ac_lnT"]), (AC, st["ac_ebh"][bk]),
                             (DV, st["dv_ebh"][bk]), (DV, st["dv_rl"][s])]
                        for dc in range(8):
                            E("tensor", lambda e, dc=dc, qc=qc, dst=dst: e.matmul(
                                out=dst[0:16, 0:128],
                                lhsT=a_bf[:, dc * 16:(dc + 1) * 16],
                                rhs=lnT[:, qc * 1024 + dc * 128:qc * 1024 + (dc + 1) * 128],
                                start=(dc == 0), stop=(dc == 7)),
                                waits=w if dc == 0 else (), incs=[(PE, 1)] if dc == 7 else ())
                        pe_xa = PE.n
                        r = E("vector", lambda e, qc=qc, dst=dst: e.tensor_copy(
                            out=xaT[:, qc * 128:(qc + 1) * 128],
                            in_=dst[0:16, 0:128]),
                            waits=[(PE, pe_xa)], incs=[(DV, 1)])
                        st["dv_xat"][qc] = r[DV]
                        st["dv_ebh"][bk] = r[DV]
                    return f

                def ffn_h(qc, h):
                    def f():
                        box = boxes[qc]
                        bk = st["eb_flip"]
                        st["eb_flip"] ^= 1
                        dst = EBs[bk]
                        w = [(DV, st["dv_xat"][qc]), (AC, st["ac_ebh"][bk]),
                             (DV, st["dv_ebh"][bk])]
                        for dc in range(8):
                            E("tensor", lambda e, h=h, dc=dc, qc=qc, dst=dst: e.matmul(
                                out=dst[:],
                                lhsT=lnT[:, qc * 1024 + dc * 128:qc * 1024 + (dc + 1) * 128],
                                rhs=wsl[:, 8192 + dc * 1024 + h * 512:8192 + dc * 1024 + (h + 1) * 512],
                                start=(dc == 0), stop=False),
                                waits=w if dc == 0 else ())
                        E("tensor", lambda e, h=h, dst=dst: e.matmul(
                            out=dst[:], lhsT=onesrow[0:1, 0:128], rhs=fbrow[0:1, h * 512:(h + 1) * 512],
                            start=False, stop=False))
                        E("tensor", lambda e, h=h, qc=qc, dst=dst: e.matmul(
                            out=dst[:], lhsT=xaT[0:16, qc * 128:(qc + 1) * 128],
                            rhs=b_bf[0:16, h * 512:(h + 1) * 512],
                            start=False, stop=True), incs=[(PE, 1)])
                        st["pe_ffn"][qc] = PE.n
                        pe_mm = PE.n
                        r = E("scalar", lambda e, h=h, qc=qc, dst=dst: e.activation(
                            out=t1[:, qc * 1024 + h * 512:qc * 1024 + (h + 1) * 512], in_=dst[:],
                            func=AF.Identity, scale=1.0),
                            waits=[(PE, pe_mm), (DV, box["dv_ln1"])], incs=[(AC, 1)])
                        st["ac_ebh"][bk] = r[AC]
                        box[f"fev{h}"] = r[AC]
                    return f

                def lnffn_chain(qc):
                    def f():
                        box = boxes[qc]
                        t = 2 * s + qc
                        sl4 = t % 4
                        acf, dvs3, dv1f = emit_ln(t1[:, qc * 1024:(qc + 1) * 1024],
                                                  sqb[:, qc * 1024:(qc + 1) * 1024], sq_slot=qc,
                                                  waits_src=[(AC, box["fev1"])],
                                                  waits_sq=[st["dv_sqb"][qc],
                                                            (PE, st["pe_ln2T"][qc])],
                                                  apply_eng="vector")
                        st["dv_t1"][qc] = dv1f
                        dv_fin = E("vector", lambda e, sl4=sl4, qc=qc: e.tensor_add(
                            out=xb[:, sl4 * 1024:(sl4 + 1) * 1024],
                            in0=x2s[qc][:], in1=sqb[:, qc * 1024:(qc + 1) * 1024]),
                            waits=[acf], incs=[(DV, 1)])[DV]
                        st["dv_sqb"][qc] = (DV, dv_fin)
                        box["dv_fin"] = dv_fin
                    return f

                def store_stage(qc):
                    def f():
                        box = boxes[qc]
                        t = 2 * s + qc
                        sl4 = t % 4
                        st["ys"][sl4] = dma("gpsimd", YS[sl4], lambda e, seq=seq, t=t, sl4=sl4: e.dma_start(
                            out=y_out[seq, t * 128:(t + 1) * 128, :],
                            in_=xb[:, sl4 * 1024:(sl4 + 1) * 1024]),
                            waits=[(DV, box["dv_fin"])])
                        if t + 4 < 2 * NSTRIP:
                            st["xs"][t + 4] = dma("sync", XS[sl4], lambda e, seq=seq, t=t + 4, sl4=sl4: e.dma_start(
                                out=xb[:, sl4 * 1024:(sl4 + 1) * 1024],
                                in_=x_in[seq, t * 128:(t + 1) * 128, :]),
                                waits=[(YS[sl4], st["ys"][sl4])])
                    return f

                main.append((0.40, outproj_h(0, 0)))
                main.append((0.44, outproj_h(0, 1)))
                main.append((0.48, outproj_h(1, 0)))
                main.append((0.52, outproj_h(1, 1)))
                main.append((0.54, lnattn_a(0)))
                main.append((0.58, lnattn_b(0)))
                main.append((0.78, ln2t_pe(0)))
                main.append((0.84, xa_stage(0)))
                main.append((0.87, ffn_h(0, 0)))
                main.append((0.90, ffn_h(0, 1)))
                main.append((0.93, lnffn_chain(0)))
                spill.append((0.02, lnattn_a(1)))
                spill.append((0.06, lnattn_b(1)))
                if st.get("store1_thunk") is not None:
                    spill.append((0.10, st["store1_thunk"]))
                spill.append((0.30, ln2t_pe(1)))
                spill.append((0.34, xa_stage(1)))
                spill.append((0.34, store_stage(0)))
                spill.append((0.38, ffn_h(1, 0)))
                spill.append((0.42, ffn_h(1, 1)))
                spill.append((0.46, lnffn_chain(1)))
                st["store1_thunk"] = store_stage(1)
                return pre, main, spill

            epi_pre = []
            pend = []
            spill_next = []

            for s in range(NSTRIP):
                st["qs"][s % 2] = dma("sync", QS[s % 2], lambda e, seq=seq, s=s: e.dma_start(
                    out=Qs[:, (s % 2) * 2048:(s % 2 + 1) * 2048], in_=q_scr[seq, s]),
                    waits=[(PE, st["pe_flashend"].get(s - 2, 0))])
                if s < 2:
                    for qc in range(2):
                        t = 2 * s + qc
                        st["xs"][t] = dma("sync", XS[t % 4], lambda e, seq=seq, t=t: e.dma_start(
                            out=xb[:, (t % 4) * 1024:(t % 4 + 1) * 1024],
                            in_=x_in[seq, t * 128:(t + 1) * 128, :]))
                nt = 2 * s + 2

                for f in epi_pre:
                    f()
                epi_pre = []

                ac_abs = {}
                ac_exp = {}
                pe_pv = {}
                qof = (s % 2) * 2048
                n_pend = len(pend)
                popped = 0

                def emit_S(j):
                    sp = j % 2
                    w = [(KS[j % 4], st["ks"][j]),
                         (AC, ac_abs.get(j - 2, st["ac_abs_bank"][sp]))]
                    if j == 0:
                        w += [(QS[s % 2], st["qs"][s % 2])]
                    if j < 2:
                        w.append((DV, st["dv_lread"]))
                    for dc in range(8):
                        E("tensor", lambda e, j=j, dc=dc, sp=sp, qof=qof: e.matmul(
                            out=SBp[:, sp * 512:sp * 512 + 256],
                            lhsT=Ktr[:, (j % 4) * 1024 + dc * 128:(j % 4) * 1024 + (dc + 1) * 128],
                            rhs=Qs[:, qof + dc * 256:qof + (dc + 1) * 256],
                            start=(dc == 0), stop=(dc == 7)),
                            waits=w if dc == 0 else (), incs=[(PE, 1)] if dc == 7 else ())
                    st["pe_slot"][j % 4] = PE.n
                    pe_sT = PE.n
                    r = E("scalar", lambda e, sp=sp: e.activation(
                        out=Sa[:, sp * 256:(sp + 1) * 256], in_=SBp[:, sp * 512:sp * 512 + 256],
                        func=AF.Abs, scale=1.0), waits=[(PE, pe_sT)], incs=[(AC, 1)])
                    ac_abs[j] = r[AC]
                    st["ac_abs_bank"][sp] = r[AC]
                    wexp = [(AC, ac_abs[j])]
                    if j >= 2 * s:
                        r = E("vector", lambda e, sp=sp, ms=(j - 2 * s): e.tensor_tensor(
                            out=Sa[:, sp * 256:(sp + 1) * 256], in0=Sa[:, sp * 256:(sp + 1) * 256],
                            in1=masks[:, ms * 256:(ms + 1) * 256], op=ALU.add),
                            waits=[(AC, ac_abs[j]), (CONST, ld_consts)], incs=[(DV, 1)])
                        wexp.append((DV, r[DV]))
                    wexp.append((PE, pe_pv.get(j - 2, st["pe_pv_slot"][sp])))
                    r = E("scalar", lambda e, sp=sp: e.activation(
                        out=PT[:, sp * 256:(sp + 1) * 256], in_=Sa[:, sp * 256:(sp + 1) * 256],
                        func=AF.Exp, scale=1.0),
                        waits=wexp, incs=[(AC, 1)])
                    ac_exp[j] = r[AC]

                def emit_PV(j):
                    sp = j % 2
                    last = (j == nt - 1)
                    frst = (j == 0)
                    w = [(AC, ac_exp[j])]
                    first = True
                    for vc in range(8):
                        b_ = vc // 2
                        col = (vc % 2) * 256
                        wv = list(w) if first else []
                        if frst and vc % 2 == 0:
                            wv.append((AC, st["ac_otevac_b"][b_]))
                        E("tensor", lambda e, j=j, sp=sp, vc=vc, b_=b_, col=col,
                          stt=(frst and vc % 2 == 0),
                          lst=(last and vc % 2 == 1): e.matmul(
                            out=OTp[b_][:, col:col + 256],
                            lhsT=V_buf[:, j * 1024 + vc * 128:j * 1024 + (vc + 1) * 128],
                            rhs=PT[:, sp * 256:(sp + 1) * 256],
                            start=stt, stop=lst),
                            waits=wv)
                        first = False
                    for qc in range(2):
                        col = sp * 512 + 256 + qc * 16 + j // 2
                        wl_ = [(DV, st["dv_lread"])] if (frst and qc == 0) else []
                        E("tensor", lambda e, j=j, sp=sp, qc=qc, col=col: e.matmul(
                            out=SBp[:, col:col + 1],
                            lhsT=PT[:, sp * 256 + qc * 128:sp * 256 + (qc + 1) * 128],
                            rhs=onescol[:, 0:1], start=True, stop=True),
                            waits=wl_, incs=[(PE, 1)] if qc == 1 else ())
                    pe_pv[j] = PE.n
                    st["pe_pv_slot"][sp] = PE.n

                for j in range(nt):
                    st["ks"][j] = dma("gpsimd", KS[j % 4], lambda e, seq=seq, j=j: e.dma_start(
                        out=Ktr[:, (j % 4) * 1024:(j % 4 + 1) * 1024], in_=k_scr[seq, j]),
                        waits=[(PE, st["pe_slot"][j % 4])])
                    emit_S(j)
                    if j > 0:
                        emit_PV(j - 1)
                    while popped < n_pend and int(pend[popped][0] * nt) <= j:
                        pend[popped][1]()
                        popped += 1
                emit_PV(nt - 1)
                while popped < n_pend:
                    pend[popped][1]()
                    popped += 1
                st["pe_flashend"][s] = PE.n
                st["pe_pvlast"][s] = PE.n
                epi_pre, e_main, e_spill = make_epilogue(s)
                pend = list(spill_next) + list(e_main)
                spill_next = e_spill

            for f in epi_pre:
                f()
            for _, f in pend:
                f()
            for _, f in spill_next:
                f()
            if st.get("store1_thunk") is not None:
                st["store1_thunk"]()
                st["store1_thunk"] = None

        def with_bank7(name, shape, dtype, fn):
            g = nc.psum_tensor(name, shape, dtype)
            t = g.__enter__()
            fn(t)
            g.__exit__(None, None, None)

        with_bank7("TRA0", [128, 1024], BF16, lambda t: phase_a(0, False, t, do_quant=True))
        barrier()
        if debug:
            dma("gpsimd", WT, lambda e: e.dma_start(out=v_dbg, in_=V_buf[:]))
        with_bank7("EB0", [128, 512], F32, lambda t: phase_b(0, t))
        barrier()
        with_bank7("TRA1", [128, 1024], BF16, lambda t: phase_a(1, True, t))
        barrier()
        with_bank7("EB1", [128, 512], F32, lambda t: phase_b(1, t))
        barrier()
        fin = [(s, s.n) for s in B.queue_sems["gpsimd"]]
        E("gpsimd", lambda e: e.nop(), waits=fin, chain=False)

        def replay(eng_name):
            def go(eng):
                for fn, waits, incs in B.ops[eng_name]:
                    for sh, v in waits:
                        eng.wait_ge(sh, v)
                    inst = fn(eng)
                    if incs:
                        for sh, amt in incs:
                            inst.then_inc(sh, amt)
            return go

        block.tensor(replay("tensor"))
        block.vector(replay("vector"))
        block.scalar(replay("scalar"))
        block.sync(replay("sync"))
        block.gpsimd(replay("gpsimd"))

    return nc


_CACHE = {}


def kernel(**inputs):
    x = np.ascontiguousarray(np.asarray(inputs["L0_attn_x"], dtype=np.float32))
    wmap = {
        "wq": inputs["L0_attn_Wq"], "wk": inputs["L0_attn_Wk"],
        "wv": inputs["L0_attn_Wv"], "wo": inputs["L0_attn_Wo"],
        "fk": inputs["L0_ffn_kernel"],
    }
    bmap = {
        "bq": inputs["L0_attn_bq"], "bk": inputs["L0_attn_bk"],
        "bv": inputs["L0_attn_bv"], "bo": inputs["L0_attn_bo"],
        "fbias": inputs["L0_ffn_bias"],
    }
    base = {}
    for k, v in wmap.items():
        base[k] = np.ascontiguousarray(np.asarray(v, dtype=np.float32))
    for k, v in bmap.items():
        base[k] = np.ascontiguousarray(np.asarray(v, dtype=np.float32)).reshape(1, D)
    base["fa"] = np.ascontiguousarray(np.asarray(inputs["L0_ffn_a"], dtype=np.float32))
    base["fb"] = np.ascontiguousarray(np.asarray(inputs["L0_ffn_b"], dtype=np.float32))

    if "nc" not in _CACHE:
        _CACHE["nc"] = build_nc()
    nc = _CACHE["nc"]

    in_maps = []
    for c in range(8):
        m = dict(base)
        m["x"] = np.ascontiguousarray(x[2 * c:2 * c + 2])
        in_maps.append(m)

    res = run_bass_kernel_spmd(nc, in_maps, core_ids=list(range(8)))
    y = np.concatenate([r["y"] for r in res.results], axis=0)
    return y.reshape(16, SEQ, D)


if __name__ == "__main__":
    build_nc()
    print("build ok")


# version 28
# speedup vs baseline: 1.1297x; 1.0446x over previous
import numpy as np
import ml_dtypes

import concourse.bass as bass
import concourse.mybir as mybir
from concourse.bass_utils import run_bass_kernel_spmd

F32 = mybir.dt.float32
BF16 = mybir.dt.bfloat16
AF = mybir.ActivationFunctionType
ALU = mybir.AluOpType
AX = mybir.AxisListType

D = 1024
SEQ = 4096
NSEQ = 2
NTILE = SEQ // 128
NGROUP = SEQ // 512
NSTRIP = SEQ // 256
NEGBIG = -1.0e30
INVSQ = 1.0 / 32.0

ENGS = ("tensor", "vector", "scalar", "sync", "gpsimd")


class Sem:
    def __init__(self, handle):
        self.h = handle
        self.n = 0


class Builder:
    def __init__(self):
        self.ops = {e: [] for e in ENGS}
        self.queue_sems = {"sync": {}, "gpsimd": {}, "scalar": {}}
        self.eng_sem = {}

    def E(self, eng, fn, waits=(), incs=(), chain=True):
        waits = list(waits)
        incs = list(incs)
        es = self.eng_sem.get(eng) if chain else None
        if es is not None:
            waits.append((es, es.n))
            if not any(s is es for s, _ in incs):
                incs.append((es, 1))
        rincs = []
        out = {}
        for s, amt in incs:
            s.n += amt
            rincs.append((s.h, amt))
            out[s] = s.n
        self.ops[eng].append(
            (fn, tuple((s.h, int(v)) for s, v in waits if v > 0), tuple(rincs)))
        return out

    def dma(self, eng, sem, fn, waits=()):
        sem.n += 16
        self.queue_sems[eng][sem] = True
        self.ops[eng].append(
            (fn, tuple((s.h, int(v)) for s, v in waits if v > 0), ((sem.h, 16),)))
        return sem.n


def build_nc(debug=False):
    nc = bass.Bass("TRN2", target_bir_lowering=False)
    scr_kind = "ExternalOutput" if debug else "Internal"

    x_in = nc.dram_tensor("x", [NSEQ, SEQ, D], F32, kind="ExternalInput").ap()
    w_h = {n: nc.dram_tensor(n, [D, D], F32, kind="ExternalInput").ap()
           for n in ("wq", "wk", "wv", "wo", "fk")}
    b_h = {n: nc.dram_tensor(n, [1, D], F32, kind="ExternalInput").ap()
           for n in ("bq", "bk", "bv", "bo", "fbias")}
    fa_h = nc.dram_tensor("fa", [D, 16], F32, kind="ExternalInput").ap()
    fb_h = nc.dram_tensor("fb", [16, D], F32, kind="ExternalInput").ap()
    y_out = nc.dram_tensor("y", [NSEQ, SEQ, D], F32, kind="ExternalOutput").ap()

    q_scr = nc.dram_tensor("q_scr", [NSEQ, NSTRIP, 128, 2048], BF16, kind=scr_kind).ap()
    k_scr = nc.dram_tensor("k_scr", [NSEQ, NTILE, 128, 1024], BF16, kind=scr_kind).ap()
    v_dbg = nc.dram_tensor("v_dbg", [128, NTILE * 1024], BF16, kind="ExternalOutput").ap() if debug else None
    wT_scr = {n: nc.dram_tensor(n + "T_h", [128, 8192], BF16, kind="Internal").ap()
              for n in ("wq", "wk", "wv", "wo")}
    wfq_h = nc.dram_tensor("wfq_h", [128, 8192], BF16, kind=scr_kind).ap()

    ident_b_h = nc.inline_tensor(np.eye(128, dtype=ml_dtypes.bfloat16), "ident_b").ap()
    ident_f_h = nc.inline_tensor(np.eye(128, dtype=np.float32), "ident_f").ap()
    mknp = np.empty((128, 512), dtype=np.float32)
    kk = np.arange(128)[:, None]
    qq = np.arange(256)[None, :]
    mknp[:, 0:256] = np.where(kk <= qq, 0.0, NEGBIG)
    mknp[:, 256:512] = np.where(128 + kk <= qq, 0.0, NEGBIG)
    masks_h = nc.inline_tensor(mknp, "masks_h").ap()

    from contextlib import ExitStack
    with ExitStack() as ctx:
        ec = ctx.enter_context

        V_buf = ec(nc.sbuf_tensor("V_buf", [128, NTILE * 1024], BF16))
        wsl = ec(nc.sbuf_tensor("wsl", [128, 3 * 8192], BF16))
        a_bf = ec(nc.sbuf_tensor("a_bf", [128, 128], BF16))
        b_bf = ec(nc.sbuf_tensor("b_bf", [16, 1024], BF16))
        ident = ec(nc.sbuf_tensor("ident", [128, 128], BF16))
        identf = ec(nc.sbuf_tensor("identf", [128, 128], F32))
        masks = ec(nc.sbuf_tensor("masks", [128, 512], F32))
        bq32 = ec(nc.sbuf_tensor("bq32", [128, 8], F32))
        bkc = ec(nc.sbuf_tensor("bkc", [128, 8], F32))
        bvrow = ec(nc.sbuf_tensor("bvrow", [1, 1024], BF16))
        fbrow = ec(nc.sbuf_tensor("fbrow", [1, 1024], BF16))
        bo_tile = ec(nc.sbuf_tensor("bo_tile", [128, 1024], F32))
        onesrow = ec(nc.sbuf_tensor("onesrow", [1, 256], BF16))
        onesrowf = ec(nc.sbuf_tensor("onesrowf", [1, 128], F32))
        onescol = ec(nc.sbuf_tensor("onescol", [128, 1], BF16))
        zrow = ec(nc.sbuf_tensor("zrow", [1, 512], BF16))
        epscol = ec(nc.sbuf_tensor("epscol", [128, 1], F32))
        rowstage = ec(nc.sbuf_tensor("rowstage", [1, 1024], F32))
        stats = ec(nc.sbuf_tensor("stats", [128, 32], F32))
        gq = ec(nc.sbuf_tensor("gq", [128, 16], F32))
        rl = ec(nc.sbuf_tensor("rl", [128, 2], F32))

        xb = ec(nc.sbuf_tensor("xb", [128, 4096], F32))
        sqb = ec(nc.sbuf_tensor("sqb", [128, 2048], F32))
        lnb = ec(nc.sbuf_tensor("lnb", [128, 2048], BF16))
        lnT = ec(nc.sbuf_tensor("lnT", [128, 4096], BF16))
        Qs = ec(nc.sbuf_tensor("Qs", [128, 4096], BF16))
        Ktr = ec(nc.sbuf_tensor("Ktr", [128, 4096], BF16))
        Sa = ec(nc.sbuf_tensor("Sa", [128, 512], F32))
        PT = ec(nc.sbuf_tensor("PT", [128, 512], BF16))
        OT_sb = ec(nc.sbuf_tensor("OT_sb", [128, 2048], BF16))
        t1 = ec(nc.sbuf_tensor("t1", [128, 2048], F32))
        t2 = ec(nc.sbuf_tensor("t2", [128, 1024], F32))
        x2 = ec(nc.sbuf_tensor("x2", [128, 1024], F32))
        xaT = ec(nc.sbuf_tensor("xaT", [16, 256], BF16))

        OTp = [ec(nc.psum_tensor(f"OTp{i}", [128, 512], F32)) for i in range(4)]
        SBp = ec(nc.psum_tensor("SBp", [128, 1024], F32))
        LBp = ec(nc.psum_tensor("LBp", [128, 512], F32))
        trs_guard = nc.psum_tensor("TRs", [128, 1024], BF16)
        TRs = trs_guard.__enter__()

        def mksem(name):
            return Sem(ec(nc.semaphore(name)))

        PE, DV, AC, GP, BAR = (mksem(n) for n in ("PE", "DV", "AC", "GP", "BAR"))
        XS = [mksem(f"XS{i}") for i in range(4)]
        YS = [mksem(f"YS{i}") for i in range(4)]
        QS = [mksem(f"QS{i}") for i in range(2)]
        KS = [mksem(f"KS{i}") for i in range(4)]
        WS = [mksem(f"WS{i}") for i in range(2)]
        FK = [mksem(f"FK{i}") for i in range(2)]
        FS = [mksem(f"FS{i}") for i in range(2)]
        WR = mksem("WR")
        TS = [mksem(f"TS{i}") for i in range(2)]
        TA, TB = mksem("TA"), mksem("TB")
        QG, KG, WL, RS, CONST, WT = (mksem(n) for n in ("QG", "KG", "WL", "RS", "CONST", "WT"))

        B = Builder()
        B.eng_sem = {"vector": DV, "scalar": AC, "gpsimd": GP}
        E, dma = B.E, B.dma

        block = ec(nc.Block())

        def barrier():
            k = BAR.n // 5 + 1
            for eng in ENGS:
                w = []
                if eng in ("sync", "gpsimd", "scalar"):
                    w = [(s, s.n) for s in B.queue_sems[eng]]

                def f(e, k=k, bh=BAR.h):
                    e.drain()
                    e.sem_inc(bh, 1)
                    e.wait_ge(bh, 5 * k)
                E(eng, f, waits=w, chain=False)
            BAR.n = 5 * k

        ln_slot = [0]

        def emit_ln(src_ap, out_ap, sq_slot, waits_src=(), waits_sq=(), waits_out=(),
                    veng="vector", vsem=None, apply_eng="scalar"):
            sl = (ln_slot[0] % 4) * 8
            ln_slot[0] += 1
            cS1 = stats[:, sl + 0:sl + 1]
            cMn = stats[:, sl + 1:sl + 2]
            cS2 = stats[:, sl + 2:sl + 3]
            cM2 = stats[:, sl + 3:sl + 4]
            cB2 = stats[:, sl + 4:sl + 5]
            cLn = stats[:, sl + 5:sl + 6]
            cR = stats[:, sl + 6:sl + 7]
            cBn = stats[:, sl + 7:sl + 8]
            sq = sqb[:, sq_slot * 1024:(sq_slot + 1) * 1024]
            VS = vsem if vsem is not None else DV

            E(veng, lambda e, s=src_ap, o=cS1: e.reduce_sum(out=o, in_=s, axis=AX.X),
              waits=waits_src)
            rsq = E(veng,
                    lambda e, s=src_ap, o=sq, a=cS2: e.scalar_tensor_tensor(
                        out=o, in0=s, scalar=1.0, in1=s, op0=ALU.mult, op1=ALU.mult, accum_out=a),
                    waits=tuple(waits_src) + tuple(waits_sq))
            r2 = E(veng, lambda e, i=cS1, o=cMn: e.tensor_scalar_mul(out=o, in0=i, scalar1=-1.0 / D))
            E("scalar", lambda e, i=cMn, o=cM2: e.activation(out=o, in_=i, func=AF.Square, scale=1.0),
              waits=[(VS, r2[VS])])
            E("scalar", lambda e, i=cM2, o=cB2, b=epscol[:, 0:1]: e.activation(
                out=o, in_=i, func=AF.Identity, bias=b, scale=-1.0))
            E("scalar", lambda e, i=cS2, o=cLn, b=cB2: e.activation(
                out=o, in_=i, func=AF.Ln, bias=b, scale=1.0 / D),
                waits=[(VS, rsq[VS])])
            ax = E("scalar", lambda e, i=cLn, o=cR: e.activation(out=o, in_=i, func=AF.Exp, scale=-0.5),
                   incs=[(AC, 1)])
            r4 = E(veng, lambda e, a=cMn, bb=cR, o=cBn: e.tensor_mul(out=o, in0=a, in1=bb),
                   waits=[(AC, ax[AC])])
            if apply_eng == "scalar":
                a3 = E("scalar",
                       lambda e, s=src_ap, o=out_ap, sc=cR, b=cBn: e.activation(out=o, in_=s, func=AF.Identity, bias=b, scale=sc),
                       waits=tuple(waits_out) + ((VS, r4[VS]),), incs=[(AC, 1)])
                return (AC, a3[AC]), rsq[VS], rsq[VS]
            a3 = E("vector",
                   lambda e, s=src_ap, o=out_ap, sc=cR, b=cBn: e.tensor_scalar(
                       out=o, in0=s, scalar1=sc, scalar2=b, op0=ALU.mult, op1=ALU.add),
                   waits=waits_out, incs=[(DV, 1)])
            return (DV, a3[DV]), rsq[VS], a3[DV]

        for dst, src in ((ident[:], ident_b_h), (identf[:], ident_f_h), (masks[:], masks_h)):
            dma("sync", CONST, lambda e, d=dst, s=src: e.dma_start(out=d, in_=s))
        ld_consts = CONST.n

        for fn in (lambda e: e.memset(onesrow[:], 1.0),
                   lambda e: e.memset(onesrowf[:], 1.0),
                   lambda e: e.memset(onescol[:], 1.0),
                   lambda e: e.memset(zrow[:], 0.0)):
            E("gpsimd", fn)
        gp_memset = E("gpsimd", lambda e: e.memset(epscol[:], 1e-5), incs=[(GP, 1)])[GP]

        pe_rowstage = 0
        dv_rowstage = 0

        def prep_colbias(src_h, dest, scale):
            nonlocal pe_rowstage, dv_rowstage
            dv_lbp_prev = DV.n
            rsv = dma("sync", RS, lambda e, s=src_h: e.dma_start(out=rowstage[:], in_=s),
                      waits=[(PE, pe_rowstage), (DV, dv_rowstage)])
            for c in range(8):
                w = [(RS, rsv), (GP, gp_memset), (DV, dv_lbp_prev)] if c == 0 else []
                E("tensor",
                  lambda e, c=c: e.matmul(out=LBp[:, c:c + 1],
                                          lhsT=rowstage[0:1, c * 128:(c + 1) * 128],
                                          rhs=onesrowf[0:1, 0:1], start=True, stop=True),
                  waits=w, incs=[(PE, 1)] if c == 7 else ())
            pe_rowstage = PE.n
            r = E("vector", lambda e, d=dest, sc=scale: e.tensor_scalar_mul(out=d, in0=LBp[:, 0:8], scalar1=sc),
                  waits=[(PE, pe_rowstage)], incs=[(DV, 1)])
            return r[DV]

        prep_colbias(b_h["bq"], bq32[:], INVSQ)
        prep_colbias(b_h["bk"], bkc[:], 1.0)

        def prep_rowbias(src_h, dest):
            nonlocal pe_rowstage, dv_rowstage
            rsv = dma("sync", RS, lambda e, s=src_h: e.dma_start(out=rowstage[:], in_=s),
                      waits=[(PE, pe_rowstage), (DV, dv_rowstage)])
            r = E("vector", lambda e, d=dest: e.tensor_copy(out=d, in_=rowstage[:]),
                  waits=[(RS, rsv)], incs=[(DV, 1)])
            dv_rowstage = r[DV]

        prep_rowbias(b_h["bv"], bvrow[:])
        prep_rowbias(b_h["fbias"], fbrow[:])

        dv_lbp_prev = DV.n
        rsv = dma("sync", RS, lambda e: e.dma_start(out=rowstage[:], in_=b_h["bo"]),
                  waits=[(PE, pe_rowstage), (DV, dv_rowstage)])
        for hh in range(2):
            E("tensor", lambda e, hh=hh: e.matmul(
                out=LBp[:, 0:512], lhsT=onesrowf[0:1, 0:128],
                rhs=rowstage[0:1, hh * 512:(hh + 1) * 512], start=True, stop=True),
                waits=[(RS, rsv), (DV, dv_lbp_prev)] if hh == 0 else [(DV, dv_lbp_prev)],
                incs=[(PE, 1)])
            dv_lbp_prev = E("vector", lambda e, hh=hh: e.tensor_copy(
                out=bo_tile[:, hh * 512:(hh + 1) * 512], in_=LBp[:, 0:512]),
                waits=[(PE, PE.n)], incs=[(DV, 1)])[DV]
        pe_rowstage = PE.n
        dv_rowstage = dv_lbp_prev

        for dc in range(8):
            dma("sync", XS[0], lambda e, dc=dc: e.dma_start(
                out=xb[:, dc * 16:(dc + 1) * 16], in_=fa_h[dc * 128:(dc + 1) * 128, :]))
        dv_a = E("vector", lambda e: e.tensor_copy(out=a_bf[:], in_=xb[:, 0:128]),
                 waits=[(XS[0], XS[0].n)], incs=[(DV, 1)])[DV]
        dma("sync", XS[1], lambda e: e.dma_start(out=xb[0:16, 1024:2048], in_=fb_h))
        dv_b = E("vector", lambda e: e.tensor_copy(out=b_bf[:], in_=xb[0:16, 1024:2048]),
                 waits=[(XS[1], XS[1].n)], incs=[(DV, 1)])[DV]

        dv_absmax = {}
        for c in range(8):
            sl = c % 4
            w = [(DV, max(dv_a, dv_b))] if c < 4 else [(DV, dv_absmax[c - 4])]
            xv = dma("sync", XS[sl], lambda e, c=c, sl=sl: e.dma_start(
                out=xb[:, sl * 1024:(sl + 1) * 1024],
                in_=w_h["fk"][c * 128:(c + 1) * 128, :]), waits=w)
            r = E("vector", lambda e, c=c, sl=sl: e.tensor_reduce(
                out=gq[:, c:c + 1], in_=xb[:, sl * 1024:(sl + 1) * 1024],
                axis=AX.X, op=ALU.max, apply_absolute_value=True),
                waits=[(XS[sl], xv)], incs=[(DV, 1)])
            dv_absmax[c] = r[DV]
        dv_g = E("vector", lambda e: e.tensor_reduce(
            out=gq[:, 8:9], in_=gq[:, 0:8], axis=AX.X, op=ALU.max),
            incs=[(DV, 1)])[DV]
        pe_gt = E("tensor", lambda e: e.transpose(out=LBp[0:1, 0:128], in_=gq[:, 8:9], identity=identf[:]),
                  waits=[(DV, dv_g), (CONST, ld_consts)], incs=[(PE, 1)])[PE]
        E("vector", lambda e: e.tensor_reduce(out=gq[0:1, 9:10], in_=LBp[0:1, 0:128], axis=AX.X, op=ALU.max),
          waits=[(PE, pe_gt)])
        E("vector", lambda e: e.reciprocal(out=gq[0:1, 10:11], in_=gq[0:1, 9:10]))
        E("vector", lambda e: e.tensor_scalar_mul(out=gq[0:1, 11:12], in0=gq[0:1, 10:11], scalar1=7.0))
        dv_qs = E("vector", lambda e: e.tensor_scalar_mul(out=gq[0:1, 12:13], in0=gq[0:1, 9:10], scalar1=1.0 / 7.0),
                  incs=[(DV, 1)])[DV]
        pe_bc = None
        for (srcc, dstc) in ((11, 13), (12, 14)):
            pe_bc = E("tensor", lambda e, s=srcc, d=dstc: e.matmul(
                out=LBp[:, 200 + d:201 + d], lhsT=onesrowf[0:1, 0:128],
                rhs=gq[0:1, s:s + 1], start=True, stop=True),
                waits=[(DV, dv_qs)], incs=[(PE, 1)])[PE]
        dv_bcast = E("vector", lambda e: e.tensor_copy(out=gq[:, 13:15], in_=LBp[:, 213:215]),
                     waits=[(PE, pe_bc)], incs=[(DV, 1)])[DV]

        dv_cast_slot = {i: 0 for i in range(4)}
        pe_stage_slot = {i: 0 for i in range(4)}
        dv_trcopy = 0
        wt_wo_store = 0

        def prep_wT(wname, region, to_scratch):
            nonlocal dv_trcopy
            dv_last = 0
            for h in range(2):
                xv = {}
                for fl in range(4):
                    xv[fl] = dma("sync", XS[fl], lambda e, w=wname, h=h, fl=fl: e.dma_start(
                        out=xb[:, fl * 1024:(fl + 1) * 1024],
                        in_=w_h[w][h * 512 + fl * 128:h * 512 + (fl + 1) * 128, :]),
                        waits=[(DV, dv_bcast), (DV, dv_cast_slot[fl])])
                for fl in range(4):
                    r = E("vector", lambda e, fl=fl: e.tensor_copy(
                        out=Qs[:, fl * 1024:(fl + 1) * 1024], in_=xb[:, fl * 1024:(fl + 1) * 1024]),
                        waits=[(XS[fl], xv[fl]), (PE, pe_stage_slot[fl])], incs=[(DV, 1)])
                    dv_cast_slot[fl] = r[DV]
                for fl in range(4):
                    for dc in range(8):
                        w = [(DV, dv_cast_slot[fl])]
                        if dc == 0:
                            w += [(DV, dv_trcopy), (CONST, ld_consts)]
                        E("tensor", lambda e, fl=fl, dc=dc: e.transpose(
                            out=TRs[:, dc * 128:(dc + 1) * 128],
                            in_=Qs[:, fl * 1024 + dc * 128:fl * 1024 + (dc + 1) * 128],
                            identity=ident[:]),
                            waits=w, incs=[(PE, 1)] if dc == 7 else ())
                    pe_stage_slot[fl] = PE.n
                    pe_last = PE.n
                    dst = wsl.ap().rearrange("p (r dcc k) -> p r dcc k", r=3, dcc=8)[
                        :, region, :, h * 512 + fl * 128:h * 512 + (fl + 1) * 128]
                    src3 = TRs.ap().rearrange("p (dcc t) -> p dcc t", dcc=8)
                    w2 = [(PE, pe_last)]
                    if region == 0 and wname != "wo":
                        w2.append((WT, wt_wo_store))
                    r = E("vector", lambda e, d=dst, s3=src3: e.tensor_copy(out=d, in_=s3),
                          waits=w2, incs=[(DV, 1)])
                    dv_trcopy = r[DV]
                    dv_last = r[DV]
            return dma("gpsimd", WT, lambda e, s=to_scratch, rg=region: e.dma_start(
                out=s, in_=wsl[:, rg * 8192:(rg + 1) * 8192]), waits=[(DV, dv_last)])

        wt_wo_store = prep_wT("wo", 0, wT_scr["wo"])
        prep_wT("wq", 0, wT_scr["wq"])
        prep_wT("wk", 1, wT_scr["wk"])
        prep_wT("wv", 2, wT_scr["wv"])

        trs_guard.__exit__(None, None, None)

        barrier()

        BW = {"wl": 0}

        def phase_a(seq, reload_weights, TRA, do_quant=False):
            wl_val = 0
            if reload_weights:
                for rg, n in ((0, "wq"), (1, "wk"), (2, "wv")):
                    wl_val = dma("gpsimd", WR, lambda e, rg=rg, n=n: e.dma_start(
                        out=wsl[:, rg * 8192:(rg + 1) * 8192], in_=wT_scr[n]))

            ac_x_free = {}
            pe_lnb = {0: 0, 1: 0}
            dv_sqb = {0: 0, 1: 0}
            st_a = {"ac_lnTc": 0}
            ac_sbbank = {0: 0, 1: 0}
            dv_obank = {0: (AC, 0), 1: (AC, 0)}
            qg_val = 0
            kg_val = 0
            chains = {}

            def emit_chain(g, sub):
                t = 4 * g + sub
                sl4 = t % 4
                sl2 = t % 2
                xs_v = dma("sync", XS[sl4], lambda e, seq=seq, t=t, sl4=sl4: e.dma_start(
                    out=xb[:, sl4 * 1024:(sl4 + 1) * 1024],
                    in_=x_in[seq, t * 128:(t + 1) * 128, :]),
                    waits=[ac_x_free.get(sl4, (AC, 0))])
                aca, dvs, _ = emit_ln(
                    xb[:, sl4 * 1024:(sl4 + 1) * 1024],
                    lnb[:, sl2 * 1024:(sl2 + 1) * 1024],
                    sq_slot=sl2,
                    waits_src=[(XS[sl4], xs_v)],
                    waits_sq=[(DV, dv_sqb[sl2])],
                    waits_out=[(PE, pe_lnb[sl2])])
                ac_x_free[sl4] = aca
                chains.setdefault(g, {})[sub] = aca
                dv_sqb[sl2] = dvs

            def emit_T(g, sub):
                t = 4 * g + sub
                sl2 = t % 2
                aca = chains[g][sub]
                for dc in range(8):
                    w = [aca]
                    if dc == 0:
                        w.append((AC, st_a["ac_lnTc"]))
                    E("tensor", lambda e, dc=dc, sl2=sl2, TRA=TRA: e.transpose(
                        out=TRA[:, dc * 128:(dc + 1) * 128],
                        in_=lnb[:, sl2 * 1024 + dc * 128:sl2 * 1024 + (dc + 1) * 128],
                        identity=ident[:]),
                        waits=w, incs=[(PE, 1)] if dc == 7 else ())
                pe_lnb[sl2] = PE.n
                pe_tr = PE.n
                dst = lnT.ap().rearrange("p (dcc t) -> p dcc t", dcc=8)[:, :, sub * 128:(sub + 1) * 128]
                src3 = TRA.ap().rearrange("p (dcc t) -> p dcc t", dcc=8)
                r = E("scalar", lambda e, d=dst, s3=src3: e.activation(out=d, in_=s3, func=AF.Identity, scale=1.0),
                      waits=[(PE, pe_tr)], incs=[(AC, 1)])
                st_a["ac_lnTc"] = r[AC]

            q_t1_free = {0: 0, 1: 0}
            q_ot_free = {0: 0, 1: 0}
            fkv = {0: 0, 1: 0}

            def quant_chunk(c):
                sl = c % 2
                fkv[sl] = dma("sync", FK[sl], lambda e, c=c, sl=sl: e.dma_start(
                    out=t1[:, sl * 1024:(sl + 1) * 1024],
                    in_=w_h["fk"][c * 128:(c + 1) * 128, :]),
                    waits=[(DV, q_t1_free[sl])])
                xsl = t1[:, sl * 1024:(sl + 1) * 1024]
                ssl = x2[:]
                tsl = t2[:, 0:1024]
                osl = OT_sb[:, sl * 1024:(sl + 1) * 1024]
                r = E("vector", lambda e, x_=xsl, s=ssl: e.tensor_scalar_mul(
                    out=s, in0=x_, scalar1=gq[:, 13:14]),
                    waits=[(FK[sl], fkv[sl])])
                q_t1_free[sl] = r[DV]
                E("vector", lambda e, s=ssl, t=tsl: e.tensor_scalar(
                    out=t, in0=s, scalar1=-6.5, scalar2=-7.0, op0=ALU.is_ge, op1=ALU.add))
                for kk_ in range(-6, 7):
                    E("vector", lambda e, s=ssl, t=tsl, c_=kk_ + 0.5: e.scalar_tensor_tensor(
                        out=t, in0=s, scalar=c_, in1=t, op0=ALU.is_ge, op1=ALU.add))
                dvq = E("vector", lambda e, t=tsl, o=osl: e.tensor_scalar_mul(
                    out=o, in0=t, scalar1=gq[:, 14:15]),
                    waits=[(FS[sl], q_ot_free[sl])], incs=[(DV, 1)])[DV]
                q_ot_free[sl] = dma("gpsimd", FS[sl], lambda e, c=c, o=osl: e.dma_start(
                    out=wfq_h[:, c * 1024:(c + 1) * 1024], in_=o), waits=[(DV, dvq)])

            emit_chain(0, 0)
            emit_chain(0, 1)
            for sub in range(4):
                emit_T(0, sub)
                if sub < 2:
                    emit_chain(0, sub + 2)

            for g in range(NGROUP):
                if g + 1 < NGROUP:
                    emit_chain(g + 1, 0)
                    emit_chain(g + 1, 1)
                if do_quant and g > 0:
                    quant_chunk(g - 1)

                def proj_fm(wreg, biascol, scale, evac_dst_fn, first_waits, src_shape_fn=None):
                    nonlocal ac_sbbank
                    if src_shape_fn is None:
                        src_shape_fn = lambda bk_: SBp[:, bk_ * 512:bk_ * 512 + 512]
                    last_ac = 0
                    for kc in range(8):
                        bk_ = kc % 2
                        w = [(AC, st_a["ac_lnTc"]), (AC, ac_sbbank[bk_])]
                        if kc == 0:
                            w += list(first_waits)
                            if wl_val:
                                w.append((WR, wl_val))
                        for dc in range(8):
                            E("tensor", lambda e, wreg=wreg, kc=kc, dc=dc, bk_=bk_: e.matmul(
                                out=SBp[:, bk_ * 512:bk_ * 512 + 512],
                                lhsT=wsl[:, wreg * 8192 + dc * 1024 + kc * 128: wreg * 8192 + dc * 1024 + (kc + 1) * 128],
                                rhs=lnT[:, dc * 512:(dc + 1) * 512],
                                start=(dc == 0), stop=(dc == 7)),
                                waits=w if dc == 0 else (), incs=[(PE, 1)] if dc == 7 else ())
                        pe_mm = PE.n
                        r = E("scalar", lambda e, kc=kc, bk_=bk_, bc=biascol, sc=scale, df=evac_dst_fn, sf=src_shape_fn: e.activation(
                            out=df(kc), in_=sf(bk_),
                            func=AF.Identity, bias=bc[:, kc:kc + 1], scale=sc),
                            waits=[(PE, pe_mm)], incs=[(AC, 1)])
                        ac_sbbank[bk_] = r[AC]
                        last_ac = r[AC]
                    return last_ac

                ac_q = proj_fm(0, bq32, INVSQ,
                               lambda kc: Qs[:, kc * 512:(kc + 1) * 512],
                               [(QG, qg_val)])
                for ss in range(2):
                    src = Qs.ap().rearrange("p (dcc t) -> p dcc t", dcc=8)[:, :, ss * 256:(ss + 1) * 256]
                    qg_val = dma("gpsimd", QG, lambda e, seq=seq, g=g, ss=ss, s=src: e.dma_start(
                        out=q_scr[seq, 2 * g + ss], in_=s), waits=[(AC, ac_q)])

                ktr_view = Ktr.ap().rearrange("p (tl ck) -> p tl ck", tl=4)
                sb3 = SBp.ap().rearrange("p (bb tl t) -> p bb tl t", bb=2, tl=4)
                ac_k = proj_fm(1, bkc, 1.0,
                               lambda kc: ktr_view[:, :, kc * 128:(kc + 1) * 128],
                               [(KG, kg_val)],
                               src_shape_fn=lambda bk_: sb3[:, bk_, :, :])
                for tl in range(4):
                    kg_val = dma("gpsimd", KG, lambda e, seq=seq, g=g, tl=tl: e.dma_start(
                        out=k_scr[seq, 4 * g + tl], in_=Ktr[:, tl * 1024:(tl + 1) * 1024]),
                        waits=[(AC, ac_k)])

                for sub in range(4):
                    t = 4 * g + sub
                    pidx = sub % 2
                    for h in range(2):
                        ps = OTp[pidx * 2 + h]
                        w = [dv_obank[pidx], chains[g][sub]]
                        for dc in range(8):
                            E("tensor", lambda e, sub=sub, h=h, dc=dc, ps=ps: e.matmul(
                                out=ps[:],
                                lhsT=lnT[:, dc * 512 + sub * 128:dc * 512 + (sub + 1) * 128],
                                rhs=wsl[:, 2 * 8192 + dc * 1024 + h * 512: 2 * 8192 + dc * 1024 + (h + 1) * 512],
                                start=(dc == 0), stop=False),
                                waits=w if dc == 0 else ())
                        E("tensor", lambda e, h=h, ps=ps: e.matmul(
                            out=ps[:],
                            lhsT=onesrow[0:1, 0:128],
                            rhs=bvrow[0:1, h * 512:(h + 1) * 512],
                            start=False, stop=True), incs=[(PE, 1)])
                    pe_v = PE.n
                    r = None
                    for h in range(2):
                        r = E("scalar", lambda e, t=t, h=h, ps=OTp[pidx * 2 + h]: e.activation(
                            out=V_buf[:, t * 1024 + h * 512:t * 1024 + (h + 1) * 512],
                            in_=ps[:], func=AF.Identity, scale=1.0),
                            waits=[(PE, pe_v)], incs=[(AC, 1)] if h == 1 else ())
                    dv_obank[pidx] = (AC, r[AC])
                    if g + 1 < NGROUP:
                        emit_T(g + 1, sub)
                        if sub < 2:
                            emit_chain(g + 1, sub + 2)
                if do_quant and g == NGROUP - 1:
                    quant_chunk(g)
                if g == NGROUP - 1:
                    BW["wl"] = dma("sync", WL, lambda e: e.dma_start(
                        out=wsl[:, 0:8192], in_=wT_scr["wo"]), waits=[(PE, PE.n)])
                    BW["wl"] = dma("sync", WL, lambda e: e.dma_start(
                        out=wsl[:, 8192:16384], in_=wfq_h),
                        waits=[(PE, PE.n), (FS[0], q_ot_free[0]), (FS[1], q_ot_free[1])])

        def phase_b(seq, EB):
            x2s = {0: x2, 1: t2}
            EBs = (EB, LBp)
            wl_val = BW["wl"]

            st = {
                "ys": {i: 0 for i in range(4)},
                "pe_slot": {i: 0 for i in range(4)},
                "pe_flashend": {},
                "pe_pvlast": {},
                "ac_otevac": 0,
                "ac_otevac_b": {i: 0 for i in range(4)},
                "dv_rl": {},
                "ac_rl_user": 0,
                "ac_ebh": {0: 0, 1: 0},
                "dv_ebh": {0: 0, 1: 0},
                "eb_flip": 0,
                "dv_lread": 0,
                "pe_eb_reader": 0,
                "dv_sqb": {0: (DV, 0), 1: (DV, 0)},
                "pe_ln2T": {0: 0, 1: 0},
                "pe_ffn": {0: 0, 1: 0},
                "dv_xat": {0: 0, 1: 0},
                "dv_t1": {0: 0, 1: 0},
                "qs": {0: 0, 1: 0},
                "xs": {},
                "ks": {},
                "ot_sb_reader": 0,
                "ac_abs_bank": {0: 0, 1: 0},
                "pe_pv_slot": {0: 0, 1: 0},
            }
            pend = []

            def make_epilogue(s):
                pre = []
                main = []
                spill = []

                def t_rl():
                    v3 = SBp.ap().rearrange("p (b c) -> p b c", b=2)
                    E("vector", lambda e, v3=v3: e.reduce_sum(
                        out=gq[:, 0:1], in_=v3[:, :, 256:256 + s + 1], axis=AX.XY),
                        waits=[(PE, st["pe_pvlast"][s]), (AC, st["ac_rl_user"])])
                    E("vector", lambda e, v3=v3: e.reduce_sum(
                        out=gq[:, 1:2], in_=v3[:, :, 272:272 + s + 1], axis=AX.XY))
                    r = E("vector", lambda e: e.reciprocal(out=rl[:], in_=gq[:, 0:2]),
                          incs=[(DV, 1)])
                    st["dv_rl"][s] = r[DV]
                    st["dv_lread"] = r[DV]
                pre.append(t_rl)

                def t_otevac(b):
                    def f():
                        r = E("scalar", lambda e, b=b: e.activation(
                            out=OT_sb[:, b * 512:(b + 1) * 512], in_=OTp[b][:],
                            func=AF.Identity, scale=1.0),
                            waits=[(PE, st["pe_pvlast"][s]), (PE, st["ot_sb_reader"])],
                            incs=[(AC, 1)])
                        st["ac_otevac"] = r[AC]
                        st["ac_otevac_b"][b] = r[AC]
                    return f
                for b_ in range(4):
                    pre.append(t_otevac(b_))

                boxes = {0: {}, 1: {}}

                def outproj_h(qc, h):
                    def f():
                        box = boxes[qc]
                        bk = st["eb_flip"]
                        st["eb_flip"] ^= 1
                        dst = EBs[bk]
                        w = [(AC, st["ac_otevac"]), (AC, st["ac_ebh"][bk]),
                             (DV, st["dv_ebh"][bk]), (WL, wl_val)]
                        for vc in range(8):
                            E("tensor", lambda e, h=h, vc=vc, qc=qc, dst=dst: e.matmul(
                                out=dst[:],
                                lhsT=OT_sb[:, vc * 256 + qc * 128:vc * 256 + (qc + 1) * 128],
                                rhs=wsl[:, vc * 1024 + h * 512:vc * 1024 + (h + 1) * 512],
                                start=(vc == 0), stop=(vc == 7)),
                                waits=w if vc == 0 else (), incs=[(PE, 1)] if vc == 7 else ())
                        st["ot_sb_reader"] = PE.n
                        pe_mm = PE.n
                        r = E("scalar", lambda e, h=h, qc=qc, dst=dst: e.activation(
                            out=t1[:, qc * 1024 + h * 512:qc * 1024 + (h + 1) * 512], in_=dst[:],
                            func=AF.Identity, scale=rl[:, qc:qc + 1]),
                            waits=[(PE, pe_mm), (DV, st["dv_rl"][s]), (DV, st["dv_t1"][qc])],
                            incs=[(AC, 1)])
                        st["ac_ebh"][bk] = r[AC]
                        st["ac_rl_user"] = r[AC]
                        box[f"ev{h}"] = r[AC]
                    return f

                def lnattn_a(qc):
                    def f():
                        box = boxes[qc]
                        t1s = t1[:, qc * 1024:(qc + 1) * 1024]
                        dvb = E("vector", lambda e, t1s=t1s: e.tensor_add(out=t1s, in0=t1s, in1=bo_tile[:]),
                                waits=[(AC, box["ev1"])], incs=[(DV, 1)])[DV]
                        aca, dvs, dv1 = emit_ln(t1s, sqb[:, qc * 1024:(qc + 1) * 1024], sq_slot=qc,
                                                waits_src=[(DV, dvb)],
                                                waits_sq=[st["dv_sqb"][qc],
                                                          (PE, st["pe_ln2T"][qc])],
                                                apply_eng="vector")
                        box["dv_ln1"] = dv1
                        box["ac_ln1"] = aca
                    return f

                def lnattn_b(qc):
                    def f():
                        box = boxes[qc]
                        t = 2 * s + qc
                        sl4 = t % 4
                        dv_x2 = E("vector", lambda e, sl4=sl4, qc=qc: e.tensor_add(
                            out=x2s[qc][:], in0=xb[:, sl4 * 1024:(sl4 + 1) * 1024],
                            in1=sqb[:, qc * 1024:(qc + 1) * 1024]),
                            waits=[box["ac_ln1"], (XS[sl4], st["xs"][t])], incs=[(DV, 1)])[DV]
                        ac2, dvs2, _ = emit_ln(x2s[qc][:], sqb[:, qc * 1024:(qc + 1) * 1024], sq_slot=qc,
                                               waits_src=[(DV, dv_x2)],
                                               waits_sq=[(DV, dv_x2)],
                                               waits_out=[(PE, st["pe_ln2T"][qc])],
                                               apply_eng="vector")
                        box["ac_ln2"] = ac2
                        st["dv_sqb"][qc] = (DV, dvs2)
                    return f

                def ln2t_pe(qc):
                    def f():
                        box = boxes[qc]
                        last = 0
                        for rnd in range(2):
                            bk = st["eb_flip"]
                            st["eb_flip"] ^= 1
                            dst = EBs[bk]
                            wfirst = [box["ac_ln2"], (AC, st["ac_ebh"][bk]),
                                      (DV, st["dv_ebh"][bk])]
                            for c in range(4):
                                dc = 4 * rnd + c
                                E("tensor", lambda e, qc=qc, dc=dc, c=c, dst=dst: e.transpose(
                                    out=dst[:, c * 128:(c + 1) * 128],
                                    in_=sqb[:, qc * 1024 + dc * 128:qc * 1024 + (dc + 1) * 128],
                                    identity=identf[:]),
                                    waits=wfirst if c == 0 else (), incs=[(PE, 1)] if c == 3 else ())
                            pe_t = PE.n
                            r2 = E("scalar", lambda e, qc=qc, rnd=rnd, dst=dst: e.activation(
                                out=lnT[:, qc * 1024 + rnd * 512:qc * 1024 + (rnd + 1) * 512],
                                in_=dst[:, 0:512], func=AF.Identity, scale=1.0),
                                waits=[(PE, pe_t)], incs=[(AC, 1)])
                            st["ac_ebh"][bk] = r2[AC]
                            last = r2[AC]
                        st["pe_ln2T"][qc] = PE.n
                        box["ac_lnT"] = last
                    return f

                def xa_stage(qc):
                    def f():
                        box = boxes[qc]
                        bk = st["eb_flip"]
                        st["eb_flip"] ^= 1
                        dst = EBs[bk]
                        w = [(AC, box["ac_lnT"]), (AC, st["ac_ebh"][bk]),
                             (DV, st["dv_ebh"][bk]), (DV, st["dv_rl"][s])]
                        for dc in range(8):
                            E("tensor", lambda e, dc=dc, qc=qc, dst=dst: e.matmul(
                                out=dst[0:16, 0:128],
                                lhsT=a_bf[:, dc * 16:(dc + 1) * 16],
                                rhs=lnT[:, qc * 1024 + dc * 128:qc * 1024 + (dc + 1) * 128],
                                start=(dc == 0), stop=(dc == 7)),
                                waits=w if dc == 0 else (), incs=[(PE, 1)] if dc == 7 else ())
                        pe_xa = PE.n
                        r = E("vector", lambda e, qc=qc, dst=dst: e.tensor_copy(
                            out=xaT[:, qc * 128:(qc + 1) * 128],
                            in_=dst[0:16, 0:128]),
                            waits=[(PE, pe_xa)], incs=[(DV, 1)])
                        st["dv_xat"][qc] = r[DV]
                        st["dv_ebh"][bk] = r[DV]
                    return f

                def ffn_h(qc, h):
                    def f():
                        box = boxes[qc]
                        bk = st["eb_flip"]
                        st["eb_flip"] ^= 1
                        dst = EBs[bk]
                        w = [(DV, st["dv_xat"][qc]), (AC, st["ac_ebh"][bk]),
                             (DV, st["dv_ebh"][bk])]
                        for dc in range(8):
                            E("tensor", lambda e, h=h, dc=dc, qc=qc, dst=dst: e.matmul(
                                out=dst[:],
                                lhsT=lnT[:, qc * 1024 + dc * 128:qc * 1024 + (dc + 1) * 128],
                                rhs=wsl[:, 8192 + dc * 1024 + h * 512:8192 + dc * 1024 + (h + 1) * 512],
                                start=(dc == 0), stop=False),
                                waits=w if dc == 0 else ())
                        E("tensor", lambda e, h=h, dst=dst: e.matmul(
                            out=dst[:], lhsT=onesrow[0:1, 0:128], rhs=fbrow[0:1, h * 512:(h + 1) * 512],
                            start=False, stop=False))
                        E("tensor", lambda e, h=h, qc=qc, dst=dst: e.matmul(
                            out=dst[:], lhsT=xaT[0:16, qc * 128:(qc + 1) * 128],
                            rhs=b_bf[0:16, h * 512:(h + 1) * 512],
                            start=False, stop=True), incs=[(PE, 1)])
                        st["pe_ffn"][qc] = PE.n
                        pe_mm = PE.n
                        r = E("scalar", lambda e, h=h, qc=qc, dst=dst: e.activation(
                            out=t1[:, qc * 1024 + h * 512:qc * 1024 + (h + 1) * 512], in_=dst[:],
                            func=AF.Identity, scale=1.0),
                            waits=[(PE, pe_mm), (DV, box["dv_ln1"])], incs=[(AC, 1)])
                        st["ac_ebh"][bk] = r[AC]
                        box[f"fev{h}"] = r[AC]
                    return f

                def lnffn_chain(qc):
                    def f():
                        box = boxes[qc]
                        t = 2 * s + qc
                        sl4 = t % 4
                        acf, dvs3, dv1f = emit_ln(t1[:, qc * 1024:(qc + 1) * 1024],
                                                  sqb[:, qc * 1024:(qc + 1) * 1024], sq_slot=qc,
                                                  waits_src=[(AC, box["fev1"])],
                                                  waits_sq=[st["dv_sqb"][qc],
                                                            (PE, st["pe_ln2T"][qc])],
                                                  apply_eng="vector")
                        st["dv_t1"][qc] = dv1f
                        dv_fin = E("vector", lambda e, sl4=sl4, qc=qc: e.tensor_add(
                            out=xb[:, sl4 * 1024:(sl4 + 1) * 1024],
                            in0=x2s[qc][:], in1=sqb[:, qc * 1024:(qc + 1) * 1024]),
                            waits=[acf], incs=[(DV, 1)])[DV]
                        st["dv_sqb"][qc] = (DV, dv_fin)
                        box["dv_fin"] = dv_fin
                    return f

                def store_stage(qc):
                    def f():
                        box = boxes[qc]
                        t = 2 * s + qc
                        sl4 = t % 4
                        st["ys"][sl4] = dma("gpsimd", YS[sl4], lambda e, seq=seq, t=t, sl4=sl4: e.dma_start(
                            out=y_out[seq, t * 128:(t + 1) * 128, :],
                            in_=xb[:, sl4 * 1024:(sl4 + 1) * 1024]),
                            waits=[(DV, box["dv_fin"])])
                        if t + 4 < 2 * NSTRIP:
                            st["xs"][t + 4] = dma("sync", XS[sl4], lambda e, seq=seq, t=t + 4, sl4=sl4: e.dma_start(
                                out=xb[:, sl4 * 1024:(sl4 + 1) * 1024],
                                in_=x_in[seq, t * 128:(t + 1) * 128, :]),
                                waits=[(YS[sl4], st["ys"][sl4])])
                    return f

                main.append((0.20, outproj_h(0, 0)))
                main.append((0.24, outproj_h(0, 1)))
                main.append((0.36, lnattn_a(0)))
                main.append((0.44, lnattn_b(0)))
                main.append((0.50, outproj_h(1, 0)))
                main.append((0.54, outproj_h(1, 1)))
                main.append((0.62, lnattn_a(1)))
                main.append((0.68, ln2t_pe(0)))
                main.append((0.72, xa_stage(0)))
                main.append((0.75, ffn_h(0, 0)))
                main.append((0.79, ffn_h(0, 1)))
                main.append((0.80, lnattn_b(1)))
                main.append((0.84, lnffn_chain(0)))
                spill.append((0.04, ln2t_pe(1)))
                spill.append((0.07, xa_stage(1)))
                if st.get("store1_thunk") is not None:
                    spill.append((0.07, st["store1_thunk"]))
                spill.append((0.10, ffn_h(1, 0)))
                spill.append((0.14, ffn_h(1, 1)))
                spill.append((0.18, lnffn_chain(1)))
                spill.append((0.28, store_stage(0)))
                st["store1_thunk"] = store_stage(1)
                return pre, main, spill

            epi_pre = []
            pend = []
            spill_next = []

            for s in range(NSTRIP):
                st["qs"][s % 2] = dma("sync", QS[s % 2], lambda e, seq=seq, s=s: e.dma_start(
                    out=Qs[:, (s % 2) * 2048:(s % 2 + 1) * 2048], in_=q_scr[seq, s]),
                    waits=[(PE, st["pe_flashend"].get(s - 2, 0))])
                if s < 2:
                    for qc in range(2):
                        t = 2 * s + qc
                        st["xs"][t] = dma("sync", XS[t % 4], lambda e, seq=seq, t=t: e.dma_start(
                            out=xb[:, (t % 4) * 1024:(t % 4 + 1) * 1024],
                            in_=x_in[seq, t * 128:(t + 1) * 128, :]))
                nt = 2 * s + 2

                for f in epi_pre:
                    f()
                epi_pre = []

                ac_abs = {}
                ac_exp = {}
                pe_pv = {}
                qof = (s % 2) * 2048
                n_pend = len(pend)
                popped = 0

                def emit_S(pos, tile):
                    sp = pos % 2
                    w = [(KS[pos % 4], st["ks"][pos]),
                         (AC, ac_abs.get(pos - 2, st["ac_abs_bank"][sp]))]
                    if pos == 0:
                        w += [(QS[s % 2], st["qs"][s % 2])]
                    if pos < 2:
                        w.append((DV, st["dv_lread"]))
                    for dc in range(8):
                        E("tensor", lambda e, pos=pos, dc=dc, sp=sp, qof=qof: e.matmul(
                            out=SBp[:, sp * 512:sp * 512 + 256],
                            lhsT=Ktr[:, (pos % 4) * 1024 + dc * 128:(pos % 4) * 1024 + (dc + 1) * 128],
                            rhs=Qs[:, qof + dc * 256:qof + (dc + 1) * 256],
                            start=(dc == 0), stop=(dc == 7)),
                            waits=w if dc == 0 else (), incs=[(PE, 1)] if dc == 7 else ())
                    st["pe_slot"][pos % 4] = PE.n
                    pe_sT = PE.n
                    r = E("scalar", lambda e, sp=sp: e.activation(
                        out=Sa[:, sp * 256:(sp + 1) * 256], in_=SBp[:, sp * 512:sp * 512 + 256],
                        func=AF.Abs, scale=1.0), waits=[(PE, pe_sT)], incs=[(AC, 1)])
                    ac_abs[pos] = r[AC]
                    st["ac_abs_bank"][sp] = r[AC]
                    wexp = [(AC, ac_abs[pos])]
                    if tile >= 2 * s:
                        r = E("vector", lambda e, sp=sp, ms=(tile - 2 * s): e.tensor_tensor(
                            out=Sa[:, sp * 256:(sp + 1) * 256], in0=Sa[:, sp * 256:(sp + 1) * 256],
                            in1=masks[:, ms * 256:(ms + 1) * 256], op=ALU.add),
                            waits=[(AC, ac_abs[pos]), (CONST, ld_consts)], incs=[(DV, 1)])
                        wexp.append((DV, r[DV]))
                    wexp.append((PE, pe_pv.get(pos - 2, st["pe_pv_slot"][sp])))
                    r = E("scalar", lambda e, sp=sp: e.activation(
                        out=PT[:, sp * 256:(sp + 1) * 256], in_=Sa[:, sp * 256:(sp + 1) * 256],
                        func=AF.Exp, scale=1.0),
                        waits=wexp, incs=[(AC, 1)])
                    ac_exp[pos] = r[AC]

                def emit_PV(pos, tile):
                    sp = pos % 2
                    last = (pos == nt - 1)
                    frst = (pos == 0)
                    w = [(AC, ac_exp[pos])]
                    first = True
                    for vc in range(8):
                        b_ = vc // 2
                        col = (vc % 2) * 256
                        wv = list(w) if first else []
                        if frst and vc % 2 == 0:
                            wv.append((AC, st["ac_otevac_b"][b_]))
                        E("tensor", lambda e, tile=tile, sp=sp, vc=vc, b_=b_, col=col,
                          stt=(frst and vc % 2 == 0),
                          lst=(last and vc % 2 == 1): e.matmul(
                            out=OTp[b_][:, col:col + 256],
                            lhsT=V_buf[:, tile * 1024 + vc * 128:tile * 1024 + (vc + 1) * 128],
                            rhs=PT[:, sp * 256:(sp + 1) * 256],
                            start=stt, stop=lst),
                            waits=wv)
                        first = False
                    for qc in range(2):
                        col = sp * 512 + 256 + qc * 16 + pos // 2
                        wl_ = [(DV, st["dv_lread"])] if (frst and qc == 0) else []
                        E("tensor", lambda e, sp=sp, qc=qc, col=col: e.matmul(
                            out=SBp[:, col:col + 1],
                            lhsT=PT[:, sp * 256 + qc * 128:sp * 256 + (qc + 1) * 128],
                            rhs=onescol[:, 0:1], start=True, stop=True),
                            waits=wl_, incs=[(PE, 1)] if qc == 1 else ())
                    pe_pv[pos] = PE.n
                    st["pe_pv_slot"][sp] = PE.n

                order = [2 * s, 2 * s + 1] + list(range(0, 2 * s))
                for pos in range(nt):
                    tile = order[pos]
                    st["ks"][pos] = dma("gpsimd", KS[pos % 4], lambda e, seq=seq, pos=pos, tile=tile: e.dma_start(
                        out=Ktr[:, (pos % 4) * 1024:(pos % 4 + 1) * 1024], in_=k_scr[seq, tile]),
                        waits=[(PE, st["pe_slot"][pos % 4])])
                    emit_S(pos, tile)
                    if pos > 0:
                        emit_PV(pos - 1, order[pos - 1])
                    while popped < n_pend and int(pend[popped][0] * nt) <= pos:
                        pend[popped][1]()
                        popped += 1
                emit_PV(nt - 1, order[nt - 1])
                while popped < n_pend:
                    pend[popped][1]()
                    popped += 1
                st["pe_flashend"][s] = PE.n
                st["pe_pvlast"][s] = PE.n
                epi_pre, e_main, e_spill = make_epilogue(s)
                pend = list(spill_next) + list(e_main)
                spill_next = e_spill

            for f in epi_pre:
                f()
            for _, f in pend:
                f()
            for _, f in spill_next:
                f()
            if st.get("store1_thunk") is not None:
                st["store1_thunk"]()
                st["store1_thunk"] = None

        def with_bank7(name, shape, dtype, fn):
            g = nc.psum_tensor(name, shape, dtype)
            t = g.__enter__()
            fn(t)
            g.__exit__(None, None, None)

        with_bank7("TRA0", [128, 1024], BF16, lambda t: phase_a(0, False, t, do_quant=True))
        barrier()
        if debug:
            dma("gpsimd", WT, lambda e: e.dma_start(out=v_dbg, in_=V_buf[:]))
        with_bank7("EB0", [128, 512], F32, lambda t: phase_b(0, t))
        barrier()
        with_bank7("TRA1", [128, 1024], BF16, lambda t: phase_a(1, True, t))
        barrier()
        with_bank7("EB1", [128, 512], F32, lambda t: phase_b(1, t))
        barrier()
        fin = [(s, s.n) for s in B.queue_sems["gpsimd"]]
        E("gpsimd", lambda e: e.nop(), waits=fin, chain=False)

        def replay(eng_name):
            def go(eng):
                for fn, waits, incs in B.ops[eng_name]:
                    for sh, v in waits:
                        eng.wait_ge(sh, v)
                    inst = fn(eng)
                    if incs:
                        for sh, amt in incs:
                            inst.then_inc(sh, amt)
            return go

        block.tensor(replay("tensor"))
        block.vector(replay("vector"))
        block.scalar(replay("scalar"))
        block.sync(replay("sync"))
        block.gpsimd(replay("gpsimd"))

    return nc


_CACHE = {}


def kernel(**inputs):
    x = np.ascontiguousarray(np.asarray(inputs["L0_attn_x"], dtype=np.float32))
    wmap = {
        "wq": inputs["L0_attn_Wq"], "wk": inputs["L0_attn_Wk"],
        "wv": inputs["L0_attn_Wv"], "wo": inputs["L0_attn_Wo"],
        "fk": inputs["L0_ffn_kernel"],
    }
    bmap = {
        "bq": inputs["L0_attn_bq"], "bk": inputs["L0_attn_bk"],
        "bv": inputs["L0_attn_bv"], "bo": inputs["L0_attn_bo"],
        "fbias": inputs["L0_ffn_bias"],
    }
    base = {}
    for k, v in wmap.items():
        base[k] = np.ascontiguousarray(np.asarray(v, dtype=np.float32))
    for k, v in bmap.items():
        base[k] = np.ascontiguousarray(np.asarray(v, dtype=np.float32)).reshape(1, D)
    base["fa"] = np.ascontiguousarray(np.asarray(inputs["L0_ffn_a"], dtype=np.float32))
    base["fb"] = np.ascontiguousarray(np.asarray(inputs["L0_ffn_b"], dtype=np.float32))

    if "nc" not in _CACHE:
        _CACHE["nc"] = build_nc()
    nc = _CACHE["nc"]

    in_maps = []
    for c in range(8):
        m = dict(base)
        m["x"] = np.ascontiguousarray(x[2 * c:2 * c + 2])
        in_maps.append(m)

    res = run_bass_kernel_spmd(nc, in_maps, core_ids=list(range(8)))
    y = np.concatenate([r["y"] for r in res.results], axis=0)
    return y.reshape(16, SEQ, D)


if __name__ == "__main__":
    build_nc()
    print("build ok")
